# version 02
# baseline (speedup 1.0000x reference)
import numpy as np
import ml_dtypes
import contextlib
import concourse.bass as bass
import concourse.mybir as mybir
from concourse.bass_utils import run_bass_kernel_spmd

F32 = mybir.dt.float32
BF16 = mybir.dt.bfloat16
I32 = mybir.dt.int32
U32 = mybir.dt.uint32
AF = mybir.ActivationFunctionType
ALU = mybir.AluOpType
AX = mybir.AxisListType

B, T, D = 2, 4096, 2048
NT = 1024
NTILE = 8
RG8 = [list(range(8))]
RG4 = [[0, 1, 2, 3], [4, 5, 6, 7]]
MLP1 = 8192
L1FGU_PARTS = 5
DEBUG_IB = False
L1F_EV_MODE = 'mult'


class Ctx:
    def __init__(self, nc, stack):
        self.nc = nc
        self.stack = stack
        self.sb = {}
        self.dram = {}
        self.semh = {}
        self.semv = {}
        self.semb = {}
        self.phys_tot = {}
        self.free_sems = []
        self.psT = [stack.enter_context(nc.psum_tensor(f"psT{i}", [128, 1024], F32))
                    for i in range(4)]
        self.bank = [self.psT[i // 2][:, (i % 2) * 512:(i % 2 + 1) * 512] for i in range(8)]

    def sem(self, name):
        if name not in self.semh:
            if self.free_sems:
                h = self.free_sems.pop()
            else:
                h = self.stack.enter_context(self.nc.semaphore(name))
                self.phys_tot[id(h)] = 0
            self.semh[name] = h
            self.semb[name] = self.phys_tot[id(h)]
            self.semv[name] = 0
        return self.semh[name]

    def inc(self, instr, name, n=1):
        h = self.sem(name)
        instr.then_inc(h, n)
        self.semv[name] += n
        self.phys_tot[id(h)] += n
        return self.semv[name]

    def val(self, name):
        self.sem(name)
        return self.semv[name]

    def wait(self, eng, name, v):
        eng.wait_ge(self.sem(name), self.semb[name] + v)

    def wait_all(self, eng, name):
        eng.wait_ge(self.sem(name), self.semb[name] + self.semv[name])

    def edge(self, instr, eng, name):
        v = self.inc(instr, name, 1)
        eng.wait_ge(self.semh[name], self.semb[name] + v)

    def retire(self):
        for name in list(self.semh):
            self.free_sems.append(self.semh.pop(name))
            self.semv.pop(name)
            self.semb.pop(name)

    def sbuf(self, name, shape, dt=F32):
        if name not in self.sb:
            self.sb[name] = self.stack.enter_context(self.nc.sbuf_tensor(name, shape, dt))
        return self.sb[name]

    def dram_in(self, name, shape, dt=F32):
        if name not in self.dram:
            self.dram[name] = self.nc.dram_tensor(name, shape, dt, kind="ExternalInput")
        return self.dram[name]

    def dram_out(self, name, shape, dt=F32):
        if name not in self.dram:
            self.dram[name] = self.nc.dram_tensor(name, shape, dt, kind="ExternalOutput")
        return self.dram[name]

    def dram_tmp(self, name, shape, dt=F32):
        if name not in self.dram:
            self.dram[name] = self.nc.dram_tensor(name, shape, dt)
        return self.dram[name]



def emit_load_residual(ctx):
    nc = ctx.nc
    res = ctx.sbuf("res", [128, NTILE * D], F32)
    x_in = ctx.dram_in("x_shard", [NT, D], F32)
    with nc.Block() as block:
        @block.sync
        def _(sync):
            ctx.inc(sync.dma_start(out=res[:, :].rearrange("p (t d) -> p t d", t=NTILE), in_=x_in.ap().rearrange("(t p) d -> p t d", p=128)), "res_ld", 16)
            ctx.wait_all(sync, "res_ld")


def emit_store_residual(ctx):
    nc = ctx.nc
    res = ctx.sb["res"]
    x_out = ctx.dram_out("x_out", [NT, D], F32)
    with nc.Block() as block:
        @block.sync
        def _(sync):
            ctx.inc(sync.dma_start(out=x_out.ap().rearrange("(t p) d -> p t d", p=128), in_=res[:, :].rearrange("p (t d) -> p t d", t=NTILE)), "res_st", 16)
            ctx.wait_all(sync, "res_st")



def emit_layernorm(ctx, tag, xfm):
    nc = ctx.nc
    res = ctx.sb["res"]
    xb = (ctx.dram_out if DEBUG_IB else ctx.dram_tmp)(f"xln_b_{tag}", [NT, D], BF16)
    stat = ctx.sbuf("ln_stat", [128, 128], F32)
    sA, sV, sR, sB_, sO = (f"lnA{tag}", f"lnV{tag}", f"lnR{tag}", f"lnB{tag}", f"lnO{tag}")

    def C(t, j):
        return stat[:, t * 8 + j: t * 8 + j + 1]

    with nc.sbuf_tensor(f"xln_tm_{tag}", [128, NTILE * D], BF16, side="right") as xln_tm:
        with nc.Block() as block:
            @block.scalar
            def _(scalar):
                for t in range(NTILE):
                    rt = res[:, t * D:(t + 1) * D]
                    ot = xln_tm[:, t * D:(t + 1) * D]
                    scalar.activation(ot, rt, AF.Copy, accum_out=C(t, 0))
                    a = scalar.activation(ot, rt, AF.Square, accum_out=C(t, 1))
                    ctx.inc(a, sA, 1)

            @block.vector
            def _(vector):
                for t in range(NTILE):
                    ctx.wait(vector, sA, t + 1)
                    vector.tensor_scalar_mul(C(t, 2), C(t, 0), 1.0 / D)
                    ctx.edge(vector.tensor_scalar_mul(C(t, 3), C(t, 1), 1.0 / D), vector, "dveS")
                    ctx.edge(vector.scalar_tensor_tensor(C(t, 4), C(t, 2), -1.0, C(t, 2),
                                                         ALU.mult, ALU.mult), vector, "dveS")
                    ctx.edge(vector.tensor_tensor(C(t, 5), C(t, 3), C(t, 4), ALU.add), vector, "dveS")
                    a = vector.tensor_scalar_add(C(t, 5), C(t, 5), 1e-5)
                    ctx.inc(a, sV, 1)

            @block.scalar
            def _(scalar):
                for t in range(NTILE):
                    ctx.wait(scalar, sV, t + 1)
                    ctx.edge(scalar.activation(C(t, 6), C(t, 5), AF.Ln), scalar, "actS")
                    a = scalar.activation(C(t, 7), C(t, 6), AF.Exp, scale=-0.5)
                    ctx.inc(a, sR, 1)

            @block.vector
            def _(vector):
                for t in range(NTILE):
                    ctx.wait(vector, sR, t + 1)
                    a = vector.scalar_tensor_tensor(stat[:, 64 + t:64 + t + 1], C(t, 2), -1.0, C(t, 7),
                                                    ALU.mult, ALU.mult)
                    ctx.edge(a, vector, "dveS")
                    rt = res[:, t * D:(t + 1) * D]
                    a = vector.tensor_scalar(xln_tm[:, t * D:(t + 1) * D], rt,
                                             C(t, 7), stat[:, 64 + t:64 + t + 1], ALU.mult, ALU.add)
                    ctx.inc(a, sO, 1)

            @block.sync
            def _(sync):
                for t in range(NTILE):
                    ctx.wait(sync, sO, t + 1)
                    ctx.inc(sync.dma_start(out=xb.ap()[t * 128:(t + 1) * 128, :],
                                           in_=xln_tm[:, t * D:(t + 1) * D]), f"xbst{tag}", 16)
                ctx.wait_all(sync, f"xbst{tag}")
                for c in range(16):
                    ctx.inc(sync.dma_start_transpose(xfm[:, c * 1024:(c + 1) * 1024],
                                                     xb.ap()[:, c * 128:(c + 1) * 128]), f"xfmld{tag}", 16)
                ctx.wait_all(sync, f"xfmld{tag}")
    return xfm



def emit_l1_ffn_gateup(ctx, xfm):
    nc = ctx.nc
    res = ctx.sb["res"]
    bank = ctx.bank
    wg = ctx.dram_in("l1f_wg", [D, MLP1], F32)
    wu = ctx.dram_in("l1f_wu", [D, MLP1], F32)
    wd = ctx.dram_in("l1f_wd", [MLP1, D], F32)
    ib = (ctx.dram_out if DEBUG_IB else ctx.dram_tmp)("l1f_inter_b", [NT, MLP1], BF16)
    NSL = 16

    with nc.sbuf_tensor("l1f_pang", [128, 2 * 8192], BF16, side="right") as pan_g, \
         nc.sbuf_tensor("l1f_panu", [128, 2 * 8192], BF16, side="right") as pan_u, \
         nc.sbuf_tensor("l1f_it", [128, 8 * 512], BF16, side="right") as it_sl, \
         nc.sbuf_tensor("l1f_habs", [128, 8 * 512], BF16, side="right") as habs, \
         nc.sbuf_tensor("l1f_spt", [128, 3 * 1024], BF16, side="right") as spt, \
         nc.sbuf_tensor("l1f_gs", [128, 2 * 512], BF16, side="right") as g_s:
        with nc.Block() as block:
            @block.gpsimd
            def _(g):
                for isl in range(NSL):
                    b = isl % 2
                    if isl >= 2 and L1FGU_PARTS >= 2:
                        ctx.wait(g, "l1f_pe", (isl - 1) * 16)
                    ctx.inc(g.dma_start(out=pan_g[:, b * 8192:(b + 1) * 8192].rearrange("p (c i) -> p c i", c=16),
                                        in_=wg.ap()[:, isl * 512:(isl + 1) * 512].rearrange("(c p) i -> p c i", p=128)),
                            "l1f_w", 16)
                    ctx.inc(g.dma_start(out=pan_u[:, b * 8192:(b + 1) * 8192].rearrange("p (c i) -> p c i", c=16),
                                        in_=wu.ap()[:, isl * 512:(isl + 1) * 512].rearrange("(c p) i -> p c i", p=128)),
                            "l1f_w", 16)

            @block.tensor
            def _(tensor):
                if L1FGU_PARTS < 2:
                    return
                for isl in range(NSL):
                    b = isl % 2
                    ctx.wait(tensor, "l1f_w", 32 * (isl + 1))
                    for ts_ in range(8):
                        k = isl * 8 + ts_
                        pb = k % 2
                        if k >= 2 and L1FGU_PARTS >= 3:
                            ctx.wait(tensor, "l1f_silu", k - 1)
                            ctx.wait(tensor, "l1f_ev", k - 1)
                        for c in range(16):
                            lhsT = xfm[:, c * 1024 + ts_ * 128: c * 1024 + ts_ * 128 + 128]
                            mm = tensor.matmul(bank[pb], lhsT,
                                               pan_g[:, b * 8192 + c * 512: b * 8192 + (c + 1) * 512],
                                               start=(c == 0), stop=(c == 15))
                        ctx.inc(mm, "l1f_pe", 1)
                        for c in range(16):
                            lhsT = xfm[:, c * 1024 + ts_ * 128: c * 1024 + ts_ * 128 + 128]
                            mm = tensor.matmul(bank[2 + pb], lhsT,
                                               pan_u[:, b * 8192 + c * 512: b * 8192 + (c + 1) * 512],
                                               start=(c == 0), stop=(c == 15))
                        ctx.inc(mm, "l1f_pe", 1)

            @block.scalar
            def _(scalar):
                if L1FGU_PARTS < 3:
                    return
                for isl in range(NSL):
                    for ts_ in range(8):
                        k = isl * 8 + ts_
                        pb = k % 2
                        ctx.wait(scalar, "l1f_pe", 2 * k + 1)
                        if k >= 2:
                            ctx.wait(scalar, "l1f_ev", k - 1)
                        a = scalar.activation(g_s[:, pb * 512:(pb + 1) * 512], bank[pb], AF.Silu)
                        ctx.inc(a, "l1f_silu", 1)
                    if L1FGU_PARTS >= 4:
                        ctx.wait(scalar, "l1f_ev", 8 * (isl + 1))
                        if isl > 0:
                            ctx.wait(scalar, "l1f_sp", isl)
                        a = scalar.activation(habs[:, :], it_sl[:, :], AF.Abs)
                        ctx.inc(a, "l1f_abs", 1)

            @block.vector
            def _(vector):
                if L1FGU_PARTS < 3:
                    return
                for isl in range(NSL):
                    for ts_ in range(8):
                        k = isl * 8 + ts_
                        pb = k % 2
                        ctx.wait(vector, "l1f_pe", 2 * k + 2)
                        ctx.wait(vector, "l1f_silu", k + 1)
                        if L1FGU_PARTS >= 5 and isl > 0 and ts_ == 0:
                            ctx.wait(vector, "l1f_ibst", 16 * isl)
                        if L1F_EV_MODE == 'copy':
                            a = vector.tensor_copy(it_sl[:, ts_ * 512:(ts_ + 1) * 512],
                                                   g_s[:, pb * 512:(pb + 1) * 512])
                        else:
                            a = vector.tensor_tensor(it_sl[:, ts_ * 512:(ts_ + 1) * 512],
                                                     g_s[:, pb * 512:(pb + 1) * 512], bank[2 + pb], ALU.mult)
                        ctx.inc(a, "l1f_ev", 1)
                    if L1FGU_PARTS < 4:
                        continue
                    ctx.wait(vector, "l1f_abs", isl + 1)
                    A = habs[:, 0:4096:4]; Bv = habs[:, 1:4096:4]
                    Cv = habs[:, 2:4096:4]; Dv = habs[:, 3:4096:4]
                    hi1 = spt[:, 0:1024]; hi2 = spt[:, 1024:2048]; t2 = spt[:, 2048:3072]
                    vector.tensor_tensor(hi1, A, Bv, ALU.max)
                    vector.tensor_tensor(hi2, Cv, Dv, ALU.max)
                    vector.tensor_tensor(t2, hi1, hi2, ALU.min)
                    vector.tensor_tensor(hi1, A, Bv, ALU.min)
                    vector.tensor_tensor(hi2, Cv, Dv, ALU.min)
                    vector.tensor_tensor(hi1, hi1, hi2, ALU.max)
                    vector.tensor_tensor(t2, t2, hi1, ALU.max)
                    for r in range(4):
                        Xr = it_sl[:, r:4096:4]
                        Hr = habs[:, r:4096:4]
                        vector.tensor_tensor(Hr, Hr, t2, ALU.is_ge)
                        a = vector.tensor_tensor(Xr, Xr, Hr, ALU.mult)
                    ctx.inc(a, "l1f_sp", 1)

            @block.sync
            def _(sync):
                if L1FGU_PARTS < 5:
                    return
                for isl in range(NSL):
                    ctx.wait(sync, "l1f_sp", isl + 1)
                    src = it_sl[:, :].rearrange("p (s i) -> p s i", s=8)
                    dst = ib.ap()[:, isl * 512:(isl + 1) * 512].rearrange("(s p) i -> p s i", p=128)
                    ctx.inc(sync.dma_start(out=dst, in_=src), "l1f_ibst", 16)
                ctx.wait_all(sync, "l1f_ibst")


def emit_l1_ffn_down(ctx):
    nc = ctx.nc
    res = ctx.sb["res"]
    bank = ctx.bank
    wd = ctx.dram["l1f_wd"]
    ib = ctx.dram["l1f_inter_b"]
    NG = 8
    with nc.sbuf_tensor("l1f_ifm", [128, 8 * 1024], BF16, side="right") as ifm, \
         nc.sbuf_tensor("l1f_wdp", [128, 8 * 2048], BF16, side="right") as wd_p, \
         nc.sbuf_tensor("l1f_acc", [128, 16384], F32, side="right") as acc:
        with nc.Block() as block:
            @block.gpsimd
            def _(g):
                for ic in range(64):
                    bw = ic % 8
                    if ic >= 8:
                        ctx.wait(g, "l1f_pe2", ((ic - 8) // NG + 1) * 64)
                    ctx.inc(g.dma_start(out=wd_p[:, bw * 2048:(bw + 1) * 2048],
                                        in_=wd.ap()[ic * 128:(ic + 1) * 128, :]), "l1f_w2", 16)

            @block.sync
            def _(sync):
                for ic in range(64):
                    bi = ic % 8
                    if ic >= 8:
                        ctx.wait(sync, "l1f_pe2", ((ic - 8) // NG + 1) * 64)
                    ctx.inc(sync.dma_start_transpose(ifm[:, bi * 1024:(bi + 1) * 1024],
                                                     ib.ap()[:, ic * 128:(ic + 1) * 128]), "l1f_ix", 16)

            @block.tensor
            def _(tensor):
                for icg in range(64 // NG):
                    ctx.wait(tensor, "l1f_w2", 16 * (icg + 1) * NG)
                    ctx.wait(tensor, "l1f_ix", 16 * (icg + 1) * NG)
                    for ts_ in range(8):
                        q = icg * 8 + ts_
                        qp = q % 2
                        if q >= 2:
                            ctx.wait(tensor, "l1f_dacc", (q - 1) * 4)
                        for j, ic in enumerate(range(icg * NG, (icg + 1) * NG)):
                            bi = ic % 8
                            lhsT = ifm[:, bi * 1024 + ts_ * 128: bi * 1024 + ts_ * 128 + 128]
                            for hs in range(4):
                                mm = tensor.matmul(bank[qp * 4 + hs], lhsT,
                                                   wd_p[:, bi * 2048 + hs * 512: bi * 2048 + (hs + 1) * 512],
                                                   start=(j == 0), stop=(j == NG - 1))
                        ctx.inc(mm, "l1f_pe2", 8)

            @block.vector
            def _(vector):
                vector.memset(acc[:, :], 0.0)
                for icg in range(64 // NG):
                    for ts_ in range(8):
                        q = icg * 8 + ts_
                        qp = q % 2
                        ctx.wait(vector, "l1f_pe2", (q + 1) * 8)
                        for hs in range(4):
                            dstv = acc[:, ts_ * 2048 + hs * 512: ts_ * 2048 + (hs + 1) * 512]
                            a = vector.tensor_tensor(dstv, dstv, bank[qp * 4 + hs], ALU.add)
                            ctx.inc(a, "l1f_dacc", 1)
                for t in range(NTILE):
                    vector.tensor_tensor(res[:, t * D:(t + 1) * D], res[:, t * D:(t + 1) * D],
                                         acc[:, t * 2048:(t + 1) * 2048], ALU.add)





NSL0 = 32
MLP0 = 16384
DEQ_BIAS = 126 << 23


def _emit_dequant_tile(ctx, eng, use_select, P, scr, SC, O):
    nc = ctx.nc
    sname = "deqV" if use_select else "deqP"

    def E(instr):
        ctx.edge(instr, eng, sname)

    for half, (mM, mS, mH, shM, shS, Kh) in enumerate([
            (7, 8, 1, 22, 28, 0x3F000000),
            (0x70, 0x80, 0x10, 18, 24, 0x03F00000),
    ]):
        t1 = scr[:, 0:1024]; t2 = scr[:, 1024:2048]; fb = scr[:, 2048:3072]
        msk = scr[:, 3072:4096]; hv = scr[:, 4096:5120]; hv2 = scr[:, 5120:6144]
        eng.tensor_scalar(t1, P, mM, shM, ALU.bitwise_and, ALU.logical_shift_left)
        eng.tensor_scalar(t2, P, mS, shS, ALU.bitwise_and, ALU.logical_shift_left)
        ctx.edge(eng.tensor_scalar(msk, P, mM, None, ALU.bitwise_and), eng, sname)
        E(eng.tensor_scalar(msk, msk, 2 if half == 0 else 0x20, None, ALU.is_lt))
        ctx.edge(eng.tensor_scalar(hv, P, mH, None, ALU.bitwise_and), eng, sname)
        E(eng.tensor_scalar(hv, hv, Kh, None, ALU.mult))
        E(eng.scalar_tensor_tensor(fb, t1, DEQ_BIAS, t2, ALU.add, ALU.add))
        E(eng.tensor_tensor(hv2, hv, t2, ALU.bitwise_or))
        if use_select:
            E(eng.select(t1, msk, hv2, fb))
            hv2 = t1
        else:
            E(eng.tensor_tensor(hv2, hv2, fb, ALU.subtract))
            E(eng.tensor_tensor(hv2, msk, hv2, ALU.mult))
            E(eng.tensor_tensor(hv2, hv2, fb, ALU.add))
        E(eng.tensor_tensor(O[:, half::2], hv2.bitcast(F32), SC[:, half::2], ALU.mult))


def emit_l0_ffn_dequant(ctx):
    nc = ctx.nc
    gWp = ctx.dram_in("l0f_gWp", [2048, 1024], I32)
    gSc = ctx.dram_in("l0f_gSc", [2048, 2048], F32)
    uWp = ctx.dram_in("l0f_uWp", [2048, 1024], I32)
    uSc = ctx.dram_in("l0f_uSc", [2048, 2048], F32)
    dWp = ctx.dram_in("l0f_dWp", [256, 8192], I32)
    dSc = ctx.dram_in("l0f_dSc", [256, 16384], F32)
    g_nat = (ctx.dram_out if DEBUG_IB else ctx.dram_tmp)("l0f_gnat", [2048, 2048], BF16)
    u_nat = ctx.dram_tmp("l0f_unat", [2048, 2048], BF16)
    d_nat = ctx.dram_tmp("l0f_dnat", [256, 16384], BF16)
    g_tin = ctx.dram_tmp("l0f_gtin", [2048, 2048], BF16)
    u_tin = ctx.dram_tmp("l0f_utin", [2048, 2048], BF16)
    ag_g = ctx.dram_tmp("l0f_ag_g", [8 * 2048, 2048], BF16)
    ag_u = ctx.dram_tmp("l0f_ag_u", [8 * 2048, 2048], BF16)
    ag_d = ctx.dram_tmp("l0f_ag_d", [8 * 256, 16384], BF16)

    tiles = []
    for (Wp, Sc, Onat) in [(gWp, gSc, g_nat), (uWp, uSc, u_nat)]:
        for rr in range(16):
            tiles.append((Wp.ap()[rr * 128:(rr + 1) * 128, :],
                          Sc.ap()[rr * 128:(rr + 1) * 128, :],
                          Onat.ap()[rr * 128:(rr + 1) * 128, :]))
    for rr in range(2):
        for cc in range(8):
            tiles.append((dWp.ap()[rr * 128:(rr + 1) * 128, cc * 1024:(cc + 1) * 1024],
                          dSc.ap()[rr * 128:(rr + 1) * 128, cc * 2048:(cc + 1) * 2048],
                          d_nat.ap()[rr * 128:(rr + 1) * 128, cc * 2048:(cc + 1) * 2048]))

    with nc.sbuf_tensor("dq_P", [128, 2 * 1024], I32, side="right") as Pt, \
         nc.sbuf_tensor("dq_SC", [128, 2 * 2048], F32, side="right") as SCt, \
         nc.sbuf_tensor("dq_O", [128, 2 * 2048], BF16, side="right") as Ot, \
         nc.sbuf_tensor("dq_scrV", [128, 6 * 1024], I32, side="right") as scrV, \
         nc.sbuf_tensor("dq_scrP", [128, 6 * 1024], I32, side="right") as scrP:
        with nc.Block() as block:
            @block.sync
            def _(sync):
                for i, (p_ap, s_ap, o_ap) in enumerate(tiles):
                    bf = i % 2
                    if i >= 2:
                        ctx.wait(sync, "dq_done", i - 1)
                    ctx.inc(sync.dma_start(out=Pt[:, bf * 1024:(bf + 1) * 1024], in_=p_ap), "dq_ld", 16)
                    ctx.inc(sync.dma_start(out=SCt[:, bf * 2048:(bf + 1) * 2048], in_=s_ap), "dq_ld", 16)

            @block.vector
            def _(vector):
                for i in range(len(tiles)):
                    bf = i % 2
                    ctx.wait(vector, "dq_ld", 32 * (i + 1))
                    if i >= 2:
                        ctx.wait(vector, "dq_st", 16 * (i - 1))
                    _emit_dequant_tile(ctx, vector, True, Pt[:, bf * 1024:(bf + 1) * 1024], scrV,
                                       SCt[:, bf * 2048:(bf + 1) * 2048], Ot[:, bf * 2048:(bf + 1) * 2048])
                    ctx.inc(vector.tensor_copy(Ot[:, bf * 2048:bf * 2048 + 8], Ot[:, bf * 2048:bf * 2048 + 8]), "dq_doneV", 1)

            @block.scalar
            def _(scalar):
                for i, (p_ap, s_ap, o_ap) in enumerate(tiles):
                    bf = i % 2
                    ctx.wait(scalar, "dq_doneV", i + 1)
                    a = scalar.dma_start(out=o_ap, in_=Ot[:, bf * 2048:(bf + 1) * 2048])
                    ctx.inc(a, "dq_st", 16)
                    ctx.inc(scalar.nop(), "dq_done", 1)
                ctx.wait_all(scalar, "dq_st")

        with nc.sbuf_tensor("dq_tt", [128, 2 * 2048], BF16, side="right") as tt_:
            with nc.Block() as block:
                @block.sync
                def _(sync):
                    for m, (nat, tin) in enumerate([(g_nat, g_tin), (u_nat, u_tin)]):
                        for hc in range(16):
                            j = m * 16 + hc
                            bf = j % 2
                            if j >= 2:
                                ctx.wait(sync, "dq_tw", 16 * (j - 1))
                            ctx.inc(sync.dma_start_transpose(tt_[:, bf * 2048:(bf + 1) * 2048],
                                                             nat.ap()[:, hc * 128:(hc + 1) * 128]), "dq_tl", 16)
                            ctx.wait(sync, "dq_tl", 16 * (j + 1))
                            ctx.inc(sync.dma_start(out=tin.ap()[hc * 128:(hc + 1) * 128, :],
                                                   in_=tt_[:, bf * 2048:(bf + 1) * 2048]), "dq_tw", 16)
                    ctx.wait_all(sync, "dq_tw")

        with nc.Block() as block:
            @block.gpsimd
            def _(g):
                for name, tin, agout in [("g", g_tin, ag_g), ("u", u_tin, ag_u), ("d", d_nat, ag_d)]:
                    ctx.inc(g.collective_compute(
                        "AllGather", ALU.bypass, replica_groups=RG8,
                        ins=[tin.ap()], outs=[agout.ap()]), "l0f_cc", 1)
                ctx.wait_all(g, "l0f_cc")
    return ag_g, ag_u, ag_d


def emit_l0_ffn_main(ctx, xfm):
    nc = ctx.nc
    res = ctx.sb["res"]
    bank = ctx.bank
    ag_g = ctx.dram["l0f_ag_g"]
    ag_u = ctx.dram["l0f_ag_u"]
    ag_d = ctx.dram["l0f_ag_d"]
    ib = (ctx.dram_out if DEBUG_IB else ctx.dram_tmp)("l0f_inter_b", [NT, MLP0], BF16)

    with nc.sbuf_tensor("l0f_pang", [128, 2 * 8192], BF16, side="right") as pan_g, \
         nc.sbuf_tensor("l0f_panu", [128, 2 * 8192], BF16, side="right") as pan_u, \
         nc.sbuf_tensor("l0f_it", [128, 8 * 512], BF16, side="right") as it_sl, \
         nc.sbuf_tensor("l0f_gs", [128, 2 * 512], BF16, side="right") as g_s:
        with nc.Block() as block:
            @block.sync
            def _(sync):
                for isl in range(NSL0):
                    b = isl % 2
                    rblk, iloc = isl // 4, (isl % 4) * 512
                    if isl >= 2:
                        ctx.wait(sync, "l0f_pe", (isl - 1) * 16)
                    ctx.inc(sync.dma_start(out=pan_g[:, b * 8192:(b + 1) * 8192].rearrange("p (c i) -> p c i", c=16),
                                           in_=ag_g.ap()[rblk * 2048:(rblk + 1) * 2048, iloc:iloc + 512]
                                           .rearrange("(c p) i -> p c i", p=128)), "l0f_w", 16)
                    ctx.inc(sync.dma_start(out=pan_u[:, b * 8192:(b + 1) * 8192].rearrange("p (c i) -> p c i", c=16),
                                           in_=ag_u.ap()[rblk * 2048:(rblk + 1) * 2048, iloc:iloc + 512]
                                           .rearrange("(c p) i -> p c i", p=128)), "l0f_w", 16)

            @block.tensor
            def _(tensor):
                for isl in range(NSL0):
                    b = isl % 2
                    ctx.wait(tensor, "l0f_w", 32 * (isl + 1))
                    for ts_ in range(8):
                        k = isl * 8 + ts_
                        pb = k % 2
                        if k >= 2:
                            ctx.wait(tensor, "l0f_gel", k - 1)
                            ctx.wait(tensor, "l0f_ev", k - 1)
                        for c in range(16):
                            lhsT = xfm[:, c * 1024 + ts_ * 128: c * 1024 + ts_ * 128 + 128]
                            mm = tensor.matmul(bank[pb], lhsT,
                                               pan_g[:, b * 8192 + c * 512: b * 8192 + (c + 1) * 512],
                                               start=(c == 0), stop=(c == 15))
                        ctx.inc(mm, "l0f_pe", 1)
                        for c in range(16):
                            lhsT = xfm[:, c * 1024 + ts_ * 128: c * 1024 + ts_ * 128 + 128]
                            mm = tensor.matmul(bank[2 + pb], lhsT,
                                               pan_u[:, b * 8192 + c * 512: b * 8192 + (c + 1) * 512],
                                               start=(c == 0), stop=(c == 15))
                        ctx.inc(mm, "l0f_pe", 1)

            @block.scalar
            def _(scalar):
                for isl in range(NSL0):
                    for ts_ in range(8):
                        k = isl * 8 + ts_
                        pb = k % 2
                        ctx.wait(scalar, "l0f_pe", 2 * k + 1)
                        if k >= 2:
                            ctx.wait(scalar, "l0f_ev", k - 1)
                        a = scalar.activation(g_s[:, pb * 512:(pb + 1) * 512], bank[pb], AF.Gelu_apprx_tanh)
                        ctx.inc(a, "l0f_gel", 1)

            @block.vector
            def _(vector):
                for isl in range(NSL0):
                    for ts_ in range(8):
                        k = isl * 8 + ts_
                        pb = k % 2
                        ctx.wait(vector, "l0f_pe", 2 * k + 2)
                        ctx.wait(vector, "l0f_gel", k + 1)
                        if isl > 0 and ts_ == 0:
                            ctx.wait(vector, "l0f_ibst", 16 * isl)
                        a = vector.tensor_tensor(it_sl[:, ts_ * 512:(ts_ + 1) * 512],
                                                 g_s[:, pb * 512:(pb + 1) * 512], bank[2 + pb], ALU.mult)
                        ctx.inc(a, "l0f_ev", 1)

            @block.gpsimd
            def _(g):
                for isl in range(NSL0):
                    ctx.wait(g, "l0f_ev", 8 * (isl + 1))
                    src = it_sl[:, :].rearrange("p (s i) -> p s i", s=8)
                    dst = ib.ap()[:, isl * 512:(isl + 1) * 512].rearrange("(s p) i -> p s i", p=128)
                    ctx.inc(g.dma_start(out=dst, in_=src), "l0f_ibst", 16)
                ctx.wait_all(g, "l0f_ibst")



def emit_l0_ffn_down(ctx):
    nc = ctx.nc
    res = ctx.sb["res"]
    bank = ctx.bank
    ag_d = ctx.dram["l0f_ag_d"]
    ib = ctx.dram["l0f_inter_b"]
    NG = 8
    with nc.sbuf_tensor("l0f_ifm", [128, 8 * 1024], BF16, side="right") as ifm, \
         nc.sbuf_tensor("l0f_wdp", [128, 8 * 2048], BF16, side="right") as wd_p, \
         nc.sbuf_tensor("l0f_acc", [128, 16384], F32, side="right") as acc:
        with nc.Block() as block:
            @block.sync
            def _(sync):
                for ic in range(128):
                    bi = ic % 8
                    if ic >= 8:
                        ctx.wait(sync, "l0f_pe2", ((ic - 8) // NG + 1) * 64)
                    ctx.inc(sync.dma_start_transpose(wd_p[:, bi * 2048:(bi + 1) * 2048],
                                                     ag_d.ap()[:, ic * 128:(ic + 1) * 128]), "l0f_w2", 16)
                    ctx.inc(sync.dma_start_transpose(ifm[:, bi * 1024:(bi + 1) * 1024],
                                                     ib.ap()[:, ic * 128:(ic + 1) * 128]), "l0f_ix", 16)

            @block.tensor
            def _(tensor):
                for icg in range(128 // NG):
                    ctx.wait(tensor, "l0f_w2", 16 * (icg + 1) * NG)
                    ctx.wait(tensor, "l0f_ix", 16 * (icg + 1) * NG)
                    for ts_ in range(8):
                        q = icg * 8 + ts_
                        qp = q % 2
                        if q >= 2:
                            ctx.wait(tensor, "l0f_dacc", (q - 1) * 4)
                        for j, ic in enumerate(range(icg * NG, (icg + 1) * NG)):
                            bi = ic % 8
                            lhsT = ifm[:, bi * 1024 + ts_ * 128: bi * 1024 + ts_ * 128 + 128]
                            for hs in range(4):
                                mm = tensor.matmul(bank[qp * 4 + hs], lhsT,
                                                   wd_p[:, bi * 2048 + hs * 512: bi * 2048 + (hs + 1) * 512],
                                                   start=(j == 0), stop=(j == NG - 1))
                        ctx.inc(mm, "l0f_pe2", 8)

            @block.vector
            def _(vector):
                vector.memset(acc[:, :], 0.0)
                for icg in range(128 // NG):
                    for ts_ in range(8):
                        q = icg * 8 + ts_
                        qp = q % 2
                        ctx.wait(vector, "l0f_pe2", (q + 1) * 8)
                        for hs in range(4):
                            dstv = acc[:, ts_ * 2048 + hs * 512: ts_ * 2048 + (hs + 1) * 512]
                            a = vector.tensor_tensor(dstv, dstv, bank[qp * 4 + hs], ALU.add)
                            ctx.inc(a, "l0f_dacc", 1)
                for t in range(NTILE):
                    vector.tensor_tensor(res[:, t * D:(t + 1) * D], res[:, t * D:(t + 1) * D],
                                         acc[:, t * 2048:(t + 1) * 2048], ALU.add)





def emit_l1_attn_latents(ctx, xfm, qlat):
    nc = ctx.nc
    bank = ctx.bank
    wq_d = ctx.dram_in("l1a_wq_d", [D, 256], F32)
    wkv_d = ctx.dram_in("l1a_wkv_d", [D, 256], F32)
    kv_tm_loc = ctx.dram_tmp("l1a_kv_loc", [NT, 256], BF16)
    kv_tm_ag = ctx.dram_tmp("l1a_kv_ag", [4 * NT, 256], BF16)
    kv_fm_b = ctx.dram_tmp("l1a_kvfm_b", [256, NT], BF16)

    with nc.sbuf_tensor("l1a_wdq", [128, 16 * 256], BF16, side="right") as wdq, \
         nc.sbuf_tensor("l1a_wdkv", [128, 16 * 256], BF16, side="right") as wdkv, \
         nc.sbuf_tensor("l1a_latk", [128, 2 * 1024], BF16, side="right") as kvlat, \
         nc.sbuf_tensor("l1a_kvtm", [128, 8 * 256], BF16, side="right") as kvtm:
        with nc.Block() as block:
            @block.gpsimd
            def _(g):
                ctx.inc(g.dma_start(out=wdq[:, :].rearrange("p (c l) -> p c l", c=16),
                                    in_=wq_d.ap().rearrange("(c p) l -> p c l", p=128)), "l1a_wl", 16)
                ctx.inc(g.dma_start(out=wdkv[:, :].rearrange("p (c l) -> p c l", c=16),
                                    in_=wkv_d.ap().rearrange("(c p) l -> p c l", p=128)), "l1a_wl", 16)

            @block.tensor
            def _(tensor):
                ctx.wait_all(tensor, "l1a_wl")
                for which, wsrc in [(0, wdq), (1, wdkv)]:
                    for lc in range(2):
                        for th in range(2):
                            if lc == 1:
                                ctx.wait(tensor, "l1a_lat_ev", which * 4 + th + 1)
                            for c in range(16):
                                mm = tensor.matmul(bank[2 * which + th],
                                                   wsrc[:, c * 256 + lc * 128: c * 256 + (lc + 1) * 128],
                                                   xfm[:, c * 1024 + th * 512: c * 1024 + (th + 1) * 512],
                                                   start=(c == 0), stop=(c == 15))
                            ctx.inc(mm, "l1a_lat_pe", 1)

            @block.scalar
            def _(scalar):
                n = 0
                for which, dst in [(0, qlat), (1, kvlat)]:
                    for lc in range(2):
                        for th in range(2):
                            n += 1
                            ctx.wait(scalar, "l1a_lat_pe", n)
                            a = scalar.activation(dst[:, lc * 1024 + th * 512: lc * 1024 + (th + 1) * 512],
                                                  bank[2 * which + th], AF.Copy)
                            ctx.inc(a, "l1a_lat_ev", 1)

        with nc.Block() as block:
            @block.sync
            def _(sync):
                ctx.wait_all(sync, "l1a_lat_ev")
                ctx.inc(sync.dma_start(out=kv_fm_b.ap().rearrange("(c p) t -> p c t", p=128),
                                       in_=kvlat[:, :].rearrange("p (c t) -> p c t", c=2)), "l1a_kvst", 16)
                ctx.wait_all(sync, "l1a_kvst")
                for tt in range(8):
                    ctx.inc(sync.dma_start_transpose(kvtm[:, tt * 256:(tt + 1) * 256],
                                                     kv_fm_b.ap()[:, tt * 128:(tt + 1) * 128]), "l1a_kvtl", 16)
                ctx.wait_all(sync, "l1a_kvtl")
                ctx.inc(sync.dma_start(out=kv_tm_loc.ap().rearrange("(tt p) l -> p tt l", p=128),
                                       in_=kvtm[:, :].rearrange("p (tt l) -> p tt l", tt=8)), "l1a_kvst2", 16)
                ctx.wait_all(sync, "l1a_kvst2")

            @block.gpsimd
            def _(g):
                ctx.wait_all(g, "l1a_kvst2")
                ctx.inc(g.collective_compute(
                    "AllGather", ALU.bypass, replica_groups=RG4,
                    ins=[kv_tm_loc.ap()], outs=[kv_tm_ag.ap()]), "l1a_cc", 1)
                ctx.wait_all(g, "l1a_cc")


def emit_l1_attn_main(ctx, qlat, O):
    nc = ctx.nc
    res = ctx.sb["res"]
    bank = ctx.bank
    psT = ctx.psT
    wq_u = ctx.dram_in("l1a_wq_u", [256, 1024], F32)
    wk_u = ctx.dram_in("l1a_wk_u", [256, 1024], F32)
    wv_u = ctx.dram_in("l1a_wv_u", [256, D], F32)
    wo = ctx.dram_in("l1a_wo", [D, D], F32)
    cos_q = ctx.dram_in("l1a_cos_q", [64, 1024], BF16)
    sin_q = ctx.dram_in("l1a_sin_q", [64, 1024], BF16)
    cos_k = ctx.dram_in("l1a_cos_k", [64, 4096], BF16)
    sin_k = ctx.dram_in("l1a_sin_k", [64, 4096], BF16)
    ones_cb = ctx.dram_in("ones_col_b", [128, 1], BF16)
    ones_rf = ctx.dram_in("ones_row_f", [1, 128], F32)
    kv_tm_ag = ctx.dram["l1a_kv_ag"]

    H = 16
    NKC = 32

    with nc.sbuf_tensor("l1a_qall", [128, 16 * 1024], BF16, side="right") as qall, \
         nc.sbuf_tensor("l1a_cs", [128, 4 * 1024], BF16, side="right") as cs, \
         nc.sbuf_tensor("l1a_scr", [128, 2 * 1024], BF16, side="right") as scr, \
         nc.sbuf_tensor("l1a_kvfm", [128, 2 * 4096], BF16, side="right") as kvfm, \
         nc.sbuf_tensor("l1a_K", [128, 4096], BF16, side="right") as Kt, \
         nc.sbuf_tensor("l1a_V", [128, 4096], BF16, side="right") as Vt, \
         nc.sbuf_tensor("l1a_ex", [128, 2 * 1024], BF16, side="right") as exT, \
         nc.sbuf_tensor("l1a_wub", [128, 6144], BF16, side="right") as wub, \
         nc.sbuf_tensor("l1a_rb", [128, 1024], F32, side="right") as rb, \
         nc.sbuf_tensor("l1a_sm", [128, 32], F32, side="right") as sm, \
         nc.sbuf_tensor("l1a_onc", [128, 1], BF16, side="right") as onesc, \
         nc.sbuf_tensor("l1a_onr", [1, 128], F32, side="right") as onesr:
        COSK = cs[0:64, 0:4096]; SINK = cs[64:128, 0:4096]
        WK = wub[:, 0:2048]; WV = wub[:, 2048:6144]
        rc = rb

        wq_stack = contextlib.ExitStack()
        WQt = wq_stack.enter_context(nc.sbuf_tensor("l1a_wq_t", [128, 4096], BF16, side="right"))
        WQ = WQt[:, 0:2048]
        COSQ = WQt[0:64, 2048:3072]; SINQ = WQt[0:64, 3072:4096]
        with nc.Block() as block:
            @block.sync
            def _(sync):
                for lc in range(2):
                    ctx.inc(sync.dma_start_transpose(kvfm[:, lc * 4096:(lc + 1) * 4096],
                                                     kv_tm_ag.ap()[:, lc * 128:(lc + 1) * 128]), "l1a_kvfl", 16)

            @block.gpsimd
            def _(g):
                ctx.inc(g.dma_start(out=WQ.rearrange("p (c s) -> p c s", c=2),
                                    in_=wq_u.ap().rearrange("(c p) s -> p c s", p=128)), "l1a_wu2", 16)
                ctx.inc(g.dma_start(out=WK.rearrange("p (c s) -> p c s", c=2),
                                    in_=wk_u.ap().rearrange("(c p) s -> p c s", p=128)), "l1a_wu2", 16)
                ctx.inc(g.dma_start(out=WV.rearrange("p (c s) -> p c s", c=2),
                                    in_=wv_u.ap().rearrange("(c p) s -> p c s", p=128)), "l1a_wu2", 16)
                ctx.inc(g.dma_start(out=COSQ, in_=cos_q.ap()), "l1a_wu2", 16)
                ctx.inc(g.dma_start(out=SINQ, in_=sin_q.ap()), "l1a_wu2", 16)
                ctx.inc(g.dma_start(out=COSK, in_=cos_k.ap()), "l1a_wu2", 16)
                ctx.inc(g.dma_start(out=SINK, in_=sin_k.ap()), "l1a_wu2", 16)
                ctx.inc(g.dma_start(out=onesc[:, :], in_=ones_cb.ap()), "l1a_wu2", 16)
                ctx.inc(g.dma_start(out=onesr[:, :], in_=ones_rf.ap()), "l1a_wu2", 16)

            @block.tensor
            def _(tensor):
                ctx.wait_all(tensor, "l1a_wu2")
                for h in range(H):
                    pb = h % 2
                    if h >= 2:
                        ctx.wait(tensor, "l1a_qev", 2 * (h - 1))
                    for th in range(2):
                        for lc in range(2):
                            mm = tensor.matmul(bank[2 * pb + th][0:64, :],
                                               WQ[:, lc * 1024 + h * 64: lc * 1024 + (h + 1) * 64],
                                               qlat[:, lc * 1024 + th * 512: lc * 1024 + (th + 1) * 512],
                                               start=(lc == 0), stop=(lc == 1))
                        ctx.inc(mm, "l1a_qpe", 1)

            @block.scalar
            def _(scalar):
                for h in range(H):
                    pb = h % 2
                    for th in range(2):
                        ctx.wait(scalar, "l1a_qpe", 2 * h + th + 1)
                        if h >= 2:
                            ctx.wait(scalar, "l1a_qrope", h - 1)
                        a = scalar.activation(scr[0:64, pb * 1024 + th * 512: pb * 1024 + (th + 1) * 512],
                                              bank[2 * pb + th][0:64, :], AF.Copy)
                        ctx.inc(a, "l1a_qev", 1)

            @block.vector
            def _(vector):
                for h in range(H):
                    pb = h % 2
                    Q = scr[:, pb * 1024:(pb + 1) * 1024]
                    DQ = qall[:, h * 1024:(h + 1) * 1024]
                    ctx.wait(vector, "l1a_qev", 2 * h + 2)
                    ctx.edge(vector.tensor_copy(DQ[0:32, :], Q[32:64, :]), vector, "dveS")
                    ctx.edge(vector.tensor_copy(DQ[32:64, :], Q[0:32, :]), vector, "dveS")
                    ctx.edge(vector.tensor_tensor(DQ[0:64, :], DQ[0:64, :], SINQ, ALU.mult), vector, "dveS")
                    ctx.edge(vector.tensor_tensor(Q[0:64, :], Q[0:64, :], COSQ, ALU.mult), vector, "dveS")
                    a = vector.tensor_tensor(DQ[0:64, :], DQ[0:64, :], Q[0:64, :], ALU.add)
                    ctx.inc(a, "l1a_qrope", 1)

        wq_stack.close()
        with nc.Block() as block:
            @block.tensor
            def _(tensor):
                for h in range(H):
                    ctx.wait(tensor, "l1a_qrope", H)
                    if h > 0:
                        ctx.wait(tensor, "l1a_ex", NKC * h)
                        ctx.wait(tensor, "l1a_fin", 4 * h)
                    for sl in range(8):
                        if sl >= 2:
                            ctx.wait(tensor, "l1a_kev", 8 * h + sl - 1)
                        for lc in range(2):
                            mm = tensor.matmul(bank[sl % 2][0:64, :],
                                               WK[:, lc * 1024 + h * 64: lc * 1024 + (h + 1) * 64],
                                               kvfm[:, lc * 4096 + sl * 512: lc * 4096 + (sl + 1) * 512],
                                               start=(lc == 0), stop=(lc == 1))
                        ctx.inc(mm, "l1a_kpe", 1)
                    for kc in range(NKC):
                        vb = 2 + (kc // 4) % 2
                        if kc >= 8:
                            ctx.wait(tensor, "l1a_vev", 8 * h + kc // 4 - 1)
                        for lc in range(2):
                            mm = tensor.matmul(bank[vb][:, (kc % 4) * 128:(kc % 4 + 1) * 128],
                                               kvfm[:, lc * 4096 + kc * 128: lc * 4096 + (kc + 1) * 128],
                                               WV[:, lc * 2048 + h * 128: lc * 2048 + (h + 1) * 128],
                                               start=(lc == 0), stop=(lc == 1))
                        if kc % 4 == 3:
                            ctx.inc(mm, "l1a_vpe", 1)
                    ctx.wait(tensor, "l1a_sq", 5 * h + 1)
                    for pc in range(2):
                        mm = tensor.matmul(bank[6 + pc][0:1, :], onesc[0:64, 0:1],
                                           scr[0:64, pc * 512:(pc + 1) * 512],
                                           start=True, stop=True)
                        ctx.inc(mm, "l1a_bnd_pe", 1)
                    for rnd in range(4):
                        ctx.wait(tensor, "l1a_sq", 5 * h + 2 + rnd)
                        ctx.wait(tensor, "l1a_bred", 10 * h + 2 * (rnd + 1))
                        for pc in range(2):
                            mm = tensor.matmul(bank[6 + pc][0:1, :], onesc[0:64, 0:1],
                                               exT[0:64, pc * 512:(pc + 1) * 512],
                                               start=True, stop=True)
                            ctx.inc(mm, "l1a_bnd_pe", 1)
                    ctx.wait(tensor, "l1a_negc", h + 1)
                    ctx.wait(tensor, "l1a_bred", 10 * h + 10)
                    mm = tensor.matmul(bank[7][:, 0:1], onesr[:, :], sm[0:1, 1:2], start=True, stop=True)
                    ctx.inc(mm, "l1a_bcc", 1)
                    for kc in range(NKC):
                        ctx.wait(tensor, "l1a_ex", NKC * h + kc)
                        if kc == 0:
                            ctx.wait(tensor, "l1a_negcpp", h + 1)
                            ctx.wait(tensor, "l1a_krope", h + 1)
                        for th in range(2):
                            mm = tensor.matmul(bank[th],
                                               Kt[0:64, kc * 128:(kc + 1) * 128],
                                               qall[0:64, h * 1024 + th * 512: h * 1024 + (th + 1) * 512],
                                               start=True, stop=True)
                        ctx.inc(mm, "l1a_sc", 1)
                        ctx.wait(tensor, "l1a_ex", NKC * h + kc + 1)
                        ctx.wait(tensor, "l1a_vev", 8 * (h + 1))
                        for th in range(2):
                            tensor.matmul(bank[2 + th][0:1, :], onesc[:, 0:1],
                                          exT[:, (kc % 2) * 1024 + th * 512:(kc % 2) * 1024 + (th + 1) * 512],
                                          start=(kc == 0), stop=(kc == NKC - 1))
                        for th in range(2):
                            mm = tensor.matmul(bank[4 + th],
                                               Vt[:, kc * 128:(kc + 1) * 128],
                                               exT[:, (kc % 2) * 1024 + th * 512:(kc % 2) * 1024 + (th + 1) * 512],
                                               start=(kc == 0), stop=(kc == NKC - 1))
                        ctx.inc(mm, "l1a_dn", 1)
                    ctx.wait(tensor, "l1a_rc", h + 1)
                    for th in range(2):
                        mm = tensor.matmul(psT[3][:, th * 512:(th + 1) * 512], onesr[:, :],
                                           rc[0:1, th * 512:(th + 1) * 512], start=True, stop=True)
                    ctx.inc(mm, "l1a_bcr", 1)

            @block.scalar
            def _(scalar):
                for h in range(H):
                    for sl in range(8):
                        ctx.wait(scalar, "l1a_kpe", 8 * h + sl + 1)
                        a = scalar.activation(Kt[0:64, sl * 512:(sl + 1) * 512], bank[sl % 2][0:64, :], AF.Copy)
                        ctx.inc(a, "l1a_kev", 1)
                    for vg in range(8):
                        ctx.wait(scalar, "l1a_vpe", 8 * h + vg + 1)
                        a = scalar.activation(Vt[:, vg * 512:(vg + 1) * 512], bank[2 + vg % 2], AF.Copy)
                        ctx.inc(a, "l1a_vev", 1)
                    ctx.wait(scalar, "l1a_qrope", H)
                    if h > 0:
                        ctx.wait(scalar, "l1a_bnd_pe", 10 * h)
                    a = scalar.activation(scr[0:64, 0:1024],
                                          qall[0:64, h * 1024:(h + 1) * 1024], AF.Square)
                    ctx.inc(a, "l1a_sq", 1)
                    ctx.wait(scalar, "l1a_krope", h + 1)
                    ctx.wait(scalar, "l1a_dn", NKC * h)
                    for rnd in range(4):
                        if rnd >= 1:
                            ctx.wait(scalar, "l1a_bnd_pe", 10 * h + 2 + 2 * rnd)
                        a = scalar.activation(exT[0:64, 0:1024],
                                              Kt[0:64, rnd * 1024:(rnd + 1) * 1024], AF.Square)
                        ctx.inc(a, "l1a_sq", 1)
                    ctx.wait(scalar, "l1a_bmax", h + 1)
                    ctx.edge(scalar.activation(sm[0:1, 1:2], sm[0:1, 4:5], AF.Sqrt), scalar, "actS")
                    a = scalar.activation(sm[0:1, 1:2], sm[0:1, 1:2], AF.Copy, scale=-1.0)
                    ctx.inc(a, "l1a_negc", 1)
                    ctx.wait(scalar, "l1a_bcc", h + 1)
                    a = scalar.activation(sm[:, 2:3], bank[7][:, 0:1], AF.Copy)
                    ctx.inc(a, "l1a_negcpp", 1)
                    for kc in range(NKC):
                        ctx.wait(scalar, "l1a_sc", NKC * h + kc + 1)
                        if kc >= 2:
                            ctx.wait(scalar, "l1a_dn", NKC * h + kc - 1)
                        a = scalar.activation(exT[:, (kc % 2) * 1024:(kc % 2 + 1) * 1024],
                                              psT[0][:, :], AF.Exp, bias=sm[:, 2:3])
                        ctx.inc(a, "l1a_ex", 1)
                    ctx.wait(scalar, "l1a_bcr", h + 1)
                    a = scalar.activation(rb[:, :], psT[3][:, :], AF.Copy)
                    ctx.inc(a, "l1a_rbe", 1)

            @block.vector
            def _(vector):
                for h in range(H):
                    ctx.wait(vector, "l1a_kev", 8 * (h + 1))
                    ctx.wait(vector, "l1a_bnd_pe", 10 * h + 2)
                    for half in range(2):
                        KH = Kt[0:64, half * 2048:(half + 1) * 2048]
                        RH = scr[64:128, 0:2048]
                        TMP = scr[0:64, 0:2048]
                        ctx.edge(vector.tensor_copy(RH[0:32, :], KH[32:64, :]), vector, "dveS")
                        ctx.edge(vector.tensor_copy(RH[32:64, :], KH[0:32, :]), vector, "dveS")
                        ctx.edge(vector.tensor_tensor(RH, RH,
                                                      SINK[:, half * 2048:(half + 1) * 2048], ALU.mult), vector, "dveS")
                        ctx.edge(vector.tensor_copy(TMP, RH), vector, "dveS")
                        ctx.edge(vector.tensor_tensor(KH[0:64, :], KH[0:64, :],
                                                      COSK[:, half * 2048:(half + 1) * 2048], ALU.mult), vector, "dveS")
                        a = vector.tensor_tensor(KH[0:64, :], KH[0:64, :], TMP, ALU.add)
                        ctx.edge(a, vector, "dveS")
                    ctx.inc(vector.tensor_copy(sm[0:1, 20:21], sm[0:1, 20:21]), "l1a_krope", 1)
                    for stage in range(5):
                        ctx.wait(vector, "l1a_bnd_pe", 10 * h + 2 * (stage + 1))
                        for pc in range(2):
                            ctx.edge(vector.tensor_reduce(sm[0:1, 8 + stage * 2 + pc: 9 + stage * 2 + pc],
                                                          bank[6 + pc][0:1, :], AX.X, ALU.max), vector, "dveS")
                        ctx.inc(vector.tensor_copy(sm[0:1, 21:22], sm[0:1, 21:22]), "l1a_bred", 2)
                    ctx.edge(vector.tensor_tensor(sm[0:1, 4:5], sm[0:1, 8:9], sm[0:1, 9:10], ALU.max), vector, "dveS")
                    ctx.edge(vector.tensor_tensor(sm[0:1, 5:6], sm[0:1, 10:11], sm[0:1, 11:12], ALU.max), vector, "dveS")
                    for jj in range(12, 18):
                        ctx.edge(vector.tensor_tensor(sm[0:1, 5:6], sm[0:1, 5:6], sm[0:1, jj:jj + 1], ALU.max), vector, "dveS")
                    a = vector.tensor_tensor(sm[0:1, 4:5], sm[0:1, 4:5], sm[0:1, 5:6], ALU.mult)
                    ctx.inc(a, "l1a_bmax", 1)
                    ctx.wait(vector, "l1a_dn", NKC * (h + 1))
                    ctx.edge(vector.reciprocal(rc[0:1, 0:512], bank[2][0:1, :]), vector, "dveS")
                    a = vector.reciprocal(rc[0:1, 512:1024], bank[3][0:1, :])
                    ctx.inc(a, "l1a_rc", 1)
                    ctx.wait(vector, "l1a_rbe", h + 1)
                    for th in range(2):
                        a = vector.tensor_tensor(O[:, h * 1024 + th * 512: h * 1024 + (th + 1) * 512],
                                                 psT[2][:, th * 512:(th + 1) * 512],
                                                 rb[:, th * 512:(th + 1) * 512], ALU.mult)
                        ctx.inc(a, "l1a_fin", 2)

    if DEBUG_IB:
        dbg_O = ctx.dram_out("dbg_O", [128, 16 * 1024], BF16)
        dbg_q = ctx.dram_out("dbg_q", [128, 16 * 1024], BF16)
        dbg_K = ctx.dram_out("dbg_K", [128, 4096], BF16)
        dbg_V = ctx.dram_out("dbg_V", [128, 4096], BF16)
        dbg_sm = ctx.dram_out("dbg_sm", [128, 32], F32)
        dbg_kvfm = ctx.dram_out("dbg_kvfm", [128, 8192], BF16)
        with nc.Block() as block:
            @block.sync
            def _(sync):
                ctx.inc(sync.dma_start(out=dbg_O.ap(), in_=O[:, :]), "dbg", 16)
                ctx.inc(sync.dma_start(out=dbg_q.ap(), in_=qall[:, :]), "dbg", 16)
                ctx.inc(sync.dma_start(out=dbg_K.ap(), in_=Kt[:, :]), "dbg", 16)
                ctx.inc(sync.dma_start(out=dbg_V.ap(), in_=Vt[:, :]), "dbg", 16)
                ctx.inc(sync.dma_start(out=dbg_sm.ap(), in_=sm[:, :]), "dbg", 16)
                ctx.inc(sync.dma_start(out=dbg_kvfm.ap(), in_=kvfm[:, :]), "dbg", 16)
                ctx.wait_all(sync, "dbg")

    with nc.sbuf_tensor("l1a_wop", [128, 4 * 2048], BF16, side="right") as wop:
        with nc.Block() as block:
            @block.gpsimd
            def _(g):
                for tp in range(4):
                    for c in range(16):
                        j = tp * 16 + c
                        bw = j % 4
                        if j >= 4:
                            ctx.wait(g, "l1a_ope", j - 3)
                        ctx.inc(g.dma_start(out=wop[:, bw * 2048:(bw + 1) * 2048],
                                            in_=wo.ap()[c * 128:(c + 1) * 128, :]), "l1a_wo", 16)

            @block.tensor
            def _(tensor):
                for tp in range(4):
                    for c in range(16):
                        j = tp * 16 + c
                        bw = j % 4
                        ctx.wait(tensor, "l1a_wo", 16 * (j + 1))
                        if c == 0 and tp >= 1:
                            ctx.wait(tensor, "l1a_ores", 8 * tp)
                        for tsp in range(2):
                            ts_ = tp * 2 + tsp
                            for hs in range(4):
                                mm = tensor.matmul(bank[tsp * 4 + hs],
                                                   O[:, c * 1024 + ts_ * 128: c * 1024 + ts_ * 128 + 128],
                                                   wop[:, bw * 2048 + hs * 512: bw * 2048 + (hs + 1) * 512],
                                                   start=(c == 0), stop=(c == 15))
                        ctx.inc(mm, "l1a_ope", 1)

            @block.vector
            def _(vector):
                for tp in range(4):
                    ctx.wait(vector, "l1a_ope", 16 * (tp + 1))
                    for tsp in range(2):
                        ts_ = tp * 2 + tsp
                        for hs in range(4):
                            dstv = res[:, ts_ * D + hs * 512: ts_ * D + (hs + 1) * 512]
                            a = vector.tensor_tensor(dstv, dstv, bank[tsp * 4 + hs], ALU.add)
                            ctx.inc(a, "l1a_ores", 1)





L0_QOF, L0_KOF, L0_VOF, L0_GOF = 0, 1536, 3072, 4608
L0_WT_COLS = 6144


def emit_l0_attn_prep(ctx):
    nc = ctx.nc
    Wq = ctx.dram_in("l0a_Wq", [1536, D], F32)
    Wk = ctx.dram_in("l0a_Wk", [1536, D], F32)
    Wv = ctx.dram_in("l0a_Wv", [1536, D], F32)
    Wg = ctx.dram_in("l0a_Wg", [1536, D], F32)
    Wb = ctx.dram_in("l0a_Wb", [6, D], F32)
    Wa = ctx.dram_in("l0a_Wa", [6, D], F32)
    ident_b = ctx.dram_in("ident_b", [128, 128], BF16)
    w_bf = ctx.dram_tmp("l0a_wbf", [4 * 1536, D], BF16)
    wt = ctx.dram_tmp("l0a_wt", [D, L0_WT_COLS], BF16)
    wbat = ctx.dram_tmp("l0a_wbat", [D, 16], BF16)

    idt = ctx.sbuf("identb", [128, 128], BF16)

    with nc.sbuf_tensor("l0a_cst", [128, 2 * 2048], F32, side="right") as cf, \
         nc.sbuf_tensor("l0a_csb", [128, 2 * 2048], BF16, side="right") as cb, \
         nc.sbuf_tensor("l0a_ba", [128, 4 * 2048], BF16, side="right") as ba, \
         nc.sbuf_tensor("l0a_bat", [128, 64], BF16, side="right") as bat:
        with nc.Block() as block:
            @block.sync
            def _(sync):
                for m, W in enumerate([Wq, Wk, Wv, Wg]):
                    for rr in range(12):
                        j = m * 12 + rr
                        bf = j % 2
                        if j >= 2:
                            ctx.wait(sync, "l0a_cw", 16 * (j - 1))
                        ctx.inc(sync.dma_start(out=cf[:, bf * 2048:(bf + 1) * 2048],
                                               in_=W.ap()[rr * 128:(rr + 1) * 128, :]), "l0a_cl", 16)

            @block.vector
            def _(vector):
                for j in range(48):
                    bf = j % 2
                    ctx.wait(vector, "l0a_cl", 16 * (j + 1))
                    if j >= 2:
                        ctx.wait(vector, "l0a_cw", 16 * (j - 1))
                    a = vector.tensor_copy(cb[:, bf * 2048:(bf + 1) * 2048], cf[:, bf * 2048:(bf + 1) * 2048])
                    ctx.inc(a, "l0a_cc", 1)

            @block.scalar
            def _(scalar):
                for j in range(48):
                    bf = j % 2
                    ctx.wait(scalar, "l0a_cc", j + 1)
                    a = scalar.dma_start(out=w_bf.ap()[j * 128:(j + 1) * 128, :],
                                         in_=cb[:, bf * 2048:(bf + 1) * 2048])
                    ctx.inc(a, "l0a_cw", 16)
                ctx.wait_all(scalar, "l0a_cw")

        with nc.Block() as block:
            @block.sync
            def _(sync):
                for m in range(4):
                    for dc in range(16):
                        j = m * 16 + dc
                        bf = j % 2
                        if j >= 2:
                            ctx.wait(sync, "l0a_tw", 16 * (j - 1))
                        ctx.inc(sync.dma_start_transpose(
                            cb[:, bf * 2048: bf * 2048 + 1536],
                            w_bf.ap()[m * 1536:(m + 1) * 1536, dc * 128:(dc + 1) * 128]), "l0a_tl", 16)
                        ctx.wait(sync, "l0a_tl", 16 * (j + 1))
                        ctx.inc(sync.dma_start(out=wt.ap()[dc * 128:(dc + 1) * 128, m * 1536:(m + 1) * 1536],
                                               in_=cb[:, bf * 2048: bf * 2048 + 1536]), "l0a_tw", 16)
                ctx.wait_all(sync, "l0a_tw")
                ctx.inc(sync.dma_start(out=idt[:, :], in_=ident_b.ap()), "l0a_id", 16)

        wba_bf = ctx.dram_tmp("l0a_wba_bf", [16, D], BF16)
        with nc.Block() as block:
            @block.sync
            def _(sync):
                ctx.wait(sync, "l0a_bz", 1)
                ctx.inc(sync.dma_start(out=cf[0:6, 0:2048], in_=Wb.ap()), "l0a_bal", 16)
                ctx.inc(sync.dma_start(out=cf[8:14, 0:2048], in_=Wa.ap()), "l0a_bal", 16)

            @block.vector
            def _(vector):
                ctx.edge(vector.memset(cf[0:16, 0:2048], 0.0), vector, "dveS")
                ctx.inc(vector.tensor_copy(cf[0:1, 0:1], cf[0:1, 0:1]), "l0a_bz", 1)
                ctx.wait_all(vector, "l0a_bal")
                a = vector.tensor_copy(cb[0:16, 0:2048], cf[0:16, 0:2048])
                ctx.inc(a, "l0a_bac", 1)

            @block.scalar
            def _(scalar):
                ctx.wait(scalar, "l0a_bac", 1)
                a = scalar.dma_start(out=wba_bf.ap(), in_=cb[0:16, 0:2048])
                ctx.inc(a, "l0a_baw", 16)

            @block.sync
            def _(sync):
                ctx.wait_all(sync, "l0a_baw")
                for dc in range(16):
                    bf = dc % 2
                    if dc >= 2:
                        ctx.wait(sync, "l0a_btw", 16 * (dc - 1))
                    ctx.inc(sync.dma_start_transpose(bat[:, bf * 16:(bf + 1) * 16],
                                                     wba_bf.ap()[:, dc * 128:(dc + 1) * 128]), "l0a_btl", 16)
                    ctx.wait(sync, "l0a_btl", 16 * (dc + 1))
                    ctx.inc(sync.dma_start(out=wbat.ap()[dc * 128:(dc + 1) * 128, :],
                                           in_=bat[:, bf * 16:(bf + 1) * 16]), "l0a_btw", 16)
                ctx.wait_all(sync, "l0a_btw")
    return wt, wbat




def emit_l0_attn_proj(ctx, xfm):
    nc = ctx.nc
    psT = ctx.psT
    wt = ctx.dram["l0a_wt"]
    qkv_loc = ctx.dram_tmp("l0a_qkv_loc", [L0_WT_COLS, NT], BF16)

    with nc.sbuf_tensor("l0a_lhs", [128, 2 * 2048], BF16, side="right") as lhs, \
         nc.sbuf_tensor("l0a_ev", [128, 2 * 1024], BF16, side="right") as evt:
        with nc.Block() as block:
            @block.gpsimd
            def _(g):
                for fc in range(48):
                    b = fc % 2
                    if fc >= 2:
                        ctx.wait(g, "l0p_pe", 2 * (fc - 1))
                    ctx.inc(g.dma_start(out=lhs[:, b * 2048:(b + 1) * 2048].rearrange("p (c f) -> p c f", c=16),
                                        in_=wt.ap()[:, fc * 128:(fc + 1) * 128].rearrange("(c p) f -> p c f", p=128)),
                            "l0p_w", 16)

            @block.tensor
            def _(tensor):
                for fc in range(48):
                    b = fc % 2
                    ctx.wait(tensor, "l0p_w", 16 * (fc + 1))
                    if fc >= 2:
                        ctx.wait(tensor, "l0p_ec", fc - 1)
                    for th in range(2):
                        for dd in range(16):
                            mm = tensor.matmul(psT[b][:, th * 512:(th + 1) * 512],
                                               lhs[:, b * 2048 + dd * 128:(b * 2048) + (dd + 1) * 128],
                                               xfm[:, dd * 1024 + th * 512: dd * 1024 + (th + 1) * 512],
                                               start=(dd == 0), stop=(dd == 15))
                    ctx.inc(mm, "l0p_pe", 2)

            @block.scalar
            def _(scalar):
                for fc in range(48):
                    b = fc % 2
                    ctx.wait(scalar, "l0p_pe", 2 * (fc + 1))
                    if fc >= 2:
                        ctx.wait(scalar, "l0p_st", 16 * (fc - 1))
                    a = scalar.activation(evt[:, b * 1024:(b + 1) * 1024], psT[b][:, :], AF.Copy)
                    ctx.inc(a, "l0p_ec", 1)

            @block.sync
            def _(sync):
                for fc in range(48):
                    b = fc % 2
                    ctx.wait(sync, "l0p_ec", fc + 1)
                    ctx.inc(sync.dma_start(out=qkv_loc.ap()[fc * 128:(fc + 1) * 128, :],
                                           in_=evt[:, b * 1024:(b + 1) * 1024]), "l0p_st", 16)
                ctx.wait_all(sync, "l0p_st")


def emit_l0_attn_bg(ctx, xfm):
    nc = ctx.nc
    bank = ctx.bank
    wbat = ctx.dram["l0a_wbat"]
    A_log = ctx.dram_in("l0a_A_log", [1, 16], F32)
    dtb = ctx.dram_in("l0a_dtb", [1, 16], F32)
    ones_rf = ctx.dram_in("ones_row_f", [1, 128], F32)
    idf = ctx.dram_in("ident_f", [128, 128], F32)
    bg_loc = ctx.dram_tmp("l0a_bg_loc", [16, NT], F32)

    with nc.sbuf_tensor("l0a_wba", [128, 256], BF16, side="right") as wba, \
         nc.sbuf_tensor("l0a_bgs", [128, 128], F32, side="right") as bgs, \
         nc.sbuf_tensor("l0a_bgf", [128, 16 * 8], F32, side="right") as bgf, \
         nc.sbuf_tensor("l0a_onr2", [1, 128], F32, side="right") as onesr, \
         nc.sbuf_tensor("l0a_idf2", [128, 128], F32, side="right") as idft:
        with nc.Block() as block:
            @block.gpsimd
            def _(g):
                ctx.inc(g.dma_start(out=wba[:, :].rearrange("p (c f) -> p c f", c=16),
                                    in_=wbat.ap().rearrange("(c p) f -> p c f", p=128)), "bg_c", 16)
                ctx.inc(g.dma_start(out=onesr[:, :], in_=ones_rf.ap()), "bg_c", 16)
                ctx.inc(g.dma_start(out=idft[:, :], in_=idf.ap()), "bg_c", 16)
                ctx.inc(g.dma_start(out=bgs[0:1, 0:16], in_=A_log.ap()), "bg_c", 16)
                ctx.inc(g.dma_start(out=bgs[0:1, 32:48], in_=dtb.ap()), "bg_c", 16)

            @block.scalar
            def _(scalar):
                ctx.wait_all(scalar, "bg_c")
                a = scalar.activation(bgs[0:1, 16:32], bgs[0:1, 0:16], AF.Exp)
                ctx.inc(a, "bg_e", 1)

            @block.vector
            def _(vector):
                ctx.wait(vector, "bg_e", 1)
                a = vector.tensor_scalar_mul(bgs[0:1, 16:32], bgs[0:1, 16:32], -1.0)
                ctx.inc(a, "bg_n", 1)

            @block.tensor
            def _(tensor):
                ctx.wait(tensor, "bg_n", 1)
                mm = tensor.matmul(bank[6][:, 0:16], onesr[:, :], bgs[0:1, 16:32], start=True, stop=True)
                ctx.inc(mm, "bg_b1", 1)
                mm = tensor.matmul(bank[7][:, 0:16], onesr[:, :], bgs[0:1, 32:48], start=True, stop=True)
                ctx.inc(mm, "bg_b2", 1)

            @block.scalar
            def _(scalar):
                ctx.wait(scalar, "bg_b1", 1)
                ctx.edge(scalar.activation(bgs[:, 16:32], bank[6][:, 0:16], AF.Copy), scalar, "actS")
                ctx.wait(scalar, "bg_b2", 1)
                a = scalar.activation(bgs[:, 32:48], bank[7][:, 0:16], AF.Copy)
                ctx.inc(a, "bg_rdy", 1)

        with nc.Block() as block:
            @block.tensor
            def _(tensor):
                for ts_ in range(8):
                    if ts_ >= 1:
                        ctx.wait(tensor, "bg_raw", ts_)
                    for dd in range(16):
                        mm = tensor.matmul(bank[4][:, 0:16],
                                           xfm[:, dd * 1024 + ts_ * 128: dd * 1024 + ts_ * 128 + 128],
                                           wba[:, dd * 16:(dd + 1) * 16],
                                           start=(dd == 0), stop=(dd == 15))
                    ctx.inc(mm, "bg_pe", 1)
                    ctx.wait(tensor, "bg_out", ts_ + 1)
                    if ts_ >= 1:
                        ctx.wait(tensor, "bg_fm", ts_)
                    mm = tensor.transpose(ctx.psT[0][0:16, 0:128], bgs[:, 64:80], idft[:, :])
                    ctx.inc(mm, "bg_tr", 1)

            @block.scalar
            def _(scalar):
                for ts_ in range(8):
                    ctx.wait(scalar, "bg_pe", ts_ + 1)
                    v = ctx.inc(scalar.activation(bgs[:, 48:64], bank[4][:, 0:16], AF.Copy), "bg_raw", 1)
                    ctx.wait(scalar, "bg_raw", v)
                    ctx.edge(scalar.activation(bgs[:, 64:70], bgs[:, 48:54], AF.Sigmoid), scalar, "actS")
                    ctx.wait(scalar, "bg_ga", ts_ + 1)
                    ctx.edge(scalar.activation(bgs[:, 80:86], bgs[:, 112:118], AF.Abs), scalar, "actS")
                    v = ctx.inc(scalar.activation(bgs[:, 86:92], bgs[:, 80:86], AF.Exp, scale=-1.0), "bg_t2", 1)
                    ctx.wait(scalar, "bg_t2", v)
                    ctx.wait(scalar, "bg_t3", ts_ + 1)
                    ctx.edge(scalar.activation(bgs[:, 96:102], bgs[:, 102:108], AF.Ln), scalar, "actS")
                    a = scalar.activation(bgs[:, 120:126], bgs[:, 112:118], AF.Relu)
                    ctx.inc(a, "bg_ln", 1)
                    ctx.wait(scalar, "bg_tr", ts_ + 1)
                    if ts_ >= 1:
                        ctx.wait(scalar, "bg_st", 16 * ts_)
                    a = scalar.activation(bgf[0:16, 0:128], ctx.psT[0][0:16, 0:128], AF.Copy)
                    ctx.inc(a, "bg_fm", 1)

            @block.vector
            def _(vector):
                ctx.wait(vector, "bg_rdy", 1)
                for ts_ in range(8):
                    ctx.wait(vector, "bg_raw", ts_ + 1)
                    a = vector.tensor_tensor(bgs[:, 112:118], bgs[:, 56:62], bgs[:, 40:46], ALU.add)
                    ctx.inc(a, "bg_ga", 1)
                    ctx.wait(vector, "bg_t2", ts_ + 1)
                    a = vector.tensor_scalar_add(bgs[:, 102:108], bgs[:, 86:92], 1.0)
                    ctx.inc(a, "bg_t3", 1)
                    ctx.wait(vector, "bg_ln", ts_ + 1)
                    ctx.edge(vector.tensor_tensor(bgs[:, 96:102], bgs[:, 96:102], bgs[:, 120:126], ALU.add), vector, "dveS")
                    a = vector.tensor_tensor(bgs[:, 72:78], bgs[:, 96:102], bgs[:, 24:30], ALU.mult)
                    ctx.inc(a, "bg_out", 1)

            @block.gpsimd
            def _(g):
                for ts_ in range(8):
                    ctx.wait(g, "bg_fm", ts_ + 1)
                    ctx.inc(g.dma_start(out=bg_loc.ap()[:, ts_ * 128:(ts_ + 1) * 128],
                                        in_=bgf[0:16, 0:128]), "bg_st", 16)
                ctx.wait_all(g, "bg_st")


def emit_l0_attn_ag(ctx):
    nc = ctx.nc
    qkv_loc = ctx.dram["l0a_qkv_loc"]
    bg_loc = ctx.dram["l0a_bg_loc"]
    qkv_ag = ctx.dram_tmp("l0a_qkv_ag", [4 * L0_WT_COLS, NT], BF16)
    bg_ag = ctx.dram_tmp("l0a_bg_ag", [64, NT], F32)
    with nc.Block() as block:
        @block.gpsimd
        def _(g):
            ctx.inc(g.collective_compute("AllGather", ALU.bypass, replica_groups=RG4,
                                         ins=[qkv_loc.ap()], outs=[qkv_ag.ap()]), "l0a_cc", 1)
            ctx.inc(g.collective_compute("AllGather", ALU.bypass, replica_groups=RG4,
                                         ins=[bg_loc.ap()], outs=[bg_ag.ap()]), "l0a_cc", 1)
            ctx.wait_all(g, "l0a_cc")




class Ser:

    def __init__(self, ctx, name):
        self.ctx = ctx
        self.name = name
        self.ops = []

    def add(self, eng, n, fn, dma=False):
        self.ops.append((eng, n, fn, dma))

    def emit_into(self, block):
        ctx = self.ctx
        sem = ctx.sem(self.name)
        base = ctx.semb[self.name] + ctx.val(self.name)

        def mk(eng_name):
            def body(eng):
                cum = base
                for (en, n, fn, dma) in self.ops:
                    amt = (16 if dma else 1)
                    if en == eng_name:
                        eng.wait_ge(sem, cum)
                        r = fn(eng)
                        instrs = r if isinstance(r, list) else [r]
                        assert len(instrs) == n, f"{eng_name} expected {n} got {len(instrs)}"
                        for ins_ in instrs:
                            ins_.then_inc(sem, amt)
                    cum += amt * n
                return None
            return body

        block.sync(mk("sync"))
        block.tensor(mk("tensor"))
        block.scalar(mk("scalar"))
        block.vector(mk("vector"))
        block.gpsimd(mk("gpsimd"))
        tot = sum((16 if dma else 1) * n for (_, n, _, dma) in self.ops)
        ctx.semv[self.name] += tot
        ctx.phys_tot[id(sem)] += tot


def emit_l0_attn_wo_prep(ctx):
    nc = ctx.nc
    Wo = ctx.dram_in("l0a_Wo", [D, 1536], F32)
    wnorm = ctx.dram_in("l0a_wnorm", [1, 1536], F32)
    ones_rf = ctx.dram_in("ones_row_f", [1, 128], F32)
    wo_bf = ctx.dram_tmp("l0a_wo_bf", [D, 1536], BF16)
    wot = ctx.dram_tmp("l0a_wot", [1536, D], BF16)
    psT = ctx.psT

    with nc.sbuf_tensor("wo_cf", [128, 2 * 1536], F32, side="right") as cf,          nc.sbuf_tensor("wo_cb", [128, 2 * 2048], BF16, side="right") as cb,          nc.sbuf_tensor("wo_nb", [128, 1536], F32, side="right") as wnb,          nc.sbuf_tensor("wo_onr", [1, 128], F32, side="right") as onr2:
        ser = Ser(ctx, "l0wo")
        ser.add("gpsimd", 2, lambda g: [
            g.dma_start(out=onr2[:, :], in_=ones_rf.ap()),
            g.dma_start(out=wnb[0:1, :], in_=wnorm.ap())], dma=True)
        ser.add("tensor", 3, lambda t: [
            t.matmul(psT[0][:, 0:512], onr2[:, :], wnb[0:1, 0:512], start=True, stop=True),
            t.matmul(psT[1][:, 0:512], onr2[:, :], wnb[0:1, 512:1024], start=True, stop=True),
            t.matmul(psT[2][:, 0:512], onr2[:, :], wnb[0:1, 1024:1536], start=True, stop=True)])
        ser.add("scalar", 3, lambda sc: [
            sc.activation(wnb[:, 0:512], psT[0][:, 0:512], AF.Copy),
            sc.activation(wnb[:, 512:1024], psT[1][:, 0:512], AF.Copy),
            sc.activation(wnb[:, 1024:1536], psT[2][:, 0:512], AF.Copy)])
        for rr in range(16):
            b = rr % 2
            ser.add("sync", 1, lambda sy, rr=rr, b=b:
                    sy.dma_start(out=cf[:, b * 1536:(b + 1) * 1536],
                                 in_=Wo.ap()[rr * 128:(rr + 1) * 128, :]), dma=True)
            ser.add("vector", 1, lambda v, b=b:
                    v.tensor_tensor(cb[:, b * 2048: b * 2048 + 1536],
                                    cf[:, b * 1536:(b + 1) * 1536], wnb[:, 0:1536], ALU.mult))
            ser.add("scalar", 1, lambda sc, rr=rr, b=b:
                    sc.dma_start(out=wo_bf.ap()[rr * 128:(rr + 1) * 128, :],
                                 in_=cb[:, b * 2048: b * 2048 + 1536]), dma=True)
        for vc in range(12):
            b = vc % 2
            ser.add("sync", 1, lambda sy, vc=vc, b=b:
                    sy.dma_start_transpose(cb[:, b * 2048:(b + 1) * 2048],
                                           wo_bf.ap()[:, vc * 128:(vc + 1) * 128]), dma=True)
            ser.add("sync", 1, lambda sy, vc=vc, b=b:
                    sy.dma_start(out=wot.ap()[vc * 128:(vc + 1) * 128, :],
                                 in_=cb[:, b * 2048:(b + 1) * 2048]), dma=True)
        with nc.Block() as block:
            ser.emit_into(block)


def emit_l0_attn_scan(ctx):
    nc = ctx.nc
    psT = ctx.psT
    qkv_ag = ctx.dram["l0a_qkv_ag"]
    bg_ag = ctx.dram["l0a_bg_ag"]
    conv_q = ctx.dram_in("l0a_conv_q", [1536, 4], F32)
    conv_k = ctx.dram_in("l0a_conv_k", [1536, 4], F32)
    conv_v = ctx.dram_in("l0a_conv_v", [1536, 4], F32)
    uinfo = ctx.dram_in("l0a_uinfo", [1, 32], I32)
    masks = ctx.dram_in("l0a_masks", [128, 384], F32)
    ident_b = ctx.dram_in("ident_b", [128, 128], BF16)
    ones_rf = ctx.dram_in("ones_row_f", [1, 128], F32)
    ones_f1 = ctx.dram_in("ones_f1", [1, 1], F32)
    ones_cb = ctx.dram_in("ones_col_b", [128, 1], BF16)
    o_loc = ctx.dram_tmp("l0a_o_loc", [3 * T, 128], BF16)
    gcb_d = ctx.dram_tmp("l0a_gcb_d", [64, 128], F32)
    row_d = ctx.dram_tmp("l0a_row_d", [1, 1024], F32)

    PAD = 16
    W_ = 4096 + PAD
    psB = psT[1].bitcast(BF16)

    _st = contextlib.ExitStack()
    qb = _st.enter_context(nc.sbuf_tensor("sc_qb", [128, 2 * W_], BF16, side="right"))
    kb = _st.enter_context(nc.sbuf_tensor("sc_kb", [128, 2 * W_], BF16, side="right"))
    vb = _st.enter_context(nc.sbuf_tensor("sc_vb", [128, W_], BF16, side="right"))
    scrc = _st.enter_context(nc.sbuf_tensor("sc_scr", [128, W_], BF16, side="right"))
    cw = _st.enter_context(nc.sbuf_tensor("sc_cw", [128, 24], F32, side="right"))
    ui = _st.enter_context(nc.sbuf_tensor("sc_ui", [1, 32], I32, side="right"))
    msk = _st.enter_context(nc.sbuf_tensor("sc_msk", [128, 384], F32, side="right"))
    idt = _st.enter_context(nc.sbuf_tensor("sc_id", [128, 128], BF16, side="right"))
    onr = _st.enter_context(nc.sbuf_tensor("sc_onr", [1, 128], F32, side="right"))
    on1 = _st.enter_context(nc.sbuf_tensor("sc_on1", [1, 1], F32, side="right"))
    onc = _st.enter_context(nc.sbuf_tensor("sc_onc", [128, 1], BF16, side="right"))
    cm = _st.enter_context(nc.sbuf_tensor("sc_cm", [32, 4 * 128], F32, side="right"))
    rows = _st.enter_context(nc.sbuf_tensor("sc_rows", [1, 512], F32, side="right"))
    rowt = _st.enter_context(nc.sbuf_tensor("sc_rowt", [1, 1024], F32, side="right"))
    cols = _st.enter_context(nc.sbuf_tensor("sc_cols", [128, 8], F32, side="right"))
    kkq = _st.enter_context(nc.sbuf_tensor("sc_KKQ", [128, 2 * 128], F32, side="right"))
    Esc = _st.enter_context(nc.sbuf_tensor("sc_E", [128, 4 * 128], F32, side="right"))
    Fsc = _st.enter_context(nc.sbuf_tensor("sc_F", [128, 4 * 128], F32, side="right"))
    ntb = _st.enter_context(nc.sbuf_tensor("sc_NT", [128, 3 * 128], F32, side="right"))
    powq = _st.enter_context(nc.sbuf_tensor("sc_pow", [128, 7 * 256], F32, side="right"))
    wbuf = _st.enter_context(nc.sbuf_tensor("sc_wb", [128, 2 * 128], F32, side="right"))
    o1s = _st.enter_context(nc.sbuf_tensor("sc_o1", [128, 128], F32, side="right"))
    vks = _st.enter_context(nc.sbuf_tensor("sc_vk", [128, 3 * 128], BF16, side="right"))
    kbar = _st.enter_context(nc.sbuf_tensor("sc_kbar", [128, 2 * 128], F32, side="right"))
    Sb = _st.enter_context(nc.sbuf_tensor("sc_S", [128, 3 * 256], BF16, side="right"))
    ob = _st.enter_context(nc.sbuf_tensor("sc_ob", [128, 128], BF16, side="right"))
    nrm = _st.enter_context(nc.sbuf_tensor("sc_nrm", [128, 512], F32, side="right"))
    with _st:
        M_SU = msk[:, 0:128]; M_IU = msk[:, 128:256]; M_SL = msk[:, 256:384]
        ser = Ser(ctx, "l0sc")

        ser.add("gpsimd", 6, lambda g: [
            g.dma_start(out=ui[:, :], in_=uinfo.ap()),
            g.dma_start(out=msk[:, :], in_=masks.ap()),
            g.dma_start(out=idt[:, :], in_=ident_b.ap()),
            g.dma_start(out=onr[:, :], in_=ones_rf.ap()),
            g.dma_start(out=on1[:, :], in_=ones_f1.ap()),
            g.dma_start(out=onc[:, :], in_=ones_cb.ap())], dma=True)

        gidx = ctx.dram_in("l0a_gidx", [128, 176], U32)
        idxs = _st.enter_context(nc.sbuf_tensor("sc_idx", [128, 176], U32, side="right"))
        from concourse.bass import IndirectOffsetOnAxis
        bg512 = bg_ag.ap().rearrange("r (c p) -> (r c) p", c=8)
        ser.add("gpsimd", 1, lambda g: g.dma_start(out=idxs[:, :], in_=gidx.ap()), dma=True)
        for u in range(3):
            def convld(g, u=u):
                out = []
                for w_i, (srcw, dstc) in enumerate([(conv_q, 0), (conv_q, 4), (conv_k, 8),
                                                    (conv_k, 12), (conv_v, 16)]):
                    icol = 60 + u * 3 + (0 if w_i == 0 else 1 if w_i == 1 else 0 if w_i == 2 else 1 if w_i == 3 else 2)
                    out.append(g.indirect_dma_start(
                        out=cw[:, dstc:dstc + 4], out_offset=None,
                        in_=srcw.ap(), in_offset=IndirectOffsetOnAxis(ap=idxs[:, icol:icol + 1], axis=0)))
                return out
            ser.add("gpsimd", 5, convld, dma=True)
            def bgld(g, u=u):
                return [g.indirect_dma_start(
                            out=cm[0:32, 0:128], out_offset=None,
                            in_=bg512, in_offset=IndirectOffsetOnAxis(ap=idxs[0:32, 69 + u * 2:70 + u * 2], axis=0)),
                        g.indirect_dma_start(
                            out=cm[0:32, 128:256], out_offset=None,
                            in_=bg512, in_offset=IndirectOffsetOnAxis(ap=idxs[0:32, 70 + u * 2:71 + u * 2], axis=0))]
            ser.add("gpsimd", 2, bgld, dma=True)
            for bi in range(5):
                def chunkld(g, u=u, bi=bi):
                    out = []
                    buf = [qb, qb, kb, kb, vb][bi]
                    wslot = [0, 1, 0, 1, 0][bi]
                    for s in range(4):
                        icol = u * 20 + bi * 4 + s
                        out.append(g.indirect_dma_start(
                            out=buf[:, wslot * W_ + PAD + s * 1024: wslot * W_ + PAD + (s + 1) * 1024],
                            out_offset=None, in_=qkv_ag.ap(),
                            in_offset=IndirectOffsetOnAxis(ap=idxs[:, icol:icol + 1], axis=0)))
                    return out
                ser.add("gpsimd", 4, chunkld, dma=True)

            ser.add("vector", 6, lambda v, u=u: [
                v.memset(qb[:, 0:PAD], 0.0), v.memset(qb[:, W_:W_ + PAD], 0.0),
                v.memset(kb[:, 0:PAD], 0.0), v.memset(kb[:, W_:W_ + PAD], 0.0),
                v.memset(vb[:, 0:PAD], 0.0), v.memset(Sb[:, u * 256:(u + 1) * 256], 0.0)])

            ser.add("vector", 1, lambda v: v.tensor_copy(cm[0:32, 256:384], cm[0:32, 0:128]))
            cur = [256]
            for kk in [1, 2, 4, 8, 16, 32, 64]:
                def step(v, kk=kk, cur=cur):
                    c0 = cur[0]; o0 = 640 - c0
                    r = [v.tensor_copy(cm[0:32, o0:o0 + kk], cm[0:32, c0:c0 + kk]),
                         v.tensor_tensor(cm[0:32, o0 + kk:o0 + 128], cm[0:32, c0 + kk:c0 + 128],
                                         cm[0:32, c0:c0 + 128 - kk], ALU.add)]
                    cur[0] = o0
                    return r
                ser.add("vector", 2, step)
            ser.add("vector", 1, lambda v: v.tensor_copy(cm[0:32, 256:384], cm[0:32, 384:512]))

            ser.add("sync", 2, lambda sy: [
                sy.dma_start(out=gcb_d.ap()[0:32, :], in_=cm[0:32, 256:384]),
                sy.dma_start(out=gcb_d.ap()[32:64, :], in_=cm[0:32, 128:256])], dma=True)

            for (buf, ch, cwo) in [(qb, 0, 0), (qb, 1, 4), (kb, 0, 8), (kb, 1, 12), (vb, 0, 16)]:
                def conv(v, buf=buf, ch=ch, cwo=cwo):
                    base = ch * W_ + PAD
                    out = []
                    for tap in range(4):
                        sh = 3 - tap
                        wcol = cw[:, cwo + tap: cwo + tap + 1]
                        xv = buf[:, base - sh: base - sh + 4096]
                        if tap == 0:
                            out.append(v.tensor_scalar(scrc[:, PAD:PAD + 4096], xv, wcol, None, ALU.mult))
                        else:
                            out.append(v.scalar_tensor_tensor(scrc[:, PAD:PAD + 4096], xv, wcol,
                                                              scrc[:, PAD:PAD + 4096], ALU.mult, ALU.add))
                    return out
                ser.add("vector", 4, conv)
                ser.add("scalar", 1, lambda sc, buf=buf, ch=ch:
                        sc.activation(buf[:, ch * W_ + PAD: ch * W_ + PAD + 4096],
                                      scrc[:, PAD:PAD + 4096], AF.Silu))

            for which, buf in [(0, qb), (1, kb)]:
                for pc in range(8):
                    ser.add("scalar", 2, lambda sc, buf=buf, pc=pc: [
                        sc.activation(scrc[:, 0:512], buf[:, 0 * W_ + PAD + pc * 512: 0 * W_ + PAD + (pc + 1) * 512], AF.Square),
                        sc.activation(scrc[:, 512:1024], buf[:, 1 * W_ + PAD + pc * 512: 1 * W_ + PAD + (pc + 1) * 512], AF.Square)])
                    ser.add("tensor", 2, lambda t: [
                        t.matmul(psT[0][0:1, 0:512], onc[:, :], scrc[:, 0:512], start=True, stop=False),
                        t.matmul(psT[0][0:1, 0:512], onc[:, :], scrc[:, 512:1024], start=False, stop=True)])
                    ser.add("vector", 1, lambda v: v.tensor_scalar_add(nrm[0:1, 0:512], psT[0][0:1, 0:512], 1e-6))
                    ser.add("scalar", 1, lambda sc: sc.activation(nrm[0:1, 0:512], nrm[0:1, 0:512], AF.Ln))
                    ser.add("scalar", 1, lambda sc:
                            sc.activation(nrm[0:1, 0:512], nrm[0:1, 0:512], AF.Exp, scale=-0.5))
                    if which == 0:
                        ser.add("vector", 1, lambda v:
                                v.tensor_scalar_mul(nrm[0:1, 0:512], nrm[0:1, 0:512], 1.0 / 16.0))
                    ser.add("tensor", 1, lambda t: t.matmul(psT[0][:, 0:512], onr[:, :], nrm[0:1, 0:512],
                                                            start=True, stop=True))
                    ser.add("vector", 2, lambda v, buf=buf, pc=pc: [
                        v.tensor_tensor(buf[:, 0 * W_ + PAD + pc * 512: 0 * W_ + PAD + (pc + 1) * 512],
                                        buf[:, 0 * W_ + PAD + pc * 512: 0 * W_ + PAD + (pc + 1) * 512],
                                        psT[0][:, 0:512], ALU.mult),
                        v.tensor_tensor(buf[:, 1 * W_ + PAD + pc * 512: 1 * W_ + PAD + (pc + 1) * 512],
                                        buf[:, 1 * W_ + PAD + pc * 512: 1 * W_ + PAD + (pc + 1) * 512],
                                        psT[0][:, 0:512], ALU.mult)])

            for c in range(32):
                T0 = PAD + c * 128
                ser.add("sync", 2, lambda sy, c=c: [
                    sy.dma_start(out=rows[0:1, 0:128], in_=gcb_d.ap()[c:c + 1, :]),
                    sy.dma_start(out=rows[0:1, 128:256], in_=gcb_d.ap()[32 + c:33 + c, :])], dma=True)
                ser.add("scalar", 2, lambda sc: [
                    sc.activation(rows[0:1, 256:384], rows[0:1, 0:128], AF.Exp),
                    sc.activation(rows[0:1, 384:512], rows[0:1, 0:128], AF.Exp, scale=-1.0,
                                  bias=rows[0:1, 127:128])])

                def phase1(t, u=u, c=c, T0=T0):
                    out = []
                    out.append(t.matmul(psT[0][:, 0:128], kb[:, 0 * W_ + T0:0 * W_ + T0 + 128],
                                        kb[:, 0 * W_ + T0:0 * W_ + T0 + 128], start=True, stop=False))
                    out.append(t.matmul(psT[0][:, 0:128], kb[:, 1 * W_ + T0:1 * W_ + T0 + 128],
                                        kb[:, 1 * W_ + T0:1 * W_ + T0 + 128], start=False, stop=True))
                    out.append(t.matmul(psT[0][:, 128:256], kb[:, 0 * W_ + T0:0 * W_ + T0 + 128],
                                        qb[:, 0 * W_ + T0:0 * W_ + T0 + 128], start=True, stop=False))
                    out.append(t.matmul(psT[0][:, 128:256], kb[:, 1 * W_ + T0:1 * W_ + T0 + 128],
                                        qb[:, 1 * W_ + T0:1 * W_ + T0 + 128], start=False, stop=True))
                    out.append(t.matmul(psT[0][:, 512:640], onr[:, :], rows[0:1, 0:128], start=True, stop=True))
                    out.append(t.matmul(psT[0][:, 640:768], onr[:, :], rows[0:1, 128:256], start=True, stop=True))
                    out.append(t.matmul(psT[1][:, 0:1], rows[0:1, 0:128], on1[:, :], start=True, stop=True))
                    out.append(t.matmul(psT[1][:, 1:2], rows[0:1, 128:256], on1[:, :], start=True, stop=True))
                    out.append(t.matmul(psT[1][:, 2:3], rows[0:1, 256:384], on1[:, :], start=True, stop=True))
                    out.append(t.matmul(psT[1][:, 3:4], rows[0:1, 384:512], on1[:, :], start=True, stop=True))
                    out.append(t.matmul(psT[1][:, 4:5], onr[:, :], rows[0:1, 383:384], start=True, stop=True))
                    out.append(t.transpose(psB[:, 1024:1152], vb[:, T0:T0 + 128], idt[:, :]))
                    out.append(t.transpose(psB[:, 1152:1280], kb[:, 0 * W_ + T0:0 * W_ + T0 + 128], idt[:, :]))
                    out.append(t.transpose(psB[:, 1280:1408], kb[:, 1 * W_ + T0:1 * W_ + T0 + 128], idt[:, :]))
                    out.append(t.matmul(psT[2][:, 0:128], kb[:, 0 * W_ + T0:0 * W_ + T0 + 128],
                                        Sb[:, u * 256:u * 256 + 128], start=True, stop=False))
                    out.append(t.matmul(psT[2][:, 0:128], kb[:, 1 * W_ + T0:1 * W_ + T0 + 128],
                                        Sb[:, u * 256 + 128:u * 256 + 256], start=False, stop=True))
                    out.append(t.matmul(psT[2][:, 512:640], qb[:, 0 * W_ + T0:0 * W_ + T0 + 128],
                                        Sb[:, u * 256:u * 256 + 128], start=True, stop=False))
                    out.append(t.matmul(psT[2][:, 512:640], qb[:, 1 * W_ + T0:1 * W_ + T0 + 128],
                                        Sb[:, u * 256 + 128:u * 256 + 256], start=False, stop=True))
                    return out
                ser.add("tensor", 18, phase1)

                ser.add("scalar", 3, lambda sc: [
                    sc.activation(kkq[:, 0:256], psT[0][:, 0:256], AF.Copy),
                    sc.activation(cols[:, 0:5], psT[1][:, 0:5], AF.Copy),
                    sc.activation(vks[:, 0:384], psB[:, 1024:1408], AF.Copy)])

                ser.add("vector", 8, lambda v, u=u: [
                    v.tensor_scalar(Esc[:, 0:128], psT[0][:, 512:640], cols[:, 0:1], None, ALU.subtract),
                    v.tensor_scalar_mul(nrm[:, 0:128], psT[0][:, 640:768], -1.0),
                    v.tensor_scalar_mul(cols[:, 5:6], cols[:, 1:2], -1.0),
                    v.tensor_scalar(nrm[:, 128:256], psT[2][:, 0:128], cols[:, 2:3], None, ALU.mult),
                    v.tensor_scalar(o1s[:, :], psT[2][:, 512:640], cols[:, 2:3], None, ALU.mult),
                    v.tensor_scalar(Sb[:, u * 256:u * 256 + 128], Sb[:, u * 256:u * 256 + 128],
                                    cols[:, 4:5], None, ALU.mult),
                    v.tensor_scalar(Sb[:, u * 256 + 128:u * 256 + 256], Sb[:, u * 256 + 128:u * 256 + 256],
                                    cols[:, 4:5], None, ALU.mult),
                    v.tensor_scalar(kbar[:, 0:128], vks[:, 128:256], cols[:, 3:4], None, ALU.mult)])
                ser.add("vector", 4, lambda v: [
                    v.tensor_scalar(kbar[:, 128:256], vks[:, 256:384], cols[:, 3:4], None, ALU.mult),
                    v.tensor_scalar_mul(Esc[:, 128:256], Esc[:, 0:128], -1.0),
                    v.tensor_scalar_min(Esc[:, 256:384], Esc[:, 0:128], 0.0),
                    v.tensor_tensor(nrm[:, 128:256], vks[:, 0:128], nrm[:, 128:256], ALU.subtract)])
                ser.add("vector", 1, lambda v: v.tensor_scalar_min(Esc[:, 128:256], Esc[:, 128:256], 0.0))
                ser.add("scalar", 2, lambda sc: [
                    sc.activation(Esc[:, 0:128], Esc[:, 256:384], AF.Exp),
                    sc.activation(Esc[:, 384:512], Esc[:, 128:256], AF.Exp)])
                ser.add("vector", 3, lambda v: [
                    v.tensor_tensor(Fsc[:, 0:128], kkq[:, 0:128], Esc[:, 0:128], ALU.mult),
                    v.tensor_tensor(Fsc[:, 128:256], kkq[:, 128:256], Esc[:, 0:128], ALU.mult),
                    v.tensor_tensor(Fsc[:, 256:384], kkq[:, 0:128], Esc[:, 384:512], ALU.mult)])
                ser.add("vector", 3, lambda v: [
                    v.tensor_tensor(Fsc[:, 0:128], Fsc[:, 0:128], M_SU, ALU.mult),
                    v.tensor_tensor(Fsc[:, 128:256], Fsc[:, 128:256], M_IU, ALU.mult),
                    v.tensor_tensor(Fsc[:, 256:384], Fsc[:, 256:384], M_SL, ALU.mult)])
                ser.add("vector", 4, lambda v: [
                    v.tensor_tensor(ntb[:, 0:128], Fsc[:, 0:128], nrm[:, 0:128], ALU.mult),
                    v.tensor_copy(ntb[:, 256:384], Fsc[:, 128:256]),
                    v.tensor_scalar(ntb[:, 128:256], Fsc[:, 256:384], cols[:, 5:6], None, ALU.mult),
                    v.tensor_scalar(wbuf[:, 0:128], nrm[:, 128:256], cols[:, 1:2], None, ALU.mult)])
                for lvl in range(6):
                    def sq(t, lvl=lvl):
                        Qp = ntb[:, 0:128] if lvl == 0 else powq[:, (lvl - 1) * 256 + 128:lvl * 256]
                        Pp = ntb[:, 128:256] if lvl == 0 else powq[:, (lvl - 1) * 256:(lvl - 1) * 256 + 128]
                        return [t.matmul(psT[3][:, 0:128], Qp, Pp, start=True, stop=True),
                                t.matmul(psT[3][:, 128:256], Pp, Qp, start=True, stop=True)]
                    ser.add("tensor", 2, sq)
                    ser.add("scalar", 1, lambda sc, lvl=lvl:
                            sc.activation(powq[:, lvl * 256:(lvl + 1) * 256], psT[3][:, 0:256], AF.Copy))
                for st in range(7):
                    def ap(t, st=st):
                        Qs = ntb[:, 0:128] if st == 0 else powq[:, (st - 1) * 256 + 128:st * 256]
                        w_in = wbuf[:, (st % 2) * 128:(st % 2 + 1) * 128]
                        return t.matmul(psT[2][:, 0:128], Qs, w_in, start=True, stop=True)
                    ser.add("tensor", 1, ap)
                    ser.add("vector", 1, lambda v, st=st:
                            v.tensor_tensor(wbuf[:, ((st + 1) % 2) * 128:((st + 1) % 2 + 1) * 128],
                                            wbuf[:, (st % 2) * 128:(st % 2 + 1) * 128],
                                            psT[2][:, 0:128], ALU.add))
                ser.add("tensor", 3, lambda t, u=u: [
                    t.matmul(psT[2][:, 512:640], ntb[:, 256:384], wbuf[:, 128:256], start=True, stop=True),
                    t.matmul(psT[3][:, 512:640], kbar[:, 0:128], wbuf[:, 128:256], start=True, stop=True),
                    t.matmul(psT[3][:, 640:768], kbar[:, 128:256], wbuf[:, 128:256], start=True, stop=True)])
                ser.add("vector", 3, lambda v, u=u: [
                    v.tensor_tensor(ob[:, :], o1s[:, :], psT[2][:, 512:640], ALU.add),
                    v.tensor_tensor(Sb[:, u * 256:u * 256 + 128], Sb[:, u * 256:u * 256 + 128],
                                    psT[3][:, 512:640], ALU.add),
                    v.tensor_tensor(Sb[:, u * 256 + 128:u * 256 + 256], Sb[:, u * 256 + 128:u * 256 + 256],
                                    psT[3][:, 640:768], ALU.add)])
                ser.add("gpsimd", 1, lambda g, u=u, c=c:
                        g.dma_start(out=o_loc.ap()[u * 4096 + c * 128: u * 4096 + (c + 1) * 128, :],
                                    in_=ob[:, :]), dma=True)

        with nc.Block() as block:
            ser.emit_into(block)
    if DEBUG_IB:
        dbg_o = ctx.dram_out("dbg_o_loc", [3 * T, 128], BF16)
        with nc.Block() as block:
            @block.sync
            def _(sync):
                ctx.inc(sync.dma_start(out=dbg_o.ap(), in_=o_loc.ap()), "dbgo", 16)
                ctx.wait_all(sync, "dbgo")


def emit_l0_attn_combine(ctx):
    nc = ctx.nc
    bank = ctx.bank
    psT = ctx.psT
    o_loc = ctx.dram["l0a_o_loc"]
    qkv_loc2 = ctx.dram["l0a_qkv_loc"]
    wot = ctx.dram["l0a_wot"]
    uinfo = ctx.dram["l0a_uinfo"]
    res = ctx.sb["res"]
    o_ag = ctx.dram_tmp("l0a_o_ag", [4 * 3 * T, 128], BF16)
    og_d = ctx.dram_tmp("l0a_og_d", [NT, 1536], BF16)

    with nc.Block() as block:
        @block.gpsimd
        def _(g):
            ctx.inc(g.collective_compute("AllGather", ALU.bypass, replica_groups=RG4,
                                         ins=[o_loc.ap()], outs=[o_ag.ap()]), "l0a_cc2", 1)
            ctx.wait_all(g, "l0a_cc2")

    with nc.sbuf_tensor("cb_otm", [128, 8 * 1536], BF16, side="right") as otm, \
         nc.sbuf_tensor("cb_gate", [128, 8 * 1536], BF16, side="right") as gsb, \
         nc.sbuf_tensor("cb_ogf", [128, 12 * 1024], BF16, side="right") as ogf, \
         nc.sbuf_tensor("cb_scr", [128, 1536], BF16, side="right") as scr3, \
         nc.sbuf_tensor("cb_ms", [128, 32], F32, side="right") as ms, \
         nc.sbuf_tensor("cb_ui", [1, 32], I32, side="right") as ui2, \
         nc.sbuf_tensor("cb_idx", [128, 96], U32, side="right") as cidx, \
         nc.sbuf_tensor("cb_wos", [128, 2 * 2048], BF16, side="right") as wos:
        ser = Ser(ctx, "l0cb")
        ser.add("gpsimd", 1, lambda g: g.dma_start(out=cidx[:, :], in_=ctx.dram["l0a_gidx"].ap()[:, 75:171][:, 0:96]), dma=True)
        ser.add("gpsimd", 1, lambda g: g.dma_start(out=ui2[:, :], in_=uinfo.ap()), dma=True)
        ser.add("sync", 8, lambda sy: [
            sy.dma_start_transpose(gsb[:, tt * 1536:(tt + 1) * 1536],
                                   qkv_loc2.ap()[4608:6144, tt * 128:(tt + 1) * 128])
            for tt in range(8)], dma=True)

        gidx2 = ctx.dram["l0a_gidx"]
        from concourse.bass import IndirectOffsetOnAxis
        idxs2 = None
        def idxld(g):
            return g.dma_start(out=ui2[:, :], in_=uinfo.ap())
        def oloads(g):
            out = []
            for s in range(4):
                for j in range(3):
                    uu = 3 * s + j
                    h = uu // 2
                    half = uu % 2
                    vof = 256 * h + 128 * half
                    for tt in range(8):
                        icol = 75 + uu * 8 + tt
                        out.append(g.indirect_dma_start(
                            out=otm[:, tt * 1536 + vof: tt * 1536 + vof + 128],
                            out_offset=None, in_=o_ag.ap(),
                            in_offset=IndirectOffsetOnAxis(ap=cidx[:, icol - 75:icol - 74], axis=0)))
            return out
        ser.add("gpsimd", 96, oloads, dma=True)

        for ts_ in range(8):
            ser.add("scalar", 1, lambda sc, ts_=ts_:
                    sc.activation(scr3[:, :], otm[:, ts_ * 1536:(ts_ + 1) * 1536], AF.Square))
            ser.add("vector", 6, lambda v: [
                v.tensor_reduce(ms[:, hh:hh + 1], scr3[:, hh * 256:(hh + 1) * 256], AX.X, ALU.add)
                for hh in range(6)])
            ser.add("vector", 1, lambda v: v.tensor_scalar(ms[:, 8:14], ms[:, 0:6], 1.0 / 256.0, 1e-5,
                                                           ALU.mult, ALU.add))
            ser.add("scalar", 1, lambda sc: sc.activation(ms[:, 8:14], ms[:, 8:14], AF.Ln))
            ser.add("scalar", 1, lambda sc: sc.activation(ms[:, 8:14], ms[:, 8:14], AF.Exp, scale=-0.5))
            ser.add("vector", 6, lambda v, ts_=ts_: [
                v.tensor_scalar(otm[:, ts_ * 1536 + hh * 256: ts_ * 1536 + (hh + 1) * 256],
                                otm[:, ts_ * 1536 + hh * 256: ts_ * 1536 + (hh + 1) * 256],
                                ms[:, 8 + hh:9 + hh], None, ALU.mult)
                for hh in range(6)])
            ser.add("scalar", 1, lambda sc, ts_=ts_:
                    sc.activation(gsb[:, ts_ * 1536:(ts_ + 1) * 1536],
                                  gsb[:, ts_ * 1536:(ts_ + 1) * 1536], AF.Silu))
            ser.add("vector", 1, lambda v, ts_=ts_:
                    v.tensor_tensor(otm[:, ts_ * 1536:(ts_ + 1) * 1536],
                                    otm[:, ts_ * 1536:(ts_ + 1) * 1536],
                                    gsb[:, ts_ * 1536:(ts_ + 1) * 1536], ALU.mult))
            ser.add("sync", 1, lambda sy, ts_=ts_:
                    sy.dma_start(out=og_d.ap()[ts_ * 128:(ts_ + 1) * 128, :],
                                 in_=otm[:, ts_ * 1536:(ts_ + 1) * 1536]), dma=True)
        ser.add("sync", 12, lambda sy: [
            sy.dma_start_transpose(ogf[:, vc * 1024:(vc + 1) * 1024],
                                   og_d.ap()[:, vc * 128:(vc + 1) * 128]) for vc in range(12)], dma=True)
        for tp in range(4):
            for vc in range(12):
                ser.add("sync", 1, lambda sy, vc=vc:
                        sy.dma_start(out=wos[:, (vc % 2) * 2048:(vc % 2 + 1) * 2048],
                                     in_=wot.ap()[vc * 128:(vc + 1) * 128, :]), dma=True)
                def omm(t, tp=tp, vc=vc):
                    out = []
                    for tsp in range(2):
                        ts_ = tp * 2 + tsp
                        for hs in range(4):
                            out.append(t.matmul(bank[tsp * 4 + hs],
                                                ogf[:, vc * 1024 + ts_ * 128: vc * 1024 + ts_ * 128 + 128],
                                                wos[:, (vc % 2) * 2048 + hs * 512:(vc % 2) * 2048 + (hs + 1) * 512],
                                                start=(vc == 0), stop=(vc == 11)))
                    return out
                ser.add("tensor", 8, omm)
            def radd(v, tp=tp):
                out = []
                for tsp in range(2):
                    ts_ = tp * 2 + tsp
                    for hs in range(4):
                        dstv = res[:, ts_ * D + hs * 512: ts_ * D + (hs + 1) * 512]
                        out.append(v.tensor_tensor(dstv, dstv, bank[tsp * 4 + hs], ALU.add))
                return out
            ser.add("vector", 8, radd)

        with nc.Block() as block:
            ser.emit_into(block)



def build_program(blocks, detect_races=True):
    nc = bass.Bass("TRN2", target_bir_lowering=False, detect_race_conditions=detect_races)
    stack = contextlib.ExitStack()
    ctx = Ctx(nc, stack)
    with stack:
        emit_load_residual(ctx)
        for b in blocks:
            if b == "ln":
                with nc.sbuf_tensor("xfm_ln", [128, 16 * 1024], BF16, side="right") as xfm:
                    emit_layernorm(ctx, "lnonly", xfm)
                    xfm_dbg = ctx.dram_out("xfm_dbg", [2048, 1024], BF16)
                    stat_dbg = ctx.dram_out("stat_dbg", [128, 64], F32)
                    with nc.Block() as block:
                        @block.sync
                        def _(sync):
                            ctx.inc(sync.dma_start(out=xfm_dbg.ap().rearrange("(c p) t -> p c t", p=128),
                                                   in_=xfm[:, :].rearrange("p (c t) -> p c t", c=16)), "xfmdbg", 16)
                            ctx.inc(sync.dma_start(out=stat_dbg.ap(), in_=ctx.sb["ln_stat"][:, 0:64]), "xfmdbg", 16)
                            ctx.wait_all(sync, "xfmdbg")
            elif b == "l1fgu":
                with nc.sbuf_tensor("xfm_l1f", [128, 16 * 1024], BF16, side="right") as xfm:
                    emit_layernorm(ctx, "l1f", xfm)
                    emit_l1_ffn_gateup(ctx, xfm)
            elif b == "l0a":
                emit_l0_attn_prep(ctx)
                with nc.sbuf_tensor("xfm_l0a", [128, 16 * 1024], BF16, side="right") as xfm:
                    emit_layernorm(ctx, "l0a", xfm)
                    emit_l0_attn_proj(ctx, xfm)
                    emit_l0_attn_bg(ctx, xfm)
                emit_l0_attn_ag(ctx)
                emit_l0_attn_wo_prep(ctx)
                emit_l0_attn_scan(ctx)
                emit_l0_attn_combine(ctx)
            elif b == "l0f":
                emit_l0_ffn_dequant(ctx)
                with nc.sbuf_tensor("xfm_l0f", [128, 16 * 1024], BF16, side="right") as xfm:
                    emit_layernorm(ctx, "l0f", xfm)
                    emit_l0_ffn_main(ctx, xfm)
                emit_l0_ffn_down(ctx)
            elif b == "l1a":
                with nc.sbuf_tensor("l1a_O_t", [128, 16 * 1024], BF16, side="right") as O, \
                     nc.sbuf_tensor("l1a_qlat_t", [128, 2 * 1024], BF16, side="right") as qlat:
                    with nc.sbuf_tensor("xfm_l1a", [128, 16 * 1024], BF16, side="right") as xfm:
                        emit_layernorm(ctx, "l1a", xfm)
                        emit_l1_attn_latents(ctx, xfm, qlat)
                    emit_l1_attn_main(ctx, qlat, O)
            elif b == "l1f":
                with nc.sbuf_tensor("xfm_l1f", [128, 16 * 1024], BF16, side="right") as xfm:
                    emit_layernorm(ctx, "l1f", xfm)
                    emit_l1_ffn_gateup(ctx, xfm)
                emit_l1_ffn_down(ctx)
            ctx.retire()
        emit_store_residual(ctx)
    return ctx, nc


_ROPE_CACHE = {}


def _rope_consts():
    if "v" not in _ROPE_CACHE:
        HALF = 64
        inv_freq = 1.0 / 10000.0 ** (np.arange(0, HALF, 2, dtype=np.float64) / HALF)
        t = np.arange(T, dtype=np.float64)
        freqs = t[:, None] * inv_freq[None, :]
        emb = np.concatenate([freqs, freqs], axis=-1)
        cos = np.cos(emb).T
        sin = np.sin(emb).T
        sin_eff = sin.copy()
        sin_eff[0:32, :] = -sin[0:32, :]
        scale = 1.0 / np.sqrt(HALF)
        cosq = (cos * scale).astype(ml_dtypes.bfloat16)
        sinq = (sin_eff * scale).astype(ml_dtypes.bfloat16)
        cosk = cos.astype(ml_dtypes.bfloat16)
        sink = sin_eff.astype(ml_dtypes.bfloat16)
        _ROPE_CACHE["v"] = (cosq, sinq, cosk, sink)
    return _ROPE_CACHE["v"]


def make_in_maps(blocks, inputs, x_full):
    in_maps = []
    for r in range(8):
        b, s = r // 4, r % 4
        im = {"x_shard": np.ascontiguousarray(x_full[b, s * NT:(s + 1) * NT, :])}
        if "l0a" in blocks:
            im["l0a_Wq"] = inputs["L0_attn_Wq"]
            im["l0a_Wk"] = inputs["L0_attn_Wk"]
            im["l0a_Wv"] = inputs["L0_attn_Wv"]
            im["l0a_Wg"] = inputs["L0_attn_Wg"]
            im["l0a_Wb"] = inputs["L0_attn_Wb"]
            im["l0a_Wa"] = inputs["L0_attn_Wa"]
            im["l0a_Wo"] = np.ascontiguousarray(inputs["L0_attn_Wo"])
            im["l0a_wnorm"] = np.tile(np.asarray(inputs["L0_attn_o_norm_w"]), 6)[None, :].astype(np.float32)
            im["l0a_conv_q"] = inputs["L0_attn_conv_q"]
            im["l0a_conv_k"] = inputs["L0_attn_conv_k"]
            im["l0a_conv_v"] = inputs["L0_attn_conv_v"]
            al = np.zeros((1, 16), np.float32); al[0, 8:14] = np.asarray(inputs["L0_attn_A_log"])
            dt_ = np.zeros((1, 16), np.float32); dt_[0, 8:14] = np.asarray(inputs["L0_attn_dt_bias"])
            im["l0a_A_log"] = al
            im["l0a_dtb"] = dt_
            ident = np.eye(128)
            im["ident_b"] = ident.astype(ml_dtypes.bfloat16)
            im["ident_f"] = ident.astype(np.float32)
            m_su = np.triu(np.ones((128, 128), np.float32), 1)
            m_iu = np.triu(np.ones((128, 128), np.float32), 0)
            m_sl = np.tril(np.ones((128, 128), np.float32), -1)
            im["l0a_masks"] = np.concatenate([m_su, m_iu, m_sl], axis=1)
            im["ones_row_f"] = np.ones((1, 128), np.float32)
            im["ones_f1"] = np.ones((1, 1), np.float32)
            im["ones_col_b"] = np.ones((128, 1), dtype=ml_dtypes.bfloat16)
            uin = np.zeros((1, 32), np.int32)
            for j in range(3):
                uu = 3 * r + j
                h = (uu % 12) // 2
                half = uu % 2
                uin[0, j * 8 + 0] = 256 * h
                uin[0, j * 8 + 1] = 256 * h + 128 * half
                uin[0, j * 8 + 2] = h
            uin[0, 24] = s
            im["l0a_uinfo"] = uin
            gi = np.zeros((128, 176), np.uint32)
            pp = np.arange(128, dtype=np.uint32)
            for j in range(3):
                uu_g = 3 * r + j
                h = (uu_g % 12) // 2
                half = uu_g % 2
                qof = 256 * h
                vof = 256 * h + 128 * half
                for bi in range(5):
                    off = [qof, qof + 128, 1536 + qof, 1536 + qof + 128, 3072 + vof][bi]
                    for sl in range(4):
                        gi[:, j * 20 + bi * 4 + sl] = sl * 6144 + off + pp
                gi[:, 60 + j * 3 + 0] = qof + pp
                gi[:, 60 + j * 3 + 1] = qof + 128 + pp
                gi[:, 60 + j * 3 + 2] = vof + pp
                cc32 = np.arange(32, dtype=np.uint32)
                gi[0:32, 69 + j * 2 + 0] = (8 + h + 16 * (cc32 // 8)) * 8 + (cc32 % 8)
                gi[0:32, 69 + j * 2 + 1] = (h + 16 * (cc32 // 8)) * 8 + (cc32 % 8)
            for uu2 in range(12):
                for tt in range(8):
                    gi[:, 75 + uu2 * 8 + tt] = uu2 * 4096 + s * 1024 + tt * 128 + pp
            im["l0a_gidx"] = gi
        if "l0f" in blocks:
            im["l0f_gWp"] = np.ascontiguousarray(inputs["L0_ffn_gate_W_packed"][r * 2048:(r + 1) * 2048]).astype(np.int32)
            im["l0f_gSc"] = np.repeat(np.ascontiguousarray(inputs["L0_ffn_gate_scales"][r * 2048:(r + 1) * 2048]), 32, axis=1)
            im["l0f_uWp"] = np.ascontiguousarray(inputs["L0_ffn_up_W_packed"][r * 2048:(r + 1) * 2048]).astype(np.int32)
            im["l0f_uSc"] = np.repeat(np.ascontiguousarray(inputs["L0_ffn_up_scales"][r * 2048:(r + 1) * 2048]), 32, axis=1)
            im["l0f_dWp"] = np.ascontiguousarray(inputs["L0_ffn_down_W_packed"][r * 256:(r + 1) * 256]).astype(np.int32)
            im["l0f_dSc"] = np.repeat(np.ascontiguousarray(inputs["L0_ffn_down_scales"][r * 256:(r + 1) * 256]), 32, axis=1)
        if "l1a" in blocks:
            im["l1a_wq_d"] = inputs["L1_attn_wq_d"]
            im["l1a_wkv_d"] = inputs["L1_attn_wkv_d"]
            im["l1a_wq_u"] = inputs["L1_attn_wq_u"]
            im["l1a_wk_u"] = inputs["L1_attn_wk_u"]
            im["l1a_wv_u"] = inputs["L1_attn_wv_u"]
            im["l1a_wo"] = inputs["L1_attn_wo"]
            cosq, sinq, cosk, sink = _rope_consts()
            im["l1a_cos_q"] = np.ascontiguousarray(cosq[:, s * NT:(s + 1) * NT])
            im["l1a_sin_q"] = np.ascontiguousarray(sinq[:, s * NT:(s + 1) * NT])
            im["l1a_cos_k"] = cosk
            im["l1a_sin_k"] = sink
            im["ones_col_b"] = np.ones((128, 1), dtype=ml_dtypes.bfloat16)
            im["ones_row_f"] = np.ones((1, 128), dtype=np.float32)
        if "l1f" in blocks or "l1fgu" in blocks:
            im["l1f_wg"] = inputs["L1_ffn_w_gate"]
            im["l1f_wu"] = inputs["L1_ffn_w_up"]
            im["l1f_wd"] = inputs["L1_ffn_w_down"]
        in_maps.append(im)
    return in_maps


def run_blocks(blocks, inputs, x_full):
    ctx, nc = build_program(blocks)
    in_maps = make_in_maps(blocks, inputs, x_full)
    res = run_bass_kernel_spmd(nc, in_maps, core_ids=list(range(8)))
    out = np.zeros((B, T, D), dtype=np.float32)
    for r in range(8):
        b, s = r // 4, r % 4
        out[b, s * NT:(s + 1) * NT, :] = res.results[r]["x_out"]
    return out


def kernel(**inputs):
    x = np.asarray(inputs["L0_attn_hidden_states"])
    return run_blocks(["l0a", "l0f", "l1a", "l1f"], inputs, x)


# version 03 (final)
# speedup vs baseline: 1.0851x; 1.0851x over previous
import numpy as np
import ml_dtypes
import contextlib
import concourse.bass as bass
import concourse.mybir as mybir
from concourse.bass_utils import run_bass_kernel_spmd

F32 = mybir.dt.float32
BF16 = mybir.dt.bfloat16
I32 = mybir.dt.int32
U32 = mybir.dt.uint32
AF = mybir.ActivationFunctionType
ALU = mybir.AluOpType
AX = mybir.AxisListType

B, T, D = 2, 4096, 2048
NT = 1024
NTILE = 8
RG8 = [list(range(8))]
RG4 = [[0, 1, 2, 3], [4, 5, 6, 7]]
MLP1 = 8192
L1FGU_PARTS = 5
DEBUG_IB = False
L1F_EV_MODE = 'mult'


class Ctx:
    def __init__(self, nc, stack):
        self.nc = nc
        self.stack = stack
        self.sb = {}
        self.dram = {}
        self.semh = {}
        self.semv = {}
        self.semb = {}
        self.phys_tot = {}
        self.free_sems = []
        self.psT = [stack.enter_context(nc.psum_tensor(f"psT{i}", [128, 1024], F32))
                    for i in range(4)]
        self.bank = [self.psT[i // 2][:, (i % 2) * 512:(i % 2 + 1) * 512] for i in range(8)]

    def sem(self, name):
        if name not in self.semh:
            if self.free_sems:
                h = self.free_sems.pop()
            else:
                h = self.stack.enter_context(self.nc.semaphore(name))
                self.phys_tot[id(h)] = 0
            self.semh[name] = h
            self.semb[name] = self.phys_tot[id(h)]
            self.semv[name] = 0
        return self.semh[name]

    def inc(self, instr, name, n=1):
        h = self.sem(name)
        instr.then_inc(h, n)
        self.semv[name] += n
        self.phys_tot[id(h)] += n
        return self.semv[name]

    def val(self, name):
        self.sem(name)
        return self.semv[name]

    def wait(self, eng, name, v):
        eng.wait_ge(self.sem(name), self.semb[name] + v)

    def wait_all(self, eng, name):
        eng.wait_ge(self.sem(name), self.semb[name] + self.semv[name])

    def edge(self, instr, eng, name):
        v = self.inc(instr, name, 1)
        eng.wait_ge(self.semh[name], self.semb[name] + v)

    def retire(self, keep=()):
        for name in list(self.semh):
            if name in keep:
                continue
            self.free_sems.append(self.semh.pop(name))
            self.semv.pop(name)
            self.semb.pop(name)

    def sbuf(self, name, shape, dt=F32):
        if name not in self.sb:
            self.sb[name] = self.stack.enter_context(self.nc.sbuf_tensor(name, shape, dt))
        return self.sb[name]

    def dram_in(self, name, shape, dt=F32):
        if name not in self.dram:
            self.dram[name] = self.nc.dram_tensor(name, shape, dt, kind="ExternalInput")
        return self.dram[name]

    def dram_out(self, name, shape, dt=F32):
        if name not in self.dram:
            self.dram[name] = self.nc.dram_tensor(name, shape, dt, kind="ExternalOutput")
        return self.dram[name]

    def dram_tmp(self, name, shape, dt=F32):
        if name not in self.dram:
            self.dram[name] = self.nc.dram_tensor(name, shape, dt)
        return self.dram[name]



def emit_load_residual(ctx):
    nc = ctx.nc
    res = ctx.sbuf("res", [128, NTILE * D], F32)
    x_in = ctx.dram_in("x_shard", [NT, D], F32)
    with nc.Block() as block:
        @block.sync
        def _(sync):
            ctx.inc(sync.dma_start(out=res[:, :].rearrange("p (t d) -> p t d", t=NTILE), in_=x_in.ap().rearrange("(t p) d -> p t d", p=128)), "res_ld", 16)
            ctx.wait_all(sync, "res_ld")


def emit_store_residual(ctx):
    nc = ctx.nc
    res = ctx.sb["res"]
    x_out = ctx.dram_out("x_out", [NT, D], F32)
    with nc.Block() as block:
        @block.sync
        def _(sync):
            ctx.inc(sync.dma_start(out=x_out.ap().rearrange("(t p) d -> p t d", p=128), in_=res[:, :].rearrange("p (t d) -> p t d", t=NTILE)), "res_st", 16)
            ctx.wait_all(sync, "res_st")



def emit_layernorm(ctx, tag, xfm):
    nc = ctx.nc
    res = ctx.sb["res"]
    xb = (ctx.dram_out if DEBUG_IB else ctx.dram_tmp)(f"xln_b_{tag}", [NT, D], BF16)
    stat = ctx.sbuf("ln_stat", [128, 128], F32)
    sA, sV, sR, sB_, sO = (f"lnA{tag}", f"lnV{tag}", f"lnR{tag}", f"lnB{tag}", f"lnO{tag}")

    def C(t, j):
        return stat[:, t * 8 + j: t * 8 + j + 1]

    with nc.sbuf_tensor(f"xln_tm_{tag}", [128, NTILE * D], BF16, side="right") as xln_tm:
        with nc.Block() as block:
            @block.scalar
            def _(scalar):
                for t in range(NTILE):
                    rt = res[:, t * D:(t + 1) * D]
                    ot = xln_tm[:, t * D:(t + 1) * D]
                    scalar.activation(ot, rt, AF.Copy, accum_out=C(t, 0))
                    a = scalar.activation(ot, rt, AF.Square, accum_out=C(t, 1))
                    ctx.inc(a, sA, 1)

            @block.vector
            def _(vector):
                for t in range(NTILE):
                    ctx.wait(vector, sA, t + 1)
                    vector.tensor_scalar_mul(C(t, 2), C(t, 0), 1.0 / D)
                    ctx.edge(vector.tensor_scalar_mul(C(t, 3), C(t, 1), 1.0 / D), vector, "dveS")
                    ctx.edge(vector.scalar_tensor_tensor(C(t, 4), C(t, 2), -1.0, C(t, 2),
                                                         ALU.mult, ALU.mult), vector, "dveS")
                    ctx.edge(vector.tensor_tensor(C(t, 5), C(t, 3), C(t, 4), ALU.add), vector, "dveS")
                    a = vector.tensor_scalar_add(C(t, 5), C(t, 5), 1e-5)
                    ctx.inc(a, sV, 1)

            @block.scalar
            def _(scalar):
                for t in range(NTILE):
                    ctx.wait(scalar, sV, t + 1)
                    ctx.edge(scalar.activation(C(t, 6), C(t, 5), AF.Ln), scalar, "actS")
                    a = scalar.activation(C(t, 7), C(t, 6), AF.Exp, scale=-0.5)
                    ctx.inc(a, sR, 1)

            @block.vector
            def _(vector):
                for t in range(NTILE):
                    ctx.wait(vector, sR, t + 1)
                    a = vector.scalar_tensor_tensor(stat[:, 64 + t:64 + t + 1], C(t, 2), -1.0, C(t, 7),
                                                    ALU.mult, ALU.mult)
                    ctx.edge(a, vector, "dveS")
                    rt = res[:, t * D:(t + 1) * D]
                    a = vector.tensor_scalar(xln_tm[:, t * D:(t + 1) * D], rt,
                                             C(t, 7), stat[:, 64 + t:64 + t + 1], ALU.mult, ALU.add)
                    ctx.inc(a, sO, 1)

            @block.sync
            def _(sync):
                for t in range(NTILE):
                    ctx.wait(sync, sO, t + 1)
                    ctx.inc(sync.dma_start(out=xb.ap()[t * 128:(t + 1) * 128, :],
                                           in_=xln_tm[:, t * D:(t + 1) * D]), f"xbst{tag}", 16)
                ctx.wait_all(sync, f"xbst{tag}")
                for c in range(16):
                    ctx.inc(sync.dma_start_transpose(xfm[:, c * 1024:(c + 1) * 1024],
                                                     xb.ap()[:, c * 128:(c + 1) * 128]), f"xfmld{tag}", 16)
                ctx.wait_all(sync, f"xfmld{tag}")
    return xfm



def emit_l1_ffn_gateup(ctx, xfm):
    nc = ctx.nc
    res = ctx.sb["res"]
    bank = ctx.bank
    wg = ctx.dram_in("l1f_wg", [D, MLP1], F32)
    wu = ctx.dram_in("l1f_wu", [D, MLP1], F32)
    wd = ctx.dram_in("l1f_wd", [MLP1, D], F32)
    ib = (ctx.dram_out if DEBUG_IB else ctx.dram_tmp)("l1f_inter_b", [NT, MLP1], BF16)
    NSL = 16

    with nc.sbuf_tensor("l1f_pang", [128, 2 * 8192], BF16, side="right") as pan_g, \
         nc.sbuf_tensor("l1f_panu", [128, 2 * 8192], BF16, side="right") as pan_u, \
         nc.sbuf_tensor("l1f_it", [128, 8 * 512], BF16, side="right") as it_sl, \
         nc.sbuf_tensor("l1f_habs", [128, 8 * 512], BF16, side="right") as habs, \
         nc.sbuf_tensor("l1f_spt", [128, 3 * 1024], BF16, side="right") as spt, \
         nc.sbuf_tensor("l1f_gs", [128, 2 * 512], BF16, side="right") as g_s:
        with nc.Block() as block:
            @block.gpsimd
            def _(g):
                for isl in range(NSL):
                    b = isl % 2
                    if isl >= 2 and L1FGU_PARTS >= 2:
                        ctx.wait(g, "l1f_pe", (isl - 1) * 16)
                    ctx.inc(g.dma_start(out=pan_g[:, b * 8192:(b + 1) * 8192].rearrange("p (c i) -> p c i", c=16),
                                        in_=wg.ap()[:, isl * 512:(isl + 1) * 512].rearrange("(c p) i -> p c i", p=128)),
                            "l1f_w", 16)
                    ctx.inc(g.dma_start(out=pan_u[:, b * 8192:(b + 1) * 8192].rearrange("p (c i) -> p c i", c=16),
                                        in_=wu.ap()[:, isl * 512:(isl + 1) * 512].rearrange("(c p) i -> p c i", p=128)),
                            "l1f_w", 16)

            @block.tensor
            def _(tensor):
                if L1FGU_PARTS < 2:
                    return
                for isl in range(NSL):
                    b = isl % 2
                    ctx.wait(tensor, "l1f_w", 32 * (isl + 1))
                    for ts_ in range(8):
                        k = isl * 8 + ts_
                        pb = k % 2
                        if k >= 2 and L1FGU_PARTS >= 3:
                            ctx.wait(tensor, "l1f_silu", k - 1)
                            ctx.wait(tensor, "l1f_ev", k - 1)
                        for c in range(16):
                            lhsT = xfm[:, c * 1024 + ts_ * 128: c * 1024 + ts_ * 128 + 128]
                            mm = tensor.matmul(bank[pb], lhsT,
                                               pan_g[:, b * 8192 + c * 512: b * 8192 + (c + 1) * 512],
                                               start=(c == 0), stop=(c == 15))
                        ctx.inc(mm, "l1f_pe", 1)
                        for c in range(16):
                            lhsT = xfm[:, c * 1024 + ts_ * 128: c * 1024 + ts_ * 128 + 128]
                            mm = tensor.matmul(bank[2 + pb], lhsT,
                                               pan_u[:, b * 8192 + c * 512: b * 8192 + (c + 1) * 512],
                                               start=(c == 0), stop=(c == 15))
                        ctx.inc(mm, "l1f_pe", 1)

            @block.scalar
            def _(scalar):
                if L1FGU_PARTS < 3:
                    return
                for isl in range(NSL):
                    for ts_ in range(8):
                        k = isl * 8 + ts_
                        pb = k % 2
                        ctx.wait(scalar, "l1f_pe", 2 * k + 1)
                        if k >= 2:
                            ctx.wait(scalar, "l1f_ev", k - 1)
                        a = scalar.activation(g_s[:, pb * 512:(pb + 1) * 512], bank[pb], AF.Silu)
                        ctx.inc(a, "l1f_silu", 1)
                    if L1FGU_PARTS >= 4:
                        ctx.wait(scalar, "l1f_ev", 8 * (isl + 1))
                        if isl > 0:
                            ctx.wait(scalar, "l1f_sp", isl)
                        a = scalar.activation(habs[:, :], it_sl[:, :], AF.Abs)
                        ctx.inc(a, "l1f_abs", 1)

            @block.vector
            def _(vector):
                if L1FGU_PARTS < 3:
                    return
                for isl in range(NSL):
                    for ts_ in range(8):
                        k = isl * 8 + ts_
                        pb = k % 2
                        ctx.wait(vector, "l1f_pe", 2 * k + 2)
                        ctx.wait(vector, "l1f_silu", k + 1)
                        if L1FGU_PARTS >= 5 and isl > 0 and ts_ == 0:
                            ctx.wait(vector, "l1f_ibst", 16 * isl)
                        if L1F_EV_MODE == 'copy':
                            a = vector.tensor_copy(it_sl[:, ts_ * 512:(ts_ + 1) * 512],
                                                   g_s[:, pb * 512:(pb + 1) * 512])
                        else:
                            a = vector.tensor_tensor(it_sl[:, ts_ * 512:(ts_ + 1) * 512],
                                                     g_s[:, pb * 512:(pb + 1) * 512], bank[2 + pb], ALU.mult)
                        ctx.inc(a, "l1f_ev", 1)
                    if L1FGU_PARTS < 4:
                        continue
                    ctx.wait(vector, "l1f_abs", isl + 1)
                    A = habs[:, 0:4096:4]; Bv = habs[:, 1:4096:4]
                    Cv = habs[:, 2:4096:4]; Dv = habs[:, 3:4096:4]
                    hi1 = spt[:, 0:1024]; hi2 = spt[:, 1024:2048]; t2 = spt[:, 2048:3072]
                    vector.tensor_tensor(hi1, A, Bv, ALU.max)
                    vector.tensor_tensor(hi2, Cv, Dv, ALU.max)
                    vector.tensor_tensor(t2, hi1, hi2, ALU.min)
                    vector.tensor_tensor(hi1, A, Bv, ALU.min)
                    vector.tensor_tensor(hi2, Cv, Dv, ALU.min)
                    vector.tensor_tensor(hi1, hi1, hi2, ALU.max)
                    vector.tensor_tensor(t2, t2, hi1, ALU.max)
                    for r in range(4):
                        Xr = it_sl[:, r:4096:4]
                        Hr = habs[:, r:4096:4]
                        vector.tensor_tensor(Hr, Hr, t2, ALU.is_ge)
                        a = vector.tensor_tensor(Xr, Xr, Hr, ALU.mult)
                    ctx.inc(a, "l1f_sp", 1)

            @block.sync
            def _(sync):
                if L1FGU_PARTS < 5:
                    return
                for isl in range(NSL):
                    ctx.wait(sync, "l1f_sp", isl + 1)
                    src = it_sl[:, :].rearrange("p (s i) -> p s i", s=8)
                    dst = ib.ap()[:, isl * 512:(isl + 1) * 512].rearrange("(s p) i -> p s i", p=128)
                    ctx.inc(sync.dma_start(out=dst, in_=src), "l1f_ibst", 16)
                ctx.wait_all(sync, "l1f_ibst")


def emit_l1_ffn_down(ctx):
    nc = ctx.nc
    res = ctx.sb["res"]
    bank = ctx.bank
    wd = ctx.dram["l1f_wd"]
    ib = ctx.dram["l1f_inter_b"]
    NG = 8
    with nc.sbuf_tensor("l1f_ifm", [128, 8 * 1024], BF16, side="right") as ifm, \
         nc.sbuf_tensor("l1f_wdp", [128, 8 * 2048], BF16, side="right") as wd_p, \
         nc.sbuf_tensor("l1f_acc", [128, 16384], F32, side="right") as acc:
        with nc.Block() as block:
            @block.gpsimd
            def _(g):
                for ic in range(64):
                    bw = ic % 8
                    if ic >= 8:
                        ctx.wait(g, "l1f_pe2", ((ic - 8) // NG + 1) * 64)
                    ctx.inc(g.dma_start(out=wd_p[:, bw * 2048:(bw + 1) * 2048],
                                        in_=wd.ap()[ic * 128:(ic + 1) * 128, :]), "l1f_w2", 16)

            @block.sync
            def _(sync):
                for ic in range(64):
                    bi = ic % 8
                    if ic >= 8:
                        ctx.wait(sync, "l1f_pe2", ((ic - 8) // NG + 1) * 64)
                    ctx.inc(sync.dma_start_transpose(ifm[:, bi * 1024:(bi + 1) * 1024],
                                                     ib.ap()[:, ic * 128:(ic + 1) * 128]), "l1f_ix", 16)

            @block.tensor
            def _(tensor):
                for icg in range(64 // NG):
                    ctx.wait(tensor, "l1f_w2", 16 * (icg + 1) * NG)
                    ctx.wait(tensor, "l1f_ix", 16 * (icg + 1) * NG)
                    for ts_ in range(8):
                        q = icg * 8 + ts_
                        qp = q % 2
                        if q >= 2:
                            ctx.wait(tensor, "l1f_dacc", (q - 1) * 4)
                        for j, ic in enumerate(range(icg * NG, (icg + 1) * NG)):
                            bi = ic % 8
                            lhsT = ifm[:, bi * 1024 + ts_ * 128: bi * 1024 + ts_ * 128 + 128]
                            for hs in range(4):
                                mm = tensor.matmul(bank[qp * 4 + hs], lhsT,
                                                   wd_p[:, bi * 2048 + hs * 512: bi * 2048 + (hs + 1) * 512],
                                                   start=(j == 0), stop=(j == NG - 1))
                        ctx.inc(mm, "l1f_pe2", 8)

            @block.vector
            def _(vector):
                vector.memset(acc[:, :], 0.0)
                for icg in range(64 // NG):
                    for ts_ in range(8):
                        q = icg * 8 + ts_
                        qp = q % 2
                        ctx.wait(vector, "l1f_pe2", (q + 1) * 8)
                        for hs in range(4):
                            dstv = acc[:, ts_ * 2048 + hs * 512: ts_ * 2048 + (hs + 1) * 512]
                            a = vector.tensor_tensor(dstv, dstv, bank[qp * 4 + hs], ALU.add)
                            ctx.inc(a, "l1f_dacc", 1)
                for t in range(NTILE):
                    vector.tensor_tensor(res[:, t * D:(t + 1) * D], res[:, t * D:(t + 1) * D],
                                         acc[:, t * 2048:(t + 1) * 2048], ALU.add)





NSL0 = 32
MLP0 = 16384
DEQ_BIAS = 126 << 23


def _emit_dequant_tile(ctx, eng, use_select, P, scr, SC, O):
    nc = ctx.nc
    sname = "deqV" if use_select else "deqP"

    def E(instr):
        ctx.edge(instr, eng, sname)

    for half, (mM, mS, mH, shM, shS, Kh) in enumerate([
            (7, 8, 1, 22, 28, 0x3F000000),
            (0x70, 0x80, 0x10, 18, 24, 0x03F00000),
    ]):
        t1 = scr[:, 0:1024]; t2 = scr[:, 1024:2048]; fb = scr[:, 2048:3072]
        msk = scr[:, 3072:4096]; hv = scr[:, 4096:5120]; hv2 = scr[:, 5120:6144]
        eng.tensor_scalar(t1, P, mM, shM, ALU.bitwise_and, ALU.logical_shift_left)
        eng.tensor_scalar(t2, P, mS, shS, ALU.bitwise_and, ALU.logical_shift_left)
        ctx.edge(eng.tensor_scalar(msk, P, mM, None, ALU.bitwise_and), eng, sname)
        E(eng.tensor_scalar(msk, msk, 2 if half == 0 else 0x20, None, ALU.is_lt))
        ctx.edge(eng.tensor_scalar(hv, P, mH, None, ALU.bitwise_and), eng, sname)
        E(eng.tensor_scalar(hv, hv, Kh, None, ALU.mult))
        E(eng.scalar_tensor_tensor(fb, t1, DEQ_BIAS, t2, ALU.add, ALU.add))
        E(eng.tensor_tensor(hv2, hv, t2, ALU.bitwise_or))
        if use_select:
            E(eng.select(t1, msk, hv2, fb))
            hv2 = t1
        else:
            E(eng.tensor_tensor(hv2, hv2, fb, ALU.subtract))
            E(eng.tensor_tensor(hv2, msk, hv2, ALU.mult))
            E(eng.tensor_tensor(hv2, hv2, fb, ALU.add))
        E(eng.tensor_tensor(O[:, half::2], hv2.bitcast(F32), SC[:, half::2], ALU.mult))


def emit_l0_ffn_dequant(ctx):
    nc = ctx.nc
    gWp = ctx.dram_in("l0f_gWp", [2048, 1024], I32)
    gSc = ctx.dram_in("l0f_gSc", [2048, 2048], F32)
    uWp = ctx.dram_in("l0f_uWp", [2048, 1024], I32)
    uSc = ctx.dram_in("l0f_uSc", [2048, 2048], F32)
    dWp = ctx.dram_in("l0f_dWp", [256, 8192], I32)
    dSc = ctx.dram_in("l0f_dSc", [256, 16384], F32)
    g_nat = (ctx.dram_out if DEBUG_IB else ctx.dram_tmp)("l0f_gnat", [2048, 2048], BF16)
    u_nat = ctx.dram_tmp("l0f_unat", [2048, 2048], BF16)
    d_nat = ctx.dram_tmp("l0f_dnat", [256, 16384], BF16)
    g_tin = ctx.dram_tmp("l0f_gtin", [2048, 2048], BF16)
    u_tin = ctx.dram_tmp("l0f_utin", [2048, 2048], BF16)
    ag_g = ctx.dram_tmp("l0f_ag_g", [8 * 2048, 2048], BF16)
    ag_u = ctx.dram_tmp("l0f_ag_u", [8 * 2048, 2048], BF16)
    ag_d = ctx.dram_tmp("l0f_ag_d", [8 * 256, 16384], BF16)

    tiles = []
    for (Wp, Sc, Onat) in [(gWp, gSc, g_nat), (uWp, uSc, u_nat)]:
        for rr in range(16):
            tiles.append((Wp.ap()[rr * 128:(rr + 1) * 128, :],
                          Sc.ap()[rr * 128:(rr + 1) * 128, :],
                          Onat.ap()[rr * 128:(rr + 1) * 128, :]))
    for rr in range(2):
        for cc in range(8):
            tiles.append((dWp.ap()[rr * 128:(rr + 1) * 128, cc * 1024:(cc + 1) * 1024],
                          dSc.ap()[rr * 128:(rr + 1) * 128, cc * 2048:(cc + 1) * 2048],
                          d_nat.ap()[rr * 128:(rr + 1) * 128, cc * 2048:(cc + 1) * 2048]))

    with nc.sbuf_tensor("dq_P", [128, 2 * 1024], I32, side="right") as Pt, \
         nc.sbuf_tensor("dq_SC", [128, 2 * 2048], F32, side="right") as SCt, \
         nc.sbuf_tensor("dq_O", [128, 2 * 2048], BF16, side="right") as Ot, \
         nc.sbuf_tensor("dq_scrV", [128, 6 * 1024], I32, side="right") as scrV, \
         nc.sbuf_tensor("dq_scrP", [128, 6 * 1024], I32, side="right") as scrP:
        with nc.Block() as block:
            @block.sync
            def _(sync):
                for i, (p_ap, s_ap, o_ap) in enumerate(tiles):
                    bf = i % 2
                    if i >= 2:
                        ctx.wait(sync, "dq_done", i - 1)
                    ctx.inc(sync.dma_start(out=Pt[:, bf * 1024:(bf + 1) * 1024], in_=p_ap), "dq_ld", 16)
                    ctx.inc(sync.dma_start(out=SCt[:, bf * 2048:(bf + 1) * 2048], in_=s_ap), "dq_ld", 16)

            @block.vector
            def _(vector):
                for i in range(len(tiles)):
                    bf = i % 2
                    ctx.wait(vector, "dq_ld", 32 * (i + 1))
                    if i >= 2:
                        ctx.wait(vector, "dq_st", 16 * (i - 1))
                    _emit_dequant_tile(ctx, vector, True, Pt[:, bf * 1024:(bf + 1) * 1024], scrV,
                                       SCt[:, bf * 2048:(bf + 1) * 2048], Ot[:, bf * 2048:(bf + 1) * 2048])
                    ctx.inc(vector.tensor_copy(Ot[:, bf * 2048:bf * 2048 + 8], Ot[:, bf * 2048:bf * 2048 + 8]), "dq_doneV", 1)

            @block.scalar
            def _(scalar):
                for i, (p_ap, s_ap, o_ap) in enumerate(tiles):
                    bf = i % 2
                    ctx.wait(scalar, "dq_doneV", i + 1)
                    a = scalar.dma_start(out=o_ap, in_=Ot[:, bf * 2048:(bf + 1) * 2048])
                    ctx.inc(a, "dq_st", 16)
                    ctx.inc(scalar.nop(), "dq_done", 1)
                ctx.wait_all(scalar, "dq_st")

        with nc.sbuf_tensor("dq_tt", [128, 2 * 2048], BF16, side="right") as tt_:
            with nc.Block() as block:
                @block.sync
                def _(sync):
                    for m, (nat, tin) in enumerate([(g_nat, g_tin), (u_nat, u_tin)]):
                        for hc in range(16):
                            j = m * 16 + hc
                            bf = j % 2
                            if j >= 2:
                                ctx.wait(sync, "dq_tw", 16 * (j - 1))
                            ctx.inc(sync.dma_start_transpose(tt_[:, bf * 2048:(bf + 1) * 2048],
                                                             nat.ap()[:, hc * 128:(hc + 1) * 128]), "dq_tl", 16)
                            ctx.wait(sync, "dq_tl", 16 * (j + 1))
                            ctx.inc(sync.dma_start(out=tin.ap()[hc * 128:(hc + 1) * 128, :],
                                                   in_=tt_[:, bf * 2048:(bf + 1) * 2048]), "dq_tw", 16)
                    ctx.wait_all(sync, "dq_tw")

        with nc.Block() as block:
            @block.gpsimd
            def _(g):
                for name, tin, agout in [("g", g_tin, ag_g), ("u", u_tin, ag_u), ("d", d_nat, ag_d)]:
                    ctx.inc(g.collective_compute(
                        "AllGather", ALU.bypass, replica_groups=RG8,
                        ins=[tin.ap()], outs=[agout.ap()]), "l0f_cc", 1)
    return ag_g, ag_u, ag_d


def emit_l0_ffn_main(ctx, xfm):
    nc = ctx.nc
    res = ctx.sb["res"]
    bank = ctx.bank
    ag_g = ctx.dram["l0f_ag_g"]
    ag_u = ctx.dram["l0f_ag_u"]
    ag_d = ctx.dram["l0f_ag_d"]
    ib = (ctx.dram_out if DEBUG_IB else ctx.dram_tmp)("l0f_inter_b", [NT, MLP0], BF16)

    with nc.sbuf_tensor("l0f_pang", [128, 2 * 8192], BF16, side="right") as pan_g, \
         nc.sbuf_tensor("l0f_panu", [128, 2 * 8192], BF16, side="right") as pan_u, \
         nc.sbuf_tensor("l0f_it", [128, 8 * 512], BF16, side="right") as it_sl, \
         nc.sbuf_tensor("l0f_gs", [128, 2 * 512], BF16, side="right") as g_s:
        with nc.Block() as block:
            @block.sync
            def _(sync):
                ctx.wait(sync, "l0f_cc", 3)
                for isl in range(NSL0):
                    b = isl % 2
                    rblk, iloc = isl // 4, (isl % 4) * 512
                    if isl >= 2:
                        ctx.wait(sync, "l0f_pe", (isl - 1) * 16)
                    ctx.inc(sync.dma_start(out=pan_g[:, b * 8192:(b + 1) * 8192].rearrange("p (c i) -> p c i", c=16),
                                           in_=ag_g.ap()[rblk * 2048:(rblk + 1) * 2048, iloc:iloc + 512]
                                           .rearrange("(c p) i -> p c i", p=128)), "l0f_w", 16)
                    ctx.inc(sync.dma_start(out=pan_u[:, b * 8192:(b + 1) * 8192].rearrange("p (c i) -> p c i", c=16),
                                           in_=ag_u.ap()[rblk * 2048:(rblk + 1) * 2048, iloc:iloc + 512]
                                           .rearrange("(c p) i -> p c i", p=128)), "l0f_w", 16)

            @block.tensor
            def _(tensor):
                for isl in range(NSL0):
                    b = isl % 2
                    ctx.wait(tensor, "l0f_w", 32 * (isl + 1))
                    for ts_ in range(8):
                        k = isl * 8 + ts_
                        pb = k % 2
                        if k >= 2:
                            ctx.wait(tensor, "l0f_gel", k - 1)
                            ctx.wait(tensor, "l0f_ev", k - 1)
                        for c in range(16):
                            lhsT = xfm[:, c * 1024 + ts_ * 128: c * 1024 + ts_ * 128 + 128]
                            mm = tensor.matmul(bank[pb], lhsT,
                                               pan_g[:, b * 8192 + c * 512: b * 8192 + (c + 1) * 512],
                                               start=(c == 0), stop=(c == 15))
                        ctx.inc(mm, "l0f_pe", 1)
                        for c in range(16):
                            lhsT = xfm[:, c * 1024 + ts_ * 128: c * 1024 + ts_ * 128 + 128]
                            mm = tensor.matmul(bank[2 + pb], lhsT,
                                               pan_u[:, b * 8192 + c * 512: b * 8192 + (c + 1) * 512],
                                               start=(c == 0), stop=(c == 15))
                        ctx.inc(mm, "l0f_pe", 1)

            @block.scalar
            def _(scalar):
                for isl in range(NSL0):
                    for ts_ in range(8):
                        k = isl * 8 + ts_
                        pb = k % 2
                        ctx.wait(scalar, "l0f_pe", 2 * k + 1)
                        if k >= 2:
                            ctx.wait(scalar, "l0f_ev", k - 1)
                        a = scalar.activation(g_s[:, pb * 512:(pb + 1) * 512], bank[pb], AF.Gelu_apprx_tanh)
                        ctx.inc(a, "l0f_gel", 1)

            @block.vector
            def _(vector):
                for isl in range(NSL0):
                    for ts_ in range(8):
                        k = isl * 8 + ts_
                        pb = k % 2
                        ctx.wait(vector, "l0f_pe", 2 * k + 2)
                        ctx.wait(vector, "l0f_gel", k + 1)
                        if isl > 0 and ts_ == 0:
                            ctx.wait(vector, "l0f_ibst", 16 * isl)
                        a = vector.tensor_tensor(it_sl[:, ts_ * 512:(ts_ + 1) * 512],
                                                 g_s[:, pb * 512:(pb + 1) * 512], bank[2 + pb], ALU.mult)
                        ctx.inc(a, "l0f_ev", 1)

            @block.gpsimd
            def _(g):
                for isl in range(NSL0):
                    ctx.wait(g, "l0f_ev", 8 * (isl + 1))
                    src = it_sl[:, :].rearrange("p (s i) -> p s i", s=8)
                    dst = ib.ap()[:, isl * 512:(isl + 1) * 512].rearrange("(s p) i -> p s i", p=128)
                    ctx.inc(g.dma_start(out=dst, in_=src), "l0f_ibst", 16)
                ctx.wait_all(g, "l0f_ibst")



def emit_l0_ffn_down(ctx):
    nc = ctx.nc
    res = ctx.sb["res"]
    bank = ctx.bank
    ag_d = ctx.dram["l0f_ag_d"]
    ib = ctx.dram["l0f_inter_b"]
    NG = 8
    with nc.sbuf_tensor("l0f_ifm", [128, 8 * 1024], BF16, side="right") as ifm, \
         nc.sbuf_tensor("l0f_wdp", [128, 8 * 2048], BF16, side="right") as wd_p, \
         nc.sbuf_tensor("l0f_acc", [128, 16384], F32, side="right") as acc:
        with nc.Block() as block:
            @block.sync
            def _(sync):
                ctx.wait(sync, "l0f_cc", 3)
                for ic in range(128):
                    bi = ic % 8
                    if ic >= 8:
                        ctx.wait(sync, "l0f_pe2", ((ic - 8) // NG + 1) * 64)
                    ctx.inc(sync.dma_start_transpose(wd_p[:, bi * 2048:(bi + 1) * 2048],
                                                     ag_d.ap()[:, ic * 128:(ic + 1) * 128]), "l0f_w2", 16)
                    ctx.inc(sync.dma_start_transpose(ifm[:, bi * 1024:(bi + 1) * 1024],
                                                     ib.ap()[:, ic * 128:(ic + 1) * 128]), "l0f_ix", 16)

            @block.tensor
            def _(tensor):
                for icg in range(128 // NG):
                    ctx.wait(tensor, "l0f_w2", 16 * (icg + 1) * NG)
                    ctx.wait(tensor, "l0f_ix", 16 * (icg + 1) * NG)
                    for ts_ in range(8):
                        q = icg * 8 + ts_
                        qp = q % 2
                        if q >= 2:
                            ctx.wait(tensor, "l0f_dacc", (q - 1) * 4)
                        for j, ic in enumerate(range(icg * NG, (icg + 1) * NG)):
                            bi = ic % 8
                            lhsT = ifm[:, bi * 1024 + ts_ * 128: bi * 1024 + ts_ * 128 + 128]
                            for hs in range(4):
                                mm = tensor.matmul(bank[qp * 4 + hs], lhsT,
                                                   wd_p[:, bi * 2048 + hs * 512: bi * 2048 + (hs + 1) * 512],
                                                   start=(j == 0), stop=(j == NG - 1))
                        ctx.inc(mm, "l0f_pe2", 8)

            @block.vector
            def _(vector):
                vector.memset(acc[:, :], 0.0)
                for icg in range(128 // NG):
                    for ts_ in range(8):
                        q = icg * 8 + ts_
                        qp = q % 2
                        ctx.wait(vector, "l0f_pe2", (q + 1) * 8)
                        for hs in range(4):
                            dstv = acc[:, ts_ * 2048 + hs * 512: ts_ * 2048 + (hs + 1) * 512]
                            a = vector.tensor_tensor(dstv, dstv, bank[qp * 4 + hs], ALU.add)
                            ctx.inc(a, "l0f_dacc", 1)
                for t in range(NTILE):
                    vector.tensor_tensor(res[:, t * D:(t + 1) * D], res[:, t * D:(t + 1) * D],
                                         acc[:, t * 2048:(t + 1) * 2048], ALU.add)





def emit_l1_attn_latents(ctx, xfm, qlat):
    nc = ctx.nc
    bank = ctx.bank
    wq_d = ctx.dram_in("l1a_wq_d", [D, 256], F32)
    wkv_d = ctx.dram_in("l1a_wkv_d", [D, 256], F32)
    kv_tm_loc = ctx.dram_tmp("l1a_kv_loc", [NT, 256], BF16)
    kv_tm_ag = ctx.dram_tmp("l1a_kv_ag", [4 * NT, 256], BF16)
    kv_fm_b = ctx.dram_tmp("l1a_kvfm_b", [256, NT], BF16)

    with nc.sbuf_tensor("l1a_wdq", [128, 16 * 256], BF16, side="right") as wdq, \
         nc.sbuf_tensor("l1a_wdkv", [128, 16 * 256], BF16, side="right") as wdkv, \
         nc.sbuf_tensor("l1a_latk", [128, 2 * 1024], BF16, side="right") as kvlat, \
         nc.sbuf_tensor("l1a_kvtm", [128, 8 * 256], BF16, side="right") as kvtm:
        with nc.Block() as block:
            @block.gpsimd
            def _(g):
                ctx.inc(g.dma_start(out=wdq[:, :].rearrange("p (c l) -> p c l", c=16),
                                    in_=wq_d.ap().rearrange("(c p) l -> p c l", p=128)), "l1a_wl", 16)
                ctx.inc(g.dma_start(out=wdkv[:, :].rearrange("p (c l) -> p c l", c=16),
                                    in_=wkv_d.ap().rearrange("(c p) l -> p c l", p=128)), "l1a_wl", 16)

            @block.tensor
            def _(tensor):
                ctx.wait_all(tensor, "l1a_wl")
                for which, wsrc in [(0, wdq), (1, wdkv)]:
                    for lc in range(2):
                        for th in range(2):
                            if lc == 1:
                                ctx.wait(tensor, "l1a_lat_ev", which * 4 + th + 1)
                            for c in range(16):
                                mm = tensor.matmul(bank[2 * which + th],
                                                   wsrc[:, c * 256 + lc * 128: c * 256 + (lc + 1) * 128],
                                                   xfm[:, c * 1024 + th * 512: c * 1024 + (th + 1) * 512],
                                                   start=(c == 0), stop=(c == 15))
                            ctx.inc(mm, "l1a_lat_pe", 1)

            @block.scalar
            def _(scalar):
                n = 0
                for which, dst in [(0, qlat), (1, kvlat)]:
                    for lc in range(2):
                        for th in range(2):
                            n += 1
                            ctx.wait(scalar, "l1a_lat_pe", n)
                            a = scalar.activation(dst[:, lc * 1024 + th * 512: lc * 1024 + (th + 1) * 512],
                                                  bank[2 * which + th], AF.Copy)
                            ctx.inc(a, "l1a_lat_ev", 1)

        with nc.Block() as block:
            @block.sync
            def _(sync):
                ctx.wait_all(sync, "l1a_lat_ev")
                ctx.inc(sync.dma_start(out=kv_fm_b.ap().rearrange("(c p) t -> p c t", p=128),
                                       in_=kvlat[:, :].rearrange("p (c t) -> p c t", c=2)), "l1a_kvst", 16)
                ctx.wait_all(sync, "l1a_kvst")
                for tt in range(8):
                    ctx.inc(sync.dma_start_transpose(kvtm[:, tt * 256:(tt + 1) * 256],
                                                     kv_fm_b.ap()[:, tt * 128:(tt + 1) * 128]), "l1a_kvtl", 16)
                ctx.wait_all(sync, "l1a_kvtl")
                ctx.inc(sync.dma_start(out=kv_tm_loc.ap().rearrange("(tt p) l -> p tt l", p=128),
                                       in_=kvtm[:, :].rearrange("p (tt l) -> p tt l", tt=8)), "l1a_kvst2", 16)
                ctx.wait_all(sync, "l1a_kvst2")

            @block.gpsimd
            def _(g):
                ctx.wait_all(g, "l1a_kvst2")
                ctx.inc(g.collective_compute(
                    "AllGather", ALU.bypass, replica_groups=RG4,
                    ins=[kv_tm_loc.ap()], outs=[kv_tm_ag.ap()]), "l1a_cc", 1)
                ctx.wait_all(g, "l1a_cc")


def emit_l1_attn_main(ctx, qlat, O):
    nc = ctx.nc
    res = ctx.sb["res"]
    bank = ctx.bank
    psT = ctx.psT
    wq_u = ctx.dram_in("l1a_wq_u", [256, 1024], F32)
    wk_u = ctx.dram_in("l1a_wk_u", [256, 1024], F32)
    wv_u = ctx.dram_in("l1a_wv_u", [256, D], F32)
    wo = ctx.dram_in("l1a_wo", [D, D], F32)
    cos_q = ctx.dram_in("l1a_cos_q", [64, 1024], BF16)
    sin_q = ctx.dram_in("l1a_sin_q", [64, 1024], BF16)
    cos_k = ctx.dram_in("l1a_cos_k", [64, 4096], BF16)
    sin_k = ctx.dram_in("l1a_sin_k", [64, 4096], BF16)
    ones_cb = ctx.dram_in("ones_col_b", [128, 1], BF16)
    ones_rf = ctx.dram_in("ones_row_f", [1, 128], F32)
    kv_tm_ag = ctx.dram["l1a_kv_ag"]

    H = 16
    NKC = 32

    with nc.sbuf_tensor("l1a_qall", [128, 16 * 1024], BF16, side="right") as qall, \
         nc.sbuf_tensor("l1a_cs", [128, 4 * 1024], BF16, side="right") as cs, \
         nc.sbuf_tensor("l1a_scr", [128, 2 * 1024], BF16, side="right") as scr, \
         nc.sbuf_tensor("l1a_kvfm", [128, 2 * 4096], BF16, side="right") as kvfm, \
         nc.sbuf_tensor("l1a_K", [128, 4096], BF16, side="right") as Kt, \
         nc.sbuf_tensor("l1a_V", [128, 4096], BF16, side="right") as Vt, \
         nc.sbuf_tensor("l1a_ex", [128, 2 * 1024], BF16, side="right") as exT, \
         nc.sbuf_tensor("l1a_wub", [128, 6144], BF16, side="right") as wub, \
         nc.sbuf_tensor("l1a_rb", [128, 1024], F32, side="right") as rb, \
         nc.sbuf_tensor("l1a_sm", [128, 32], F32, side="right") as sm, \
         nc.sbuf_tensor("l1a_onc", [128, 1], BF16, side="right") as onesc, \
         nc.sbuf_tensor("l1a_onr", [1, 128], F32, side="right") as onesr:
        COSK = cs[0:64, 0:4096]; SINK = cs[64:128, 0:4096]
        WK = wub[:, 0:2048]; WV = wub[:, 2048:6144]
        rc = rb

        wq_stack = contextlib.ExitStack()
        WQt = wq_stack.enter_context(nc.sbuf_tensor("l1a_wq_t", [128, 4096], BF16, side="right"))
        WQ = WQt[:, 0:2048]
        COSQ = WQt[0:64, 2048:3072]; SINQ = WQt[0:64, 3072:4096]
        with nc.Block() as block:
            @block.sync
            def _(sync):
                for lc in range(2):
                    ctx.inc(sync.dma_start_transpose(kvfm[:, lc * 4096:(lc + 1) * 4096],
                                                     kv_tm_ag.ap()[:, lc * 128:(lc + 1) * 128]), "l1a_kvfl", 16)

            @block.gpsimd
            def _(g):
                ctx.inc(g.dma_start(out=WQ.rearrange("p (c s) -> p c s", c=2),
                                    in_=wq_u.ap().rearrange("(c p) s -> p c s", p=128)), "l1a_wu2", 16)
                ctx.inc(g.dma_start(out=WK.rearrange("p (c s) -> p c s", c=2),
                                    in_=wk_u.ap().rearrange("(c p) s -> p c s", p=128)), "l1a_wu2", 16)
                ctx.inc(g.dma_start(out=WV.rearrange("p (c s) -> p c s", c=2),
                                    in_=wv_u.ap().rearrange("(c p) s -> p c s", p=128)), "l1a_wu2", 16)
                ctx.inc(g.dma_start(out=COSQ, in_=cos_q.ap()), "l1a_wu2", 16)
                ctx.inc(g.dma_start(out=SINQ, in_=sin_q.ap()), "l1a_wu2", 16)
                ctx.inc(g.dma_start(out=COSK, in_=cos_k.ap()), "l1a_wu2", 16)
                ctx.inc(g.dma_start(out=SINK, in_=sin_k.ap()), "l1a_wu2", 16)
                ctx.inc(g.dma_start(out=onesc[:, :], in_=ones_cb.ap()), "l1a_wu2", 16)
                ctx.inc(g.dma_start(out=onesr[:, :], in_=ones_rf.ap()), "l1a_wu2", 16)

            @block.tensor
            def _(tensor):
                ctx.wait_all(tensor, "l1a_wu2")
                for h in range(H):
                    pb = h % 2
                    if h >= 2:
                        ctx.wait(tensor, "l1a_qev", 2 * (h - 1))
                    for th in range(2):
                        for lc in range(2):
                            mm = tensor.matmul(bank[2 * pb + th][0:64, :],
                                               WQ[:, lc * 1024 + h * 64: lc * 1024 + (h + 1) * 64],
                                               qlat[:, lc * 1024 + th * 512: lc * 1024 + (th + 1) * 512],
                                               start=(lc == 0), stop=(lc == 1))
                        ctx.inc(mm, "l1a_qpe", 1)

            @block.scalar
            def _(scalar):
                for h in range(H):
                    pb = h % 2
                    for th in range(2):
                        ctx.wait(scalar, "l1a_qpe", 2 * h + th + 1)
                        if h >= 2:
                            ctx.wait(scalar, "l1a_qrope", h - 1)
                        a = scalar.activation(scr[0:64, pb * 1024 + th * 512: pb * 1024 + (th + 1) * 512],
                                              bank[2 * pb + th][0:64, :], AF.Copy)
                        ctx.inc(a, "l1a_qev", 1)

            @block.vector
            def _(vector):
                for h in range(H):
                    pb = h % 2
                    Q = scr[:, pb * 1024:(pb + 1) * 1024]
                    DQ = qall[:, h * 1024:(h + 1) * 1024]
                    ctx.wait(vector, "l1a_qev", 2 * h + 2)
                    ctx.edge(vector.tensor_copy(DQ[0:32, :], Q[32:64, :]), vector, "dveS")
                    ctx.edge(vector.tensor_copy(DQ[32:64, :], Q[0:32, :]), vector, "dveS")
                    ctx.edge(vector.tensor_tensor(DQ[0:64, :], DQ[0:64, :], SINQ, ALU.mult), vector, "dveS")
                    ctx.edge(vector.tensor_tensor(Q[0:64, :], Q[0:64, :], COSQ, ALU.mult), vector, "dveS")
                    a = vector.tensor_tensor(DQ[0:64, :], DQ[0:64, :], Q[0:64, :], ALU.add)
                    ctx.inc(a, "l1a_qrope", 1)

        wq_stack.close()
        with nc.Block() as block:
            @block.tensor
            def _(tensor):
                for h in range(H):
                    ctx.wait(tensor, "l1a_qrope", H)
                    if h > 0:
                        ctx.wait(tensor, "l1a_ex", NKC * h)
                        ctx.wait(tensor, "l1a_fin", 4 * h)
                    for sl in range(8):
                        if sl >= 2:
                            ctx.wait(tensor, "l1a_kev", 8 * h + sl - 1)
                        for lc in range(2):
                            mm = tensor.matmul(bank[sl % 2][0:64, :],
                                               WK[:, lc * 1024 + h * 64: lc * 1024 + (h + 1) * 64],
                                               kvfm[:, lc * 4096 + sl * 512: lc * 4096 + (sl + 1) * 512],
                                               start=(lc == 0), stop=(lc == 1))
                        ctx.inc(mm, "l1a_kpe", 1)
                    for kc in range(NKC):
                        vb = 2 + (kc // 4) % 2
                        if kc >= 8:
                            ctx.wait(tensor, "l1a_vev", 8 * h + kc // 4 - 1)
                        for lc in range(2):
                            mm = tensor.matmul(bank[vb][:, (kc % 4) * 128:(kc % 4 + 1) * 128],
                                               kvfm[:, lc * 4096 + kc * 128: lc * 4096 + (kc + 1) * 128],
                                               WV[:, lc * 2048 + h * 128: lc * 2048 + (h + 1) * 128],
                                               start=(lc == 0), stop=(lc == 1))
                        if kc % 4 == 3:
                            ctx.inc(mm, "l1a_vpe", 1)
                    ctx.wait(tensor, "l1a_sq", 5 * h + 1)
                    for pc in range(2):
                        mm = tensor.matmul(bank[6 + pc][0:1, :], onesc[0:64, 0:1],
                                           scr[0:64, pc * 512:(pc + 1) * 512],
                                           start=True, stop=True)
                        ctx.inc(mm, "l1a_bnd_pe", 1)
                    for rnd in range(4):
                        ctx.wait(tensor, "l1a_sq", 5 * h + 2 + rnd)
                        ctx.wait(tensor, "l1a_bred", 10 * h + 2 * (rnd + 1))
                        for pc in range(2):
                            mm = tensor.matmul(bank[6 + pc][0:1, :], onesc[0:64, 0:1],
                                               exT[0:64, pc * 512:(pc + 1) * 512],
                                               start=True, stop=True)
                            ctx.inc(mm, "l1a_bnd_pe", 1)
                    ctx.wait(tensor, "l1a_negc", h + 1)
                    ctx.wait(tensor, "l1a_bred", 10 * h + 10)
                    mm = tensor.matmul(bank[7][:, 0:1], onesr[:, :], sm[0:1, 1:2], start=True, stop=True)
                    ctx.inc(mm, "l1a_bcc", 1)
                    for kc in range(NKC):
                        ctx.wait(tensor, "l1a_ex", NKC * h + kc)
                        if kc == 0:
                            ctx.wait(tensor, "l1a_negcpp", h + 1)
                            ctx.wait(tensor, "l1a_krope", h + 1)
                        for th in range(2):
                            mm = tensor.matmul(bank[th],
                                               Kt[0:64, kc * 128:(kc + 1) * 128],
                                               qall[0:64, h * 1024 + th * 512: h * 1024 + (th + 1) * 512],
                                               start=True, stop=True)
                        ctx.inc(mm, "l1a_sc", 1)
                        ctx.wait(tensor, "l1a_ex", NKC * h + kc + 1)
                        ctx.wait(tensor, "l1a_vev", 8 * (h + 1))
                        for th in range(2):
                            tensor.matmul(bank[2 + th][0:1, :], onesc[:, 0:1],
                                          exT[:, (kc % 2) * 1024 + th * 512:(kc % 2) * 1024 + (th + 1) * 512],
                                          start=(kc == 0), stop=(kc == NKC - 1))
                        for th in range(2):
                            mm = tensor.matmul(bank[4 + th],
                                               Vt[:, kc * 128:(kc + 1) * 128],
                                               exT[:, (kc % 2) * 1024 + th * 512:(kc % 2) * 1024 + (th + 1) * 512],
                                               start=(kc == 0), stop=(kc == NKC - 1))
                        ctx.inc(mm, "l1a_dn", 1)
                    ctx.wait(tensor, "l1a_rc", h + 1)
                    for th in range(2):
                        mm = tensor.matmul(psT[3][:, th * 512:(th + 1) * 512], onesr[:, :],
                                           rc[0:1, th * 512:(th + 1) * 512], start=True, stop=True)
                    ctx.inc(mm, "l1a_bcr", 1)

            @block.scalar
            def _(scalar):
                for h in range(H):
                    for sl in range(8):
                        ctx.wait(scalar, "l1a_kpe", 8 * h + sl + 1)
                        a = scalar.activation(Kt[0:64, sl * 512:(sl + 1) * 512], bank[sl % 2][0:64, :], AF.Copy)
                        ctx.inc(a, "l1a_kev", 1)
                    for vg in range(8):
                        ctx.wait(scalar, "l1a_vpe", 8 * h + vg + 1)
                        a = scalar.activation(Vt[:, vg * 512:(vg + 1) * 512], bank[2 + vg % 2], AF.Copy)
                        ctx.inc(a, "l1a_vev", 1)
                    ctx.wait(scalar, "l1a_qrope", H)
                    if h > 0:
                        ctx.wait(scalar, "l1a_bnd_pe", 10 * h)
                    a = scalar.activation(scr[0:64, 0:1024],
                                          qall[0:64, h * 1024:(h + 1) * 1024], AF.Square)
                    ctx.inc(a, "l1a_sq", 1)
                    ctx.wait(scalar, "l1a_krope", h + 1)
                    ctx.wait(scalar, "l1a_dn", NKC * h)
                    for rnd in range(4):
                        if rnd >= 1:
                            ctx.wait(scalar, "l1a_bnd_pe", 10 * h + 2 + 2 * rnd)
                        a = scalar.activation(exT[0:64, 0:1024],
                                              Kt[0:64, rnd * 1024:(rnd + 1) * 1024], AF.Square)
                        ctx.inc(a, "l1a_sq", 1)
                    ctx.wait(scalar, "l1a_bmax", h + 1)
                    ctx.edge(scalar.activation(sm[0:1, 1:2], sm[0:1, 4:5], AF.Sqrt), scalar, "actS")
                    a = scalar.activation(sm[0:1, 1:2], sm[0:1, 1:2], AF.Copy, scale=-1.0)
                    ctx.inc(a, "l1a_negc", 1)
                    ctx.wait(scalar, "l1a_bcc", h + 1)
                    a = scalar.activation(sm[:, 2:3], bank[7][:, 0:1], AF.Copy)
                    ctx.inc(a, "l1a_negcpp", 1)
                    for kc in range(NKC):
                        ctx.wait(scalar, "l1a_sc", NKC * h + kc + 1)
                        if kc >= 2:
                            ctx.wait(scalar, "l1a_dn", NKC * h + kc - 1)
                        a = scalar.activation(exT[:, (kc % 2) * 1024:(kc % 2 + 1) * 1024],
                                              psT[0][:, :], AF.Exp, bias=sm[:, 2:3])
                        ctx.inc(a, "l1a_ex", 1)
                    ctx.wait(scalar, "l1a_bcr", h + 1)
                    a = scalar.activation(rb[:, :], psT[3][:, :], AF.Copy)
                    ctx.inc(a, "l1a_rbe", 1)

            @block.vector
            def _(vector):
                for h in range(H):
                    ctx.wait(vector, "l1a_kev", 8 * (h + 1))
                    ctx.wait(vector, "l1a_bnd_pe", 10 * h + 2)
                    for half in range(2):
                        KH = Kt[0:64, half * 2048:(half + 1) * 2048]
                        RH = scr[64:128, 0:2048]
                        TMP = scr[0:64, 0:2048]
                        ctx.edge(vector.tensor_copy(RH[0:32, :], KH[32:64, :]), vector, "dveS")
                        ctx.edge(vector.tensor_copy(RH[32:64, :], KH[0:32, :]), vector, "dveS")
                        ctx.edge(vector.tensor_tensor(RH, RH,
                                                      SINK[:, half * 2048:(half + 1) * 2048], ALU.mult), vector, "dveS")
                        ctx.edge(vector.tensor_copy(TMP, RH), vector, "dveS")
                        ctx.edge(vector.tensor_tensor(KH[0:64, :], KH[0:64, :],
                                                      COSK[:, half * 2048:(half + 1) * 2048], ALU.mult), vector, "dveS")
                        a = vector.tensor_tensor(KH[0:64, :], KH[0:64, :], TMP, ALU.add)
                        ctx.edge(a, vector, "dveS")
                    ctx.inc(vector.tensor_copy(sm[0:1, 20:21], sm[0:1, 20:21]), "l1a_krope", 1)
                    for stage in range(5):
                        ctx.wait(vector, "l1a_bnd_pe", 10 * h + 2 * (stage + 1))
                        for pc in range(2):
                            ctx.edge(vector.tensor_reduce(sm[0:1, 8 + stage * 2 + pc: 9 + stage * 2 + pc],
                                                          bank[6 + pc][0:1, :], AX.X, ALU.max), vector, "dveS")
                        ctx.inc(vector.tensor_copy(sm[0:1, 21:22], sm[0:1, 21:22]), "l1a_bred", 2)
                    ctx.edge(vector.tensor_tensor(sm[0:1, 4:5], sm[0:1, 8:9], sm[0:1, 9:10], ALU.max), vector, "dveS")
                    ctx.edge(vector.tensor_tensor(sm[0:1, 5:6], sm[0:1, 10:11], sm[0:1, 11:12], ALU.max), vector, "dveS")
                    for jj in range(12, 18):
                        ctx.edge(vector.tensor_tensor(sm[0:1, 5:6], sm[0:1, 5:6], sm[0:1, jj:jj + 1], ALU.max), vector, "dveS")
                    a = vector.tensor_tensor(sm[0:1, 4:5], sm[0:1, 4:5], sm[0:1, 5:6], ALU.mult)
                    ctx.inc(a, "l1a_bmax", 1)
                    ctx.wait(vector, "l1a_dn", NKC * (h + 1))
                    ctx.edge(vector.reciprocal(rc[0:1, 0:512], bank[2][0:1, :]), vector, "dveS")
                    a = vector.reciprocal(rc[0:1, 512:1024], bank[3][0:1, :])
                    ctx.inc(a, "l1a_rc", 1)
                    ctx.wait(vector, "l1a_rbe", h + 1)
                    for th in range(2):
                        a = vector.tensor_tensor(O[:, h * 1024 + th * 512: h * 1024 + (th + 1) * 512],
                                                 psT[2][:, th * 512:(th + 1) * 512],
                                                 rb[:, th * 512:(th + 1) * 512], ALU.mult)
                        ctx.inc(a, "l1a_fin", 2)

    if DEBUG_IB:
        dbg_O = ctx.dram_out("dbg_O", [128, 16 * 1024], BF16)
        dbg_q = ctx.dram_out("dbg_q", [128, 16 * 1024], BF16)
        dbg_K = ctx.dram_out("dbg_K", [128, 4096], BF16)
        dbg_V = ctx.dram_out("dbg_V", [128, 4096], BF16)
        dbg_sm = ctx.dram_out("dbg_sm", [128, 32], F32)
        dbg_kvfm = ctx.dram_out("dbg_kvfm", [128, 8192], BF16)
        with nc.Block() as block:
            @block.sync
            def _(sync):
                ctx.inc(sync.dma_start(out=dbg_O.ap(), in_=O[:, :]), "dbg", 16)
                ctx.inc(sync.dma_start(out=dbg_q.ap(), in_=qall[:, :]), "dbg", 16)
                ctx.inc(sync.dma_start(out=dbg_K.ap(), in_=Kt[:, :]), "dbg", 16)
                ctx.inc(sync.dma_start(out=dbg_V.ap(), in_=Vt[:, :]), "dbg", 16)
                ctx.inc(sync.dma_start(out=dbg_sm.ap(), in_=sm[:, :]), "dbg", 16)
                ctx.inc(sync.dma_start(out=dbg_kvfm.ap(), in_=kvfm[:, :]), "dbg", 16)
                ctx.wait_all(sync, "dbg")

    with nc.sbuf_tensor("l1a_wop", [128, 4 * 2048], BF16, side="right") as wop:
        with nc.Block() as block:
            @block.gpsimd
            def _(g):
                for tp in range(4):
                    for c in range(16):
                        j = tp * 16 + c
                        bw = j % 4
                        if j >= 4:
                            ctx.wait(g, "l1a_ope", j - 3)
                        ctx.inc(g.dma_start(out=wop[:, bw * 2048:(bw + 1) * 2048],
                                            in_=wo.ap()[c * 128:(c + 1) * 128, :]), "l1a_wo", 16)

            @block.tensor
            def _(tensor):
                for tp in range(4):
                    for c in range(16):
                        j = tp * 16 + c
                        bw = j % 4
                        ctx.wait(tensor, "l1a_wo", 16 * (j + 1))
                        if c == 0 and tp >= 1:
                            ctx.wait(tensor, "l1a_ores", 8 * tp)
                        for tsp in range(2):
                            ts_ = tp * 2 + tsp
                            for hs in range(4):
                                mm = tensor.matmul(bank[tsp * 4 + hs],
                                                   O[:, c * 1024 + ts_ * 128: c * 1024 + ts_ * 128 + 128],
                                                   wop[:, bw * 2048 + hs * 512: bw * 2048 + (hs + 1) * 512],
                                                   start=(c == 0), stop=(c == 15))
                        ctx.inc(mm, "l1a_ope", 1)

            @block.vector
            def _(vector):
                for tp in range(4):
                    ctx.wait(vector, "l1a_ope", 16 * (tp + 1))
                    for tsp in range(2):
                        ts_ = tp * 2 + tsp
                        for hs in range(4):
                            dstv = res[:, ts_ * D + hs * 512: ts_ * D + (hs + 1) * 512]
                            a = vector.tensor_tensor(dstv, dstv, bank[tsp * 4 + hs], ALU.add)
                            ctx.inc(a, "l1a_ores", 1)





L0_QOF, L0_KOF, L0_VOF, L0_GOF = 0, 1536, 3072, 4608
L0_WT_COLS = 6144


def emit_l0_attn_prep(ctx):
    nc = ctx.nc
    Wq = ctx.dram_in("l0a_Wq", [1536, D], F32)
    Wk = ctx.dram_in("l0a_Wk", [1536, D], F32)
    Wv = ctx.dram_in("l0a_Wv", [1536, D], F32)
    Wg = ctx.dram_in("l0a_Wg", [1536, D], F32)
    Wb = ctx.dram_in("l0a_Wb", [6, D], F32)
    Wa = ctx.dram_in("l0a_Wa", [6, D], F32)
    ident_b = ctx.dram_in("ident_b", [128, 128], BF16)
    w_bf = ctx.dram_tmp("l0a_wbf", [4 * 1536, D], BF16)
    wt = ctx.dram_tmp("l0a_wt", [D, L0_WT_COLS], BF16)
    wbat = ctx.dram_tmp("l0a_wbat", [D, 16], BF16)

    idt = ctx.sbuf("identb", [128, 128], BF16)

    with nc.sbuf_tensor("l0a_cst", [128, 2 * 2048], F32, side="right") as cf, \
         nc.sbuf_tensor("l0a_csb", [128, 2 * 2048], BF16, side="right") as cb, \
         nc.sbuf_tensor("l0a_ba", [128, 4 * 2048], BF16, side="right") as ba, \
         nc.sbuf_tensor("l0a_bat", [128, 64], BF16, side="right") as bat:
        with nc.Block() as block:
            @block.sync
            def _(sync):
                for m, W in enumerate([Wq, Wk, Wv, Wg]):
                    for rr in range(12):
                        j = m * 12 + rr
                        bf = j % 2
                        if j >= 2:
                            ctx.wait(sync, "l0a_cw", 16 * (j - 1))
                        ctx.inc(sync.dma_start(out=cf[:, bf * 2048:(bf + 1) * 2048],
                                               in_=W.ap()[rr * 128:(rr + 1) * 128, :]), "l0a_cl", 16)

            @block.vector
            def _(vector):
                for j in range(48):
                    bf = j % 2
                    ctx.wait(vector, "l0a_cl", 16 * (j + 1))
                    if j >= 2:
                        ctx.wait(vector, "l0a_cw", 16 * (j - 1))
                    a = vector.tensor_copy(cb[:, bf * 2048:(bf + 1) * 2048], cf[:, bf * 2048:(bf + 1) * 2048])
                    ctx.inc(a, "l0a_cc", 1)

            @block.scalar
            def _(scalar):
                for j in range(48):
                    bf = j % 2
                    ctx.wait(scalar, "l0a_cc", j + 1)
                    a = scalar.dma_start(out=w_bf.ap()[j * 128:(j + 1) * 128, :],
                                         in_=cb[:, bf * 2048:(bf + 1) * 2048])
                    ctx.inc(a, "l0a_cw", 16)
                ctx.wait_all(scalar, "l0a_cw")

        with nc.Block() as block:
            @block.sync
            def _(sync):
                for m in range(4):
                    for dc in range(16):
                        j = m * 16 + dc
                        bf = j % 2
                        if j >= 2:
                            ctx.wait(sync, "l0a_tw", 16 * (j - 1))
                        ctx.inc(sync.dma_start_transpose(
                            cb[:, bf * 2048: bf * 2048 + 1536],
                            w_bf.ap()[m * 1536:(m + 1) * 1536, dc * 128:(dc + 1) * 128]), "l0a_tl", 16)
                        ctx.wait(sync, "l0a_tl", 16 * (j + 1))
                        ctx.inc(sync.dma_start(out=wt.ap()[dc * 128:(dc + 1) * 128, m * 1536:(m + 1) * 1536],
                                               in_=cb[:, bf * 2048: bf * 2048 + 1536]), "l0a_tw", 16)
                ctx.wait_all(sync, "l0a_tw")
                ctx.inc(sync.dma_start(out=idt[:, :], in_=ident_b.ap()), "l0a_id", 16)

        wba_bf = ctx.dram_tmp("l0a_wba_bf", [16, D], BF16)
        with nc.Block() as block:
            @block.sync
            def _(sync):
                ctx.wait(sync, "l0a_bz", 1)
                ctx.inc(sync.dma_start(out=cf[0:6, 0:2048], in_=Wb.ap()), "l0a_bal", 16)
                ctx.inc(sync.dma_start(out=cf[8:14, 0:2048], in_=Wa.ap()), "l0a_bal", 16)

            @block.vector
            def _(vector):
                ctx.edge(vector.memset(cf[0:16, 0:2048], 0.0), vector, "dveS")
                ctx.inc(vector.tensor_copy(cf[0:1, 0:1], cf[0:1, 0:1]), "l0a_bz", 1)
                ctx.wait_all(vector, "l0a_bal")
                a = vector.tensor_copy(cb[0:16, 0:2048], cf[0:16, 0:2048])
                ctx.inc(a, "l0a_bac", 1)

            @block.scalar
            def _(scalar):
                ctx.wait(scalar, "l0a_bac", 1)
                a = scalar.dma_start(out=wba_bf.ap(), in_=cb[0:16, 0:2048])
                ctx.inc(a, "l0a_baw", 16)

            @block.sync
            def _(sync):
                ctx.wait_all(sync, "l0a_baw")
                for dc in range(16):
                    bf = dc % 2
                    if dc >= 2:
                        ctx.wait(sync, "l0a_btw", 16 * (dc - 1))
                    ctx.inc(sync.dma_start_transpose(bat[:, bf * 16:(bf + 1) * 16],
                                                     wba_bf.ap()[:, dc * 128:(dc + 1) * 128]), "l0a_btl", 16)
                    ctx.wait(sync, "l0a_btl", 16 * (dc + 1))
                    ctx.inc(sync.dma_start(out=wbat.ap()[dc * 128:(dc + 1) * 128, :],
                                           in_=bat[:, bf * 16:(bf + 1) * 16]), "l0a_btw", 16)
                ctx.wait_all(sync, "l0a_btw")
    return wt, wbat




def emit_l0_attn_proj(ctx, xfm):
    nc = ctx.nc
    psT = ctx.psT
    wt = ctx.dram["l0a_wt"]
    qkv_loc = ctx.dram_tmp("l0a_qkv_loc", [L0_WT_COLS, NT], BF16)

    with nc.sbuf_tensor("l0a_lhs", [128, 2 * 2048], BF16, side="right") as lhs, \
         nc.sbuf_tensor("l0a_ev", [128, 2 * 1024], BF16, side="right") as evt:
        with nc.Block() as block:
            @block.gpsimd
            def _(g):
                for fc in range(48):
                    b = fc % 2
                    if fc >= 2:
                        ctx.wait(g, "l0p_pe", 2 * (fc - 1))
                    ctx.inc(g.dma_start(out=lhs[:, b * 2048:(b + 1) * 2048].rearrange("p (c f) -> p c f", c=16),
                                        in_=wt.ap()[:, fc * 128:(fc + 1) * 128].rearrange("(c p) f -> p c f", p=128)),
                            "l0p_w", 16)

            @block.tensor
            def _(tensor):
                for fc in range(48):
                    b = fc % 2
                    ctx.wait(tensor, "l0p_w", 16 * (fc + 1))
                    if fc >= 2:
                        ctx.wait(tensor, "l0p_ec", fc - 1)
                    for th in range(2):
                        for dd in range(16):
                            mm = tensor.matmul(psT[b][:, th * 512:(th + 1) * 512],
                                               lhs[:, b * 2048 + dd * 128:(b * 2048) + (dd + 1) * 128],
                                               xfm[:, dd * 1024 + th * 512: dd * 1024 + (th + 1) * 512],
                                               start=(dd == 0), stop=(dd == 15))
                    ctx.inc(mm, "l0p_pe", 2)

            @block.scalar
            def _(scalar):
                for fc in range(48):
                    b = fc % 2
                    ctx.wait(scalar, "l0p_pe", 2 * (fc + 1))
                    if fc >= 2:
                        ctx.wait(scalar, "l0p_st", 16 * (fc - 1))
                    a = scalar.activation(evt[:, b * 1024:(b + 1) * 1024], psT[b][:, :], AF.Copy)
                    ctx.inc(a, "l0p_ec", 1)

            @block.sync
            def _(sync):
                for fc in range(48):
                    b = fc % 2
                    ctx.wait(sync, "l0p_ec", fc + 1)
                    ctx.inc(sync.dma_start(out=qkv_loc.ap()[fc * 128:(fc + 1) * 128, :],
                                           in_=evt[:, b * 1024:(b + 1) * 1024]), "l0p_st", 16)
                ctx.wait_all(sync, "l0p_st")


def emit_l0_attn_bg(ctx, xfm):
    nc = ctx.nc
    bank = ctx.bank
    wbat = ctx.dram["l0a_wbat"]
    A_log = ctx.dram_in("l0a_A_log", [1, 16], F32)
    dtb = ctx.dram_in("l0a_dtb", [1, 16], F32)
    ones_rf = ctx.dram_in("ones_row_f", [1, 128], F32)
    idf = ctx.dram_in("ident_f", [128, 128], F32)
    bg_loc = ctx.dram_tmp("l0a_bg_loc", [16, NT], F32)

    with nc.sbuf_tensor("l0a_wba", [128, 256], BF16, side="right") as wba, \
         nc.sbuf_tensor("l0a_bgs", [128, 128], F32, side="right") as bgs, \
         nc.sbuf_tensor("l0a_bgf", [128, 16 * 8], F32, side="right") as bgf, \
         nc.sbuf_tensor("l0a_onr2", [1, 128], F32, side="right") as onesr, \
         nc.sbuf_tensor("l0a_idf2", [128, 128], F32, side="right") as idft:
        with nc.Block() as block:
            @block.gpsimd
            def _(g):
                ctx.inc(g.dma_start(out=wba[:, :].rearrange("p (c f) -> p c f", c=16),
                                    in_=wbat.ap().rearrange("(c p) f -> p c f", p=128)), "bg_c", 16)
                ctx.inc(g.dma_start(out=onesr[:, :], in_=ones_rf.ap()), "bg_c", 16)
                ctx.inc(g.dma_start(out=idft[:, :], in_=idf.ap()), "bg_c", 16)
                ctx.inc(g.dma_start(out=bgs[0:1, 0:16], in_=A_log.ap()), "bg_c", 16)
                ctx.inc(g.dma_start(out=bgs[0:1, 32:48], in_=dtb.ap()), "bg_c", 16)

            @block.scalar
            def _(scalar):
                ctx.wait_all(scalar, "bg_c")
                a = scalar.activation(bgs[0:1, 16:32], bgs[0:1, 0:16], AF.Exp)
                ctx.inc(a, "bg_e", 1)

            @block.vector
            def _(vector):
                ctx.wait(vector, "bg_e", 1)
                a = vector.tensor_scalar_mul(bgs[0:1, 16:32], bgs[0:1, 16:32], -1.0)
                ctx.inc(a, "bg_n", 1)

            @block.tensor
            def _(tensor):
                ctx.wait(tensor, "bg_n", 1)
                mm = tensor.matmul(bank[6][:, 0:16], onesr[:, :], bgs[0:1, 16:32], start=True, stop=True)
                ctx.inc(mm, "bg_b1", 1)
                mm = tensor.matmul(bank[7][:, 0:16], onesr[:, :], bgs[0:1, 32:48], start=True, stop=True)
                ctx.inc(mm, "bg_b2", 1)

            @block.scalar
            def _(scalar):
                ctx.wait(scalar, "bg_b1", 1)
                ctx.edge(scalar.activation(bgs[:, 16:32], bank[6][:, 0:16], AF.Copy), scalar, "actS")
                ctx.wait(scalar, "bg_b2", 1)
                a = scalar.activation(bgs[:, 32:48], bank[7][:, 0:16], AF.Copy)
                ctx.inc(a, "bg_rdy", 1)

        with nc.Block() as block:
            @block.tensor
            def _(tensor):
                for ts_ in range(8):
                    if ts_ >= 1:
                        ctx.wait(tensor, "bg_raw", ts_)
                    for dd in range(16):
                        mm = tensor.matmul(bank[4][:, 0:16],
                                           xfm[:, dd * 1024 + ts_ * 128: dd * 1024 + ts_ * 128 + 128],
                                           wba[:, dd * 16:(dd + 1) * 16],
                                           start=(dd == 0), stop=(dd == 15))
                    ctx.inc(mm, "bg_pe", 1)
                    ctx.wait(tensor, "bg_out", ts_ + 1)
                    if ts_ >= 1:
                        ctx.wait(tensor, "bg_fm", ts_)
                    mm = tensor.transpose(ctx.psT[0][0:16, 0:128], bgs[:, 64:80], idft[:, :])
                    ctx.inc(mm, "bg_tr", 1)

            @block.scalar
            def _(scalar):
                for ts_ in range(8):
                    ctx.wait(scalar, "bg_pe", ts_ + 1)
                    v = ctx.inc(scalar.activation(bgs[:, 48:64], bank[4][:, 0:16], AF.Copy), "bg_raw", 1)
                    ctx.wait(scalar, "bg_raw", v)
                    ctx.edge(scalar.activation(bgs[:, 64:70], bgs[:, 48:54], AF.Sigmoid), scalar, "actS")
                    ctx.wait(scalar, "bg_ga", ts_ + 1)
                    ctx.edge(scalar.activation(bgs[:, 80:86], bgs[:, 112:118], AF.Abs), scalar, "actS")
                    v = ctx.inc(scalar.activation(bgs[:, 86:92], bgs[:, 80:86], AF.Exp, scale=-1.0), "bg_t2", 1)
                    ctx.wait(scalar, "bg_t2", v)
                    ctx.wait(scalar, "bg_t3", ts_ + 1)
                    ctx.edge(scalar.activation(bgs[:, 96:102], bgs[:, 102:108], AF.Ln), scalar, "actS")
                    a = scalar.activation(bgs[:, 120:126], bgs[:, 112:118], AF.Relu)
                    ctx.inc(a, "bg_ln", 1)
                    ctx.wait(scalar, "bg_tr", ts_ + 1)
                    if ts_ >= 1:
                        ctx.wait(scalar, "bg_st", 16 * ts_)
                    a = scalar.activation(bgf[0:16, 0:128], ctx.psT[0][0:16, 0:128], AF.Copy)
                    ctx.inc(a, "bg_fm", 1)

            @block.vector
            def _(vector):
                ctx.wait(vector, "bg_rdy", 1)
                for ts_ in range(8):
                    ctx.wait(vector, "bg_raw", ts_ + 1)
                    a = vector.tensor_tensor(bgs[:, 112:118], bgs[:, 56:62], bgs[:, 40:46], ALU.add)
                    ctx.inc(a, "bg_ga", 1)
                    ctx.wait(vector, "bg_t2", ts_ + 1)
                    a = vector.tensor_scalar_add(bgs[:, 102:108], bgs[:, 86:92], 1.0)
                    ctx.inc(a, "bg_t3", 1)
                    ctx.wait(vector, "bg_ln", ts_ + 1)
                    ctx.edge(vector.tensor_tensor(bgs[:, 96:102], bgs[:, 96:102], bgs[:, 120:126], ALU.add), vector, "dveS")
                    a = vector.tensor_tensor(bgs[:, 72:78], bgs[:, 96:102], bgs[:, 24:30], ALU.mult)
                    ctx.inc(a, "bg_out", 1)

            @block.gpsimd
            def _(g):
                for ts_ in range(8):
                    ctx.wait(g, "bg_fm", ts_ + 1)
                    ctx.inc(g.dma_start(out=bg_loc.ap()[:, ts_ * 128:(ts_ + 1) * 128],
                                        in_=bgf[0:16, 0:128]), "bg_st", 16)
                ctx.wait_all(g, "bg_st")


def emit_l0_attn_ag(ctx):
    nc = ctx.nc
    qkv_loc = ctx.dram["l0a_qkv_loc"]
    bg_loc = ctx.dram["l0a_bg_loc"]
    qkv_ag = ctx.dram_tmp("l0a_qkv_ag", [4 * L0_WT_COLS, NT], BF16)
    bg_ag = ctx.dram_tmp("l0a_bg_ag", [64, NT], F32)
    with nc.Block() as block:
        @block.gpsimd
        def _(g):
            ctx.inc(g.collective_compute("AllGather", ALU.bypass, replica_groups=RG4,
                                         ins=[qkv_loc.ap()], outs=[qkv_ag.ap()]), "l0a_cc", 1)
            ctx.inc(g.collective_compute("AllGather", ALU.bypass, replica_groups=RG4,
                                         ins=[bg_loc.ap()], outs=[bg_ag.ap()]), "l0a_cc", 1)
            ctx.wait_all(g, "l0a_cc")




class Ser:

    def __init__(self, ctx, name):
        self.ctx = ctx
        self.name = name
        self.ops = []

    def add(self, eng, n, fn, dma=False, bg=None, wait=None):
        self.ops.append((eng, n, fn, dma, bg, wait))

    def emit_into(self, block):
        ctx = self.ctx
        sem = ctx.sem(self.name)
        base = ctx.semb[self.name] + ctx.val(self.name)

        def mk(eng_name):
            def body(eng):
                cum = base
                for (en, n, fn, dma, bg, wt) in self.ops:
                    amt = (16 if dma else 1)
                    if en == eng_name:
                        if wt is not None:
                            eng.wait_ge(ctx.sem(wt[0]), ctx.semb[wt[0]] + wt[1])
                        if fn is not None:
                            eng.wait_ge(sem, cum)
                            r = fn(eng)
                            instrs = r if isinstance(r, list) else [r]
                            assert len(instrs) == n, f"{eng_name} expected {n} got {len(instrs)}"
                            tgt = sem if bg is None else ctx.sem(bg)
                            for ins_ in instrs:
                                ins_.then_inc(tgt, amt)
                    if bg is None:
                        cum += amt * n
                return None
            return body

        for (_, n, fn, dma, bg, _) in self.ops:
            if bg is not None and fn is not None:
                ctx.sem(bg)
                ctx.semv[bg] += (16 if dma else 1) * n
                ctx.phys_tot[id(ctx.semh[bg])] += (16 if dma else 1) * n
        block.sync(mk("sync"))
        block.tensor(mk("tensor"))
        block.scalar(mk("scalar"))
        block.vector(mk("vector"))
        block.gpsimd(mk("gpsimd"))
        tot = sum((16 if dma else 1) * n for (_, n, _, dma, bg, _) in self.ops if bg is None)
        ctx.semv[self.name] += tot
        ctx.phys_tot[id(sem)] += tot


def emit_l0_attn_wo_prep(ctx):
    nc = ctx.nc
    Wo = ctx.dram_in("l0a_Wo", [D, 1536], F32)
    wnorm = ctx.dram_in("l0a_wnorm", [1, 1536], F32)
    ones_rf = ctx.dram_in("ones_row_f", [1, 128], F32)
    wo_bf = ctx.dram_tmp("l0a_wo_bf", [D, 1536], BF16)
    wot = ctx.dram_tmp("l0a_wot", [1536, D], BF16)
    psT = ctx.psT

    with nc.sbuf_tensor("wo_cf", [128, 2 * 1536], F32, side="right") as cf,          nc.sbuf_tensor("wo_cb", [128, 2 * 2048], BF16, side="right") as cb,          nc.sbuf_tensor("wo_nb", [128, 1536], F32, side="right") as wnb,          nc.sbuf_tensor("wo_onr", [1, 128], F32, side="right") as onr2:
        ser = Ser(ctx, "l0wo")
        ser.add("gpsimd", 2, lambda g: [
            g.dma_start(out=onr2[:, :], in_=ones_rf.ap()),
            g.dma_start(out=wnb[0:1, :], in_=wnorm.ap())], dma=True)
        ser.add("tensor", 3, lambda t: [
            t.matmul(psT[0][:, 0:512], onr2[:, :], wnb[0:1, 0:512], start=True, stop=True),
            t.matmul(psT[1][:, 0:512], onr2[:, :], wnb[0:1, 512:1024], start=True, stop=True),
            t.matmul(psT[2][:, 0:512], onr2[:, :], wnb[0:1, 1024:1536], start=True, stop=True)])
        ser.add("scalar", 3, lambda sc: [
            sc.activation(wnb[:, 0:512], psT[0][:, 0:512], AF.Copy),
            sc.activation(wnb[:, 512:1024], psT[1][:, 0:512], AF.Copy),
            sc.activation(wnb[:, 1024:1536], psT[2][:, 0:512], AF.Copy)])
        for rr in range(16):
            b = rr % 2
            ser.add("sync", 1, lambda sy, rr=rr, b=b:
                    sy.dma_start(out=cf[:, b * 1536:(b + 1) * 1536],
                                 in_=Wo.ap()[rr * 128:(rr + 1) * 128, :]), dma=True)
            ser.add("vector", 1, lambda v, b=b:
                    v.tensor_tensor(cb[:, b * 2048: b * 2048 + 1536],
                                    cf[:, b * 1536:(b + 1) * 1536], wnb[:, 0:1536], ALU.mult))
            ser.add("scalar", 1, lambda sc, rr=rr, b=b:
                    sc.dma_start(out=wo_bf.ap()[rr * 128:(rr + 1) * 128, :],
                                 in_=cb[:, b * 2048: b * 2048 + 1536]), dma=True)
        for vc in range(12):
            b = vc % 2
            ser.add("sync", 1, lambda sy, vc=vc, b=b:
                    sy.dma_start_transpose(cb[:, b * 2048:(b + 1) * 2048],
                                           wo_bf.ap()[:, vc * 128:(vc + 1) * 128]), dma=True)
            ser.add("sync", 1, lambda sy, vc=vc, b=b:
                    sy.dma_start(out=wot.ap()[vc * 128:(vc + 1) * 128, :],
                                 in_=cb[:, b * 2048:(b + 1) * 2048]), dma=True)
        with nc.Block() as block:
            ser.emit_into(block)


def emit_l0_attn_scan(ctx):
    nc = ctx.nc
    psT = ctx.psT
    qkv_ag = ctx.dram["l0a_qkv_ag"]
    bg_ag = ctx.dram["l0a_bg_ag"]
    conv_q = ctx.dram_in("l0a_conv_q", [1536, 4], F32)
    conv_k = ctx.dram_in("l0a_conv_k", [1536, 4], F32)
    conv_v = ctx.dram_in("l0a_conv_v", [1536, 4], F32)
    uinfo = ctx.dram_in("l0a_uinfo", [1, 32], I32)
    masks = ctx.dram_in("l0a_masks", [128, 384], F32)
    ident_b = ctx.dram_in("ident_b", [128, 128], BF16)
    ones_rf = ctx.dram_in("ones_row_f", [1, 128], F32)
    ones_f1 = ctx.dram_in("ones_f1", [1, 1], F32)
    ones_cb = ctx.dram_in("ones_col_b", [128, 1], BF16)
    o_loc = ctx.dram_tmp("l0a_o_loc", [3 * T, 128], BF16)

    PAD = 16
    W_ = 4096 + PAD
    psB = psT[1].bitcast(BF16)

    _st = contextlib.ExitStack()
    qb = _st.enter_context(nc.sbuf_tensor("sc_qb", [128, 2 * W_], BF16, side="right"))
    kb = _st.enter_context(nc.sbuf_tensor("sc_kb", [128, 2 * W_], BF16, side="right"))
    vb = _st.enter_context(nc.sbuf_tensor("sc_vb", [128, W_], BF16, side="right"))
    scrc = _st.enter_context(nc.sbuf_tensor("sc_scr", [128, W_], BF16, side="right"))
    cw = _st.enter_context(nc.sbuf_tensor("sc_cw", [128, 24], F32, side="right"))
    ui = _st.enter_context(nc.sbuf_tensor("sc_ui", [1, 32], I32, side="right"))
    msk = _st.enter_context(nc.sbuf_tensor("sc_msk", [128, 384], F32, side="right"))
    idt = _st.enter_context(nc.sbuf_tensor("sc_id", [128, 128], BF16, side="right"))
    onr = _st.enter_context(nc.sbuf_tensor("sc_onr", [1, 128], F32, side="right"))
    on1 = _st.enter_context(nc.sbuf_tensor("sc_on1", [1, 1], F32, side="right"))
    onc = _st.enter_context(nc.sbuf_tensor("sc_onc", [128, 1], BF16, side="right"))
    cm = _st.enter_context(nc.sbuf_tensor("sc_cm", [32, 4 * 128], F32, side="right"))
    rows = _st.enter_context(nc.sbuf_tensor("sc_rows", [1, 1024], F32, side="right"))
    rowt = _st.enter_context(nc.sbuf_tensor("sc_rowt", [1, 1024], F32, side="right"))
    cols = _st.enter_context(nc.sbuf_tensor("sc_cols", [128, 8], F32, side="right"))
    kkq = _st.enter_context(nc.sbuf_tensor("sc_KKQ", [128, 2 * 128], F32, side="right"))
    Esc = _st.enter_context(nc.sbuf_tensor("sc_E", [128, 4 * 128], F32, side="right"))
    Fsc = _st.enter_context(nc.sbuf_tensor("sc_F", [128, 4 * 128], F32, side="right"))
    ntb = _st.enter_context(nc.sbuf_tensor("sc_NT", [128, 3 * 128], F32, side="right"))
    powq = _st.enter_context(nc.sbuf_tensor("sc_pow", [128, 7 * 256], F32, side="right"))
    wbuf = _st.enter_context(nc.sbuf_tensor("sc_wb", [128, 2 * 128], F32, side="right"))
    o1s = _st.enter_context(nc.sbuf_tensor("sc_o1", [128, 128], F32, side="right"))
    vks = _st.enter_context(nc.sbuf_tensor("sc_vk", [128, 3 * 128], BF16, side="right"))
    kbar = _st.enter_context(nc.sbuf_tensor("sc_kbar", [128, 2 * 128], F32, side="right"))
    Sb = _st.enter_context(nc.sbuf_tensor("sc_S", [128, 3 * 256], BF16, side="right"))
    ob = _st.enter_context(nc.sbuf_tensor("sc_ob", [128, 2 * 128], BF16, side="right"))
    nrm = _st.enter_context(nc.sbuf_tensor("sc_nrm", [128, 512], F32, side="right"))
    with _st:
        M_SU = msk[:, 0:128]; M_IU = msk[:, 128:256]; M_SL = msk[:, 256:384]
        ser = Ser(ctx, "l0sc")
        rbg_n = [0]

        ser.add("gpsimd", 6, lambda g: [
            g.dma_start(out=ui[:, :], in_=uinfo.ap()),
            g.dma_start(out=msk[:, :], in_=masks.ap()),
            g.dma_start(out=idt[:, :], in_=ident_b.ap()),
            g.dma_start(out=onr[:, :], in_=ones_rf.ap()),
            g.dma_start(out=on1[:, :], in_=ones_f1.ap()),
            g.dma_start(out=onc[:, :], in_=ones_cb.ap())], dma=True)

        gidx = ctx.dram_in("l0a_gidx", [128, 176], U32)
        idxs = _st.enter_context(nc.sbuf_tensor("sc_idx", [128, 176], U32, side="right"))
        from concourse.bass import IndirectOffsetOnAxis
        bg512 = bg_ag.ap().rearrange("r (c p) -> (r c) p", c=8)
        ser.add("gpsimd", 1, lambda g: g.dma_start(out=idxs[:, :], in_=gidx.ap()), dma=True)
        for u in range(3):
            def convld(g, u=u):
                out = []
                for w_i, (srcw, dstc) in enumerate([(conv_q, 0), (conv_q, 4), (conv_k, 8),
                                                    (conv_k, 12), (conv_v, 16)]):
                    icol = 60 + u * 3 + (0 if w_i == 0 else 1 if w_i == 1 else 0 if w_i == 2 else 1 if w_i == 3 else 2)
                    out.append(g.indirect_dma_start(
                        out=cw[:, dstc:dstc + 4], out_offset=None,
                        in_=srcw.ap(), in_offset=IndirectOffsetOnAxis(ap=idxs[:, icol:icol + 1], axis=0)))
                return out
            ser.add("gpsimd", 5, convld, dma=True)
            def bgld(g, u=u):
                return [g.indirect_dma_start(
                            out=cm[0:32, 0:128], out_offset=None,
                            in_=bg512, in_offset=IndirectOffsetOnAxis(ap=idxs[0:32, 69 + u * 2:70 + u * 2], axis=0)),
                        g.indirect_dma_start(
                            out=cm[0:32, 128:256], out_offset=None,
                            in_=bg512, in_offset=IndirectOffsetOnAxis(ap=idxs[0:32, 70 + u * 2:71 + u * 2], axis=0))]
            ser.add("gpsimd", 2, bgld, dma=True)
            for bi in range(5):
                def chunkld(g, u=u, bi=bi):
                    out = []
                    buf = [qb, qb, kb, kb, vb][bi]
                    wslot = [0, 1, 0, 1, 0][bi]
                    for s in range(4):
                        icol = u * 20 + bi * 4 + s
                        out.append(g.indirect_dma_start(
                            out=buf[:, wslot * W_ + PAD + s * 1024: wslot * W_ + PAD + (s + 1) * 1024],
                            out_offset=None, in_=qkv_ag.ap(),
                            in_offset=IndirectOffsetOnAxis(ap=idxs[:, icol:icol + 1], axis=0)))
                    return out
                ser.add("gpsimd", 4, chunkld, dma=True)

            ser.add("vector", 6, lambda v, u=u: [
                v.memset(qb[:, 0:PAD], 0.0), v.memset(qb[:, W_:W_ + PAD], 0.0),
                v.memset(kb[:, 0:PAD], 0.0), v.memset(kb[:, W_:W_ + PAD], 0.0),
                v.memset(vb[:, 0:PAD], 0.0), v.memset(Sb[:, u * 256:(u + 1) * 256], 0.0)])

            ser.add("vector", 1, lambda v: v.tensor_copy(cm[0:32, 256:384], cm[0:32, 0:128]))
            cur = [256]
            for kk in [1, 2, 4, 8, 16, 32, 64]:
                def step(v, kk=kk, cur=cur):
                    c0 = cur[0]; o0 = 640 - c0
                    r = [v.tensor_copy(cm[0:32, o0:o0 + kk], cm[0:32, c0:c0 + kk]),
                         v.tensor_tensor(cm[0:32, o0 + kk:o0 + 128], cm[0:32, c0 + kk:c0 + 128],
                                         cm[0:32, c0:c0 + 128 - kk], ALU.add)]
                    cur[0] = o0
                    return r
                ser.add("vector", 2, step)
            ser.add("vector", 1, lambda v: v.tensor_copy(cm[0:32, 256:384], cm[0:32, 384:512]))

            def rows_pf(sy, cc):
                so = (cc % 2) * 512
                return [sy.dma_start(out=rows[0:1, so + 0:so + 128], in_=cm[cc:cc + 1, 256:384]),
                        sy.dma_start(out=rows[0:1, so + 128:so + 256], in_=cm[cc:cc + 1, 128:256])]
            ser.add("sync", 2, lambda sy: rows_pf(sy, 0), dma=True, bg="l0sc_rbg")
            rbg_n[0] += 32

            for (buf, ch, cwo) in [(qb, 0, 0), (qb, 1, 4), (kb, 0, 8), (kb, 1, 12), (vb, 0, 16)]:
                def conv(v, buf=buf, ch=ch, cwo=cwo):
                    base = ch * W_ + PAD
                    out = []
                    for tap in range(4):
                        sh = 3 - tap
                        wcol = cw[:, cwo + tap: cwo + tap + 1]
                        xv = buf[:, base - sh: base - sh + 4096]
                        if tap == 0:
                            out.append(v.tensor_scalar(scrc[:, PAD:PAD + 4096], xv, wcol, None, ALU.mult))
                        else:
                            out.append(v.scalar_tensor_tensor(scrc[:, PAD:PAD + 4096], xv, wcol,
                                                              scrc[:, PAD:PAD + 4096], ALU.mult, ALU.add))
                    return out
                ser.add("vector", 4, conv)
                ser.add("scalar", 1, lambda sc, buf=buf, ch=ch:
                        sc.activation(buf[:, ch * W_ + PAD: ch * W_ + PAD + 4096],
                                      scrc[:, PAD:PAD + 4096], AF.Silu))

            for which, buf in [(0, qb), (1, kb)]:
                for pc in range(8):
                    ser.add("scalar", 2, lambda sc, buf=buf, pc=pc: [
                        sc.activation(scrc[:, 0:512], buf[:, 0 * W_ + PAD + pc * 512: 0 * W_ + PAD + (pc + 1) * 512], AF.Square),
                        sc.activation(scrc[:, 512:1024], buf[:, 1 * W_ + PAD + pc * 512: 1 * W_ + PAD + (pc + 1) * 512], AF.Square)])
                    ser.add("tensor", 2, lambda t: [
                        t.matmul(psT[0][0:1, 0:512], onc[:, :], scrc[:, 0:512], start=True, stop=False),
                        t.matmul(psT[0][0:1, 0:512], onc[:, :], scrc[:, 512:1024], start=False, stop=True)])
                    ser.add("vector", 1, lambda v: v.tensor_scalar_add(nrm[0:1, 0:512], psT[0][0:1, 0:512], 1e-6))
                    ser.add("scalar", 1, lambda sc: sc.activation(nrm[0:1, 0:512], nrm[0:1, 0:512], AF.Ln))
                    ser.add("scalar", 1, lambda sc:
                            sc.activation(nrm[0:1, 0:512], nrm[0:1, 0:512], AF.Exp, scale=-0.5))
                    if which == 0:
                        ser.add("vector", 1, lambda v:
                                v.tensor_scalar_mul(nrm[0:1, 0:512], nrm[0:1, 0:512], 1.0 / 16.0))
                    ser.add("tensor", 1, lambda t: t.matmul(psT[0][:, 0:512], onr[:, :], nrm[0:1, 0:512],
                                                            start=True, stop=True))
                    ser.add("vector", 2, lambda v, buf=buf, pc=pc: [
                        v.tensor_tensor(buf[:, 0 * W_ + PAD + pc * 512: 0 * W_ + PAD + (pc + 1) * 512],
                                        buf[:, 0 * W_ + PAD + pc * 512: 0 * W_ + PAD + (pc + 1) * 512],
                                        psT[0][:, 0:512], ALU.mult),
                        v.tensor_tensor(buf[:, 1 * W_ + PAD + pc * 512: 1 * W_ + PAD + (pc + 1) * 512],
                                        buf[:, 1 * W_ + PAD + pc * 512: 1 * W_ + PAD + (pc + 1) * 512],
                                        psT[0][:, 0:512], ALU.mult)])

            for c in range(32):
                T0 = PAD + c * 128
                so = (c % 2) * 512
                gi = u * 32 + c
                ser.add("scalar", 2, lambda sc, so=so: [
                    sc.activation(rows[0:1, so + 256:so + 384], rows[0:1, so + 0:so + 128], AF.Exp),
                    sc.activation(rows[0:1, so + 384:so + 512], rows[0:1, so + 0:so + 128], AF.Exp,
                                  scale=-1.0, bias=rows[0:1, so + 127:so + 128])],
                        wait=("l0sc_rbg", rbg_n[0]))
                if c < 31:
                    ser.add("sync", 2, lambda sy, c=c: rows_pf(sy, c + 1), dma=True, bg="l0sc_rbg")
                    rbg_n[0] += 32

                def phase1(t, u=u, c=c, T0=T0, so=so):
                    out = []
                    out.append(t.matmul(psT[0][:, 0:128], kb[:, 0 * W_ + T0:0 * W_ + T0 + 128],
                                        kb[:, 0 * W_ + T0:0 * W_ + T0 + 128], start=True, stop=False))
                    out.append(t.matmul(psT[0][:, 0:128], kb[:, 1 * W_ + T0:1 * W_ + T0 + 128],
                                        kb[:, 1 * W_ + T0:1 * W_ + T0 + 128], start=False, stop=True))
                    out.append(t.matmul(psT[0][:, 128:256], kb[:, 0 * W_ + T0:0 * W_ + T0 + 128],
                                        qb[:, 0 * W_ + T0:0 * W_ + T0 + 128], start=True, stop=False))
                    out.append(t.matmul(psT[0][:, 128:256], kb[:, 1 * W_ + T0:1 * W_ + T0 + 128],
                                        qb[:, 1 * W_ + T0:1 * W_ + T0 + 128], start=False, stop=True))
                    out.append(t.matmul(psT[0][:, 512:640], onr[:, :], rows[0:1, so + 0:so + 128], start=True, stop=True))
                    out.append(t.matmul(psT[0][:, 640:768], onr[:, :], rows[0:1, so + 128:so + 256], start=True, stop=True))
                    out.append(t.matmul(psT[1][:, 0:1], rows[0:1, so + 0:so + 128], on1[:, :], start=True, stop=True))
                    out.append(t.matmul(psT[1][:, 1:2], rows[0:1, so + 128:so + 256], on1[:, :], start=True, stop=True))
                    out.append(t.matmul(psT[1][:, 2:3], rows[0:1, so + 256:so + 384], on1[:, :], start=True, stop=True))
                    out.append(t.matmul(psT[1][:, 3:4], rows[0:1, so + 384:so + 512], on1[:, :], start=True, stop=True))
                    out.append(t.matmul(psT[1][:, 4:5], onr[:, :], rows[0:1, so + 383:so + 384], start=True, stop=True))
                    out.append(t.transpose(psB[:, 1024:1152], vb[:, T0:T0 + 128], idt[:, :]))
                    out.append(t.transpose(psB[:, 1152:1280], kb[:, 0 * W_ + T0:0 * W_ + T0 + 128], idt[:, :]))
                    out.append(t.transpose(psB[:, 1280:1408], kb[:, 1 * W_ + T0:1 * W_ + T0 + 128], idt[:, :]))
                    out.append(t.matmul(psT[2][:, 0:128], kb[:, 0 * W_ + T0:0 * W_ + T0 + 128],
                                        Sb[:, u * 256:u * 256 + 128], start=True, stop=False))
                    out.append(t.matmul(psT[2][:, 0:128], kb[:, 1 * W_ + T0:1 * W_ + T0 + 128],
                                        Sb[:, u * 256 + 128:u * 256 + 256], start=False, stop=True))
                    out.append(t.matmul(psT[2][:, 512:640], qb[:, 0 * W_ + T0:0 * W_ + T0 + 128],
                                        Sb[:, u * 256:u * 256 + 128], start=True, stop=False))
                    out.append(t.matmul(psT[2][:, 512:640], qb[:, 1 * W_ + T0:1 * W_ + T0 + 128],
                                        Sb[:, u * 256 + 128:u * 256 + 256], start=False, stop=True))
                    return out
                ser.add("tensor", 18, phase1)

                ser.add("scalar", 3, lambda sc: [
                    sc.activation(kkq[:, 0:256], psT[0][:, 0:256], AF.Copy),
                    sc.activation(cols[:, 0:5], psT[1][:, 0:5], AF.Copy),
                    sc.activation(vks[:, 0:384], psB[:, 1024:1408], AF.Copy)])

                ser.add("vector", 8, lambda v, u=u: [
                    v.tensor_scalar(Esc[:, 0:128], psT[0][:, 512:640], cols[:, 0:1], None, ALU.subtract),
                    v.tensor_scalar_mul(nrm[:, 0:128], psT[0][:, 640:768], -1.0),
                    v.tensor_scalar_mul(cols[:, 5:6], cols[:, 1:2], -1.0),
                    v.tensor_scalar(nrm[:, 128:256], psT[2][:, 0:128], cols[:, 2:3], None, ALU.mult),
                    v.tensor_scalar(o1s[:, :], psT[2][:, 512:640], cols[:, 2:3], None, ALU.mult),
                    v.tensor_scalar(Sb[:, u * 256:u * 256 + 128], Sb[:, u * 256:u * 256 + 128],
                                    cols[:, 4:5], None, ALU.mult),
                    v.tensor_scalar(Sb[:, u * 256 + 128:u * 256 + 256], Sb[:, u * 256 + 128:u * 256 + 256],
                                    cols[:, 4:5], None, ALU.mult),
                    v.tensor_scalar(kbar[:, 0:128], vks[:, 128:256], cols[:, 3:4], None, ALU.mult)])
                ser.add("vector", 4, lambda v: [
                    v.tensor_scalar(kbar[:, 128:256], vks[:, 256:384], cols[:, 3:4], None, ALU.mult),
                    v.tensor_scalar_mul(Esc[:, 128:256], Esc[:, 0:128], -1.0),
                    v.tensor_scalar_min(Esc[:, 256:384], Esc[:, 0:128], 0.0),
                    v.tensor_tensor(nrm[:, 128:256], vks[:, 0:128], nrm[:, 128:256], ALU.subtract)])
                ser.add("vector", 1, lambda v: v.tensor_scalar_min(Esc[:, 128:256], Esc[:, 128:256], 0.0))
                ser.add("scalar", 2, lambda sc: [
                    sc.activation(Esc[:, 0:128], Esc[:, 256:384], AF.Exp),
                    sc.activation(Esc[:, 384:512], Esc[:, 128:256], AF.Exp)])
                ser.add("vector", 3, lambda v: [
                    v.tensor_tensor(Fsc[:, 0:128], kkq[:, 0:128], Esc[:, 0:128], ALU.mult),
                    v.tensor_tensor(Fsc[:, 128:256], kkq[:, 128:256], Esc[:, 0:128], ALU.mult),
                    v.tensor_tensor(Fsc[:, 256:384], kkq[:, 0:128], Esc[:, 384:512], ALU.mult)])
                ser.add("vector", 3, lambda v: [
                    v.tensor_tensor(Fsc[:, 0:128], Fsc[:, 0:128], M_SU, ALU.mult),
                    v.tensor_tensor(Fsc[:, 128:256], Fsc[:, 128:256], M_IU, ALU.mult),
                    v.tensor_tensor(Fsc[:, 256:384], Fsc[:, 256:384], M_SL, ALU.mult)])
                ser.add("vector", 4, lambda v: [
                    v.tensor_tensor(ntb[:, 0:128], Fsc[:, 0:128], nrm[:, 0:128], ALU.mult),
                    v.tensor_copy(ntb[:, 256:384], Fsc[:, 128:256]),
                    v.tensor_scalar(ntb[:, 128:256], Fsc[:, 256:384], cols[:, 5:6], None, ALU.mult),
                    v.tensor_scalar(wbuf[:, 0:128], nrm[:, 128:256], cols[:, 1:2], None, ALU.mult)])
                for lvl in range(6):
                    def sq(t, lvl=lvl):
                        Qp = ntb[:, 0:128] if lvl == 0 else powq[:, (lvl - 1) * 256 + 128:lvl * 256]
                        Pp = ntb[:, 128:256] if lvl == 0 else powq[:, (lvl - 1) * 256:(lvl - 1) * 256 + 128]
                        return [t.matmul(psT[3][:, 0:128], Qp, Pp, start=True, stop=True),
                                t.matmul(psT[3][:, 128:256], Pp, Qp, start=True, stop=True)]
                    ser.add("tensor", 2, sq)
                    ser.add("scalar", 1, lambda sc, lvl=lvl:
                            sc.activation(powq[:, lvl * 256:(lvl + 1) * 256], psT[3][:, 0:256], AF.Copy))
                for st in range(7):
                    def ap(t, st=st):
                        Qs = ntb[:, 0:128] if st == 0 else powq[:, (st - 1) * 256 + 128:st * 256]
                        w_in = wbuf[:, (st % 2) * 128:(st % 2 + 1) * 128]
                        return t.matmul(psT[2][:, 0:128], Qs, w_in, start=True, stop=True)
                    ser.add("tensor", 1, ap)
                    ser.add("vector", 1, lambda v, st=st:
                            v.tensor_tensor(wbuf[:, ((st + 1) % 2) * 128:((st + 1) % 2 + 1) * 128],
                                            wbuf[:, (st % 2) * 128:(st % 2 + 1) * 128],
                                            psT[2][:, 0:128], ALU.add))
                ser.add("tensor", 3, lambda t, u=u: [
                    t.matmul(psT[2][:, 512:640], ntb[:, 256:384], wbuf[:, 128:256], start=True, stop=True),
                    t.matmul(psT[3][:, 512:640], kbar[:, 0:128], wbuf[:, 128:256], start=True, stop=True),
                    t.matmul(psT[3][:, 640:768], kbar[:, 128:256], wbuf[:, 128:256], start=True, stop=True)])
                obs = (gi % 2) * 128
                ser.add("vector", 3, lambda v, u=u, obs=obs: [
                    v.tensor_tensor(ob[:, obs:obs + 128], o1s[:, :], psT[2][:, 512:640], ALU.add),
                    v.tensor_tensor(Sb[:, u * 256:u * 256 + 128], Sb[:, u * 256:u * 256 + 128],
                                    psT[3][:, 512:640], ALU.add),
                    v.tensor_tensor(Sb[:, u * 256 + 128:u * 256 + 256], Sb[:, u * 256 + 128:u * 256 + 256],
                                    psT[3][:, 640:768], ALU.add)],
                        wait=(("l0sc_obg", 16 * (gi - 1)) if gi >= 2 else None))
                ser.add("gpsimd", 1, lambda g, u=u, c=c, obs=obs:
                        g.dma_start(out=o_loc.ap()[u * 4096 + c * 128: u * 4096 + (c + 1) * 128, :],
                                    in_=ob[:, obs:obs + 128]), dma=True, bg="l0sc_obg")

        ser.add("gpsimd", 0, None, wait=("l0sc_obg", 16 * 96))
        with nc.Block() as block:
            ser.emit_into(block)
    if DEBUG_IB:
        dbg_o = ctx.dram_out("dbg_o_loc", [3 * T, 128], BF16)
        with nc.Block() as block:
            @block.sync
            def _(sync):
                ctx.inc(sync.dma_start(out=dbg_o.ap(), in_=o_loc.ap()), "dbgo", 16)
                ctx.wait_all(sync, "dbgo")


def emit_l0_attn_combine(ctx):
    nc = ctx.nc
    bank = ctx.bank
    psT = ctx.psT
    o_loc = ctx.dram["l0a_o_loc"]
    qkv_loc2 = ctx.dram["l0a_qkv_loc"]
    wot = ctx.dram["l0a_wot"]
    uinfo = ctx.dram["l0a_uinfo"]
    res = ctx.sb["res"]
    o_ag = ctx.dram_tmp("l0a_o_ag", [4 * 3 * T, 128], BF16)
    og_d = ctx.dram_tmp("l0a_og_d", [NT, 1536], BF16)

    with nc.Block() as block:
        @block.gpsimd
        def _(g):
            ctx.inc(g.collective_compute("AllGather", ALU.bypass, replica_groups=RG4,
                                         ins=[o_loc.ap()], outs=[o_ag.ap()]), "l0a_cc2", 1)
            ctx.wait_all(g, "l0a_cc2")

    with nc.sbuf_tensor("cb_otm", [128, 8 * 1536], BF16, side="right") as otm, \
         nc.sbuf_tensor("cb_gate", [128, 8 * 1536], BF16, side="right") as gsb, \
         nc.sbuf_tensor("cb_ogf", [128, 12 * 1024], BF16, side="right") as ogf, \
         nc.sbuf_tensor("cb_scr", [128, 1536], BF16, side="right") as scr3, \
         nc.sbuf_tensor("cb_ms", [128, 32], F32, side="right") as ms, \
         nc.sbuf_tensor("cb_ui", [1, 32], I32, side="right") as ui2, \
         nc.sbuf_tensor("cb_idx", [128, 96], U32, side="right") as cidx, \
         nc.sbuf_tensor("cb_wos", [128, 2 * 2048], BF16, side="right") as wos:
        ser = Ser(ctx, "l0cb")
        ser.add("gpsimd", 1, lambda g: g.dma_start(out=cidx[:, :], in_=ctx.dram["l0a_gidx"].ap()[:, 75:171][:, 0:96]), dma=True)
        ser.add("gpsimd", 1, lambda g: g.dma_start(out=ui2[:, :], in_=uinfo.ap()), dma=True)
        ser.add("sync", 8, lambda sy: [
            sy.dma_start_transpose(gsb[:, tt * 1536:(tt + 1) * 1536],
                                   qkv_loc2.ap()[4608:6144, tt * 128:(tt + 1) * 128])
            for tt in range(8)], dma=True)

        gidx2 = ctx.dram["l0a_gidx"]
        from concourse.bass import IndirectOffsetOnAxis
        idxs2 = None
        def idxld(g):
            return g.dma_start(out=ui2[:, :], in_=uinfo.ap())
        def oloads(g):
            out = []
            for s in range(4):
                for j in range(3):
                    uu = 3 * s + j
                    h = uu // 2
                    half = uu % 2
                    vof = 256 * h + 128 * half
                    for tt in range(8):
                        icol = 75 + uu * 8 + tt
                        out.append(g.indirect_dma_start(
                            out=otm[:, tt * 1536 + vof: tt * 1536 + vof + 128],
                            out_offset=None, in_=o_ag.ap(),
                            in_offset=IndirectOffsetOnAxis(ap=cidx[:, icol - 75:icol - 74], axis=0)))
            return out
        ser.add("gpsimd", 96, oloads, dma=True)

        for ts_ in range(8):
            ser.add("scalar", 1, lambda sc, ts_=ts_:
                    sc.activation(scr3[:, :], otm[:, ts_ * 1536:(ts_ + 1) * 1536], AF.Square))
            ser.add("vector", 6, lambda v: [
                v.tensor_reduce(ms[:, hh:hh + 1], scr3[:, hh * 256:(hh + 1) * 256], AX.X, ALU.add)
                for hh in range(6)])
            ser.add("vector", 1, lambda v: v.tensor_scalar(ms[:, 8:14], ms[:, 0:6], 1.0 / 256.0, 1e-5,
                                                           ALU.mult, ALU.add))
            ser.add("scalar", 1, lambda sc: sc.activation(ms[:, 8:14], ms[:, 8:14], AF.Ln))
            ser.add("scalar", 1, lambda sc: sc.activation(ms[:, 8:14], ms[:, 8:14], AF.Exp, scale=-0.5))
            ser.add("vector", 6, lambda v, ts_=ts_: [
                v.tensor_scalar(otm[:, ts_ * 1536 + hh * 256: ts_ * 1536 + (hh + 1) * 256],
                                otm[:, ts_ * 1536 + hh * 256: ts_ * 1536 + (hh + 1) * 256],
                                ms[:, 8 + hh:9 + hh], None, ALU.mult)
                for hh in range(6)])
            ser.add("scalar", 1, lambda sc, ts_=ts_:
                    sc.activation(gsb[:, ts_ * 1536:(ts_ + 1) * 1536],
                                  gsb[:, ts_ * 1536:(ts_ + 1) * 1536], AF.Silu))
            ser.add("vector", 1, lambda v, ts_=ts_:
                    v.tensor_tensor(otm[:, ts_ * 1536:(ts_ + 1) * 1536],
                                    otm[:, ts_ * 1536:(ts_ + 1) * 1536],
                                    gsb[:, ts_ * 1536:(ts_ + 1) * 1536], ALU.mult))
            ser.add("sync", 1, lambda sy, ts_=ts_:
                    sy.dma_start(out=og_d.ap()[ts_ * 128:(ts_ + 1) * 128, :],
                                 in_=otm[:, ts_ * 1536:(ts_ + 1) * 1536]), dma=True)
        ser.add("sync", 12, lambda sy: [
            sy.dma_start_transpose(ogf[:, vc * 1024:(vc + 1) * 1024],
                                   og_d.ap()[:, vc * 128:(vc + 1) * 128]) for vc in range(12)], dma=True)
        for tp in range(4):
            for vc in range(12):
                ser.add("sync", 1, lambda sy, vc=vc:
                        sy.dma_start(out=wos[:, (vc % 2) * 2048:(vc % 2 + 1) * 2048],
                                     in_=wot.ap()[vc * 128:(vc + 1) * 128, :]), dma=True)
                def omm(t, tp=tp, vc=vc):
                    out = []
                    for tsp in range(2):
                        ts_ = tp * 2 + tsp
                        for hs in range(4):
                            out.append(t.matmul(bank[tsp * 4 + hs],
                                                ogf[:, vc * 1024 + ts_ * 128: vc * 1024 + ts_ * 128 + 128],
                                                wos[:, (vc % 2) * 2048 + hs * 512:(vc % 2) * 2048 + (hs + 1) * 512],
                                                start=(vc == 0), stop=(vc == 11)))
                    return out
                ser.add("tensor", 8, omm)
            def radd(v, tp=tp):
                out = []
                for tsp in range(2):
                    ts_ = tp * 2 + tsp
                    for hs in range(4):
                        dstv = res[:, ts_ * D + hs * 512: ts_ * D + (hs + 1) * 512]
                        out.append(v.tensor_tensor(dstv, dstv, bank[tsp * 4 + hs], ALU.add))
                return out
            ser.add("vector", 8, radd)

        with nc.Block() as block:
            ser.emit_into(block)



def build_program(blocks, detect_races=True):
    nc = bass.Bass("TRN2", target_bir_lowering=False, detect_race_conditions=detect_races)
    stack = contextlib.ExitStack()
    ctx = Ctx(nc, stack)
    with stack:
        emit_load_residual(ctx)
        for b in blocks:
            if b == "ln":
                with nc.sbuf_tensor("xfm_ln", [128, 16 * 1024], BF16, side="right") as xfm:
                    emit_layernorm(ctx, "lnonly", xfm)
                    xfm_dbg = ctx.dram_out("xfm_dbg", [2048, 1024], BF16)
                    stat_dbg = ctx.dram_out("stat_dbg", [128, 64], F32)
                    with nc.Block() as block:
                        @block.sync
                        def _(sync):
                            ctx.inc(sync.dma_start(out=xfm_dbg.ap().rearrange("(c p) t -> p c t", p=128),
                                                   in_=xfm[:, :].rearrange("p (c t) -> p c t", c=16)), "xfmdbg", 16)
                            ctx.inc(sync.dma_start(out=stat_dbg.ap(), in_=ctx.sb["ln_stat"][:, 0:64]), "xfmdbg", 16)
                            ctx.wait_all(sync, "xfmdbg")
            elif b == "l1fgu":
                with nc.sbuf_tensor("xfm_l1f", [128, 16 * 1024], BF16, side="right") as xfm:
                    emit_layernorm(ctx, "l1f", xfm)
                    emit_l1_ffn_gateup(ctx, xfm)
            elif b == "l0a":
                emit_l0_attn_prep(ctx)
                with nc.sbuf_tensor("xfm_l0a", [128, 16 * 1024], BF16, side="right") as xfm:
                    emit_layernorm(ctx, "l0a", xfm)
                    emit_l0_attn_proj(ctx, xfm)
                    emit_l0_attn_bg(ctx, xfm)
                emit_l0_attn_ag(ctx)
                emit_l0_attn_wo_prep(ctx)
                if "l0f" in blocks:
                    emit_l0_ffn_dequant(ctx)
                emit_l0_attn_scan(ctx)
                emit_l0_attn_combine(ctx)
            elif b == "l0f":
                if "l0a" not in blocks:
                    emit_l0_ffn_dequant(ctx)
                with nc.sbuf_tensor("xfm_l0f", [128, 16 * 1024], BF16, side="right") as xfm:
                    emit_layernorm(ctx, "l0f", xfm)
                    emit_l0_ffn_main(ctx, xfm)
                emit_l0_ffn_down(ctx)
            elif b == "l1a":
                with nc.sbuf_tensor("l1a_O_t", [128, 16 * 1024], BF16, side="right") as O, \
                     nc.sbuf_tensor("l1a_qlat_t", [128, 2 * 1024], BF16, side="right") as qlat:
                    with nc.sbuf_tensor("xfm_l1a", [128, 16 * 1024], BF16, side="right") as xfm:
                        emit_layernorm(ctx, "l1a", xfm)
                        emit_l1_attn_latents(ctx, xfm, qlat)
                    emit_l1_attn_main(ctx, qlat, O)
            elif b == "l1f":
                with nc.sbuf_tensor("xfm_l1f", [128, 16 * 1024], BF16, side="right") as xfm:
                    emit_layernorm(ctx, "l1f", xfm)
                    emit_l1_ffn_gateup(ctx, xfm)
                emit_l1_ffn_down(ctx)
            ctx.retire(keep=("l0f_cc",))
        emit_store_residual(ctx)
    return ctx, nc


_ROPE_CACHE = {}


def _rope_consts():
    if "v" not in _ROPE_CACHE:
        HALF = 64
        inv_freq = 1.0 / 10000.0 ** (np.arange(0, HALF, 2, dtype=np.float64) / HALF)
        t = np.arange(T, dtype=np.float64)
        freqs = t[:, None] * inv_freq[None, :]
        emb = np.concatenate([freqs, freqs], axis=-1)
        cos = np.cos(emb).T
        sin = np.sin(emb).T
        sin_eff = sin.copy()
        sin_eff[0:32, :] = -sin[0:32, :]
        scale = 1.0 / np.sqrt(HALF)
        cosq = (cos * scale).astype(ml_dtypes.bfloat16)
        sinq = (sin_eff * scale).astype(ml_dtypes.bfloat16)
        cosk = cos.astype(ml_dtypes.bfloat16)
        sink = sin_eff.astype(ml_dtypes.bfloat16)
        _ROPE_CACHE["v"] = (cosq, sinq, cosk, sink)
    return _ROPE_CACHE["v"]


def make_in_maps(blocks, inputs, x_full):
    in_maps = []
    for r in range(8):
        b, s = r // 4, r % 4
        im = {"x_shard": np.ascontiguousarray(x_full[b, s * NT:(s + 1) * NT, :])}
        if "l0a" in blocks:
            im["l0a_Wq"] = inputs["L0_attn_Wq"]
            im["l0a_Wk"] = inputs["L0_attn_Wk"]
            im["l0a_Wv"] = inputs["L0_attn_Wv"]
            im["l0a_Wg"] = inputs["L0_attn_Wg"]
            im["l0a_Wb"] = inputs["L0_attn_Wb"]
            im["l0a_Wa"] = inputs["L0_attn_Wa"]
            im["l0a_Wo"] = np.ascontiguousarray(inputs["L0_attn_Wo"])
            im["l0a_wnorm"] = np.tile(np.asarray(inputs["L0_attn_o_norm_w"]), 6)[None, :].astype(np.float32)
            im["l0a_conv_q"] = inputs["L0_attn_conv_q"]
            im["l0a_conv_k"] = inputs["L0_attn_conv_k"]
            im["l0a_conv_v"] = inputs["L0_attn_conv_v"]
            al = np.zeros((1, 16), np.float32); al[0, 8:14] = np.asarray(inputs["L0_attn_A_log"])
            dt_ = np.zeros((1, 16), np.float32); dt_[0, 8:14] = np.asarray(inputs["L0_attn_dt_bias"])
            im["l0a_A_log"] = al
            im["l0a_dtb"] = dt_
            ident = np.eye(128)
            im["ident_b"] = ident.astype(ml_dtypes.bfloat16)
            im["ident_f"] = ident.astype(np.float32)
            m_su = np.triu(np.ones((128, 128), np.float32), 1)
            m_iu = np.triu(np.ones((128, 128), np.float32), 0)
            m_sl = np.tril(np.ones((128, 128), np.float32), -1)
            im["l0a_masks"] = np.concatenate([m_su, m_iu, m_sl], axis=1)
            im["ones_row_f"] = np.ones((1, 128), np.float32)
            im["ones_f1"] = np.ones((1, 1), np.float32)
            im["ones_col_b"] = np.ones((128, 1), dtype=ml_dtypes.bfloat16)
            uin = np.zeros((1, 32), np.int32)
            for j in range(3):
                uu = 3 * r + j
                h = (uu % 12) // 2
                half = uu % 2
                uin[0, j * 8 + 0] = 256 * h
                uin[0, j * 8 + 1] = 256 * h + 128 * half
                uin[0, j * 8 + 2] = h
            uin[0, 24] = s
            im["l0a_uinfo"] = uin
            gi = np.zeros((128, 176), np.uint32)
            pp = np.arange(128, dtype=np.uint32)
            for j in range(3):
                uu_g = 3 * r + j
                h = (uu_g % 12) // 2
                half = uu_g % 2
                qof = 256 * h
                vof = 256 * h + 128 * half
                for bi in range(5):
                    off = [qof, qof + 128, 1536 + qof, 1536 + qof + 128, 3072 + vof][bi]
                    for sl in range(4):
                        gi[:, j * 20 + bi * 4 + sl] = sl * 6144 + off + pp
                gi[:, 60 + j * 3 + 0] = qof + pp
                gi[:, 60 + j * 3 + 1] = qof + 128 + pp
                gi[:, 60 + j * 3 + 2] = vof + pp
                cc32 = np.arange(32, dtype=np.uint32)
                gi[0:32, 69 + j * 2 + 0] = (8 + h + 16 * (cc32 // 8)) * 8 + (cc32 % 8)
                gi[0:32, 69 + j * 2 + 1] = (h + 16 * (cc32 // 8)) * 8 + (cc32 % 8)
            for uu2 in range(12):
                for tt in range(8):
                    gi[:, 75 + uu2 * 8 + tt] = uu2 * 4096 + s * 1024 + tt * 128 + pp
            im["l0a_gidx"] = gi
        if "l0f" in blocks:
            im["l0f_gWp"] = np.ascontiguousarray(inputs["L0_ffn_gate_W_packed"][r * 2048:(r + 1) * 2048]).astype(np.int32)
            im["l0f_gSc"] = np.repeat(np.ascontiguousarray(inputs["L0_ffn_gate_scales"][r * 2048:(r + 1) * 2048]), 32, axis=1)
            im["l0f_uWp"] = np.ascontiguousarray(inputs["L0_ffn_up_W_packed"][r * 2048:(r + 1) * 2048]).astype(np.int32)
            im["l0f_uSc"] = np.repeat(np.ascontiguousarray(inputs["L0_ffn_up_scales"][r * 2048:(r + 1) * 2048]), 32, axis=1)
            im["l0f_dWp"] = np.ascontiguousarray(inputs["L0_ffn_down_W_packed"][r * 256:(r + 1) * 256]).astype(np.int32)
            im["l0f_dSc"] = np.repeat(np.ascontiguousarray(inputs["L0_ffn_down_scales"][r * 256:(r + 1) * 256]), 32, axis=1)
        if "l1a" in blocks:
            im["l1a_wq_d"] = inputs["L1_attn_wq_d"]
            im["l1a_wkv_d"] = inputs["L1_attn_wkv_d"]
            im["l1a_wq_u"] = inputs["L1_attn_wq_u"]
            im["l1a_wk_u"] = inputs["L1_attn_wk_u"]
            im["l1a_wv_u"] = inputs["L1_attn_wv_u"]
            im["l1a_wo"] = inputs["L1_attn_wo"]
            cosq, sinq, cosk, sink = _rope_consts()
            im["l1a_cos_q"] = np.ascontiguousarray(cosq[:, s * NT:(s + 1) * NT])
            im["l1a_sin_q"] = np.ascontiguousarray(sinq[:, s * NT:(s + 1) * NT])
            im["l1a_cos_k"] = cosk
            im["l1a_sin_k"] = sink
            im["ones_col_b"] = np.ones((128, 1), dtype=ml_dtypes.bfloat16)
            im["ones_row_f"] = np.ones((1, 128), dtype=np.float32)
        if "l1f" in blocks or "l1fgu" in blocks:
            im["l1f_wg"] = inputs["L1_ffn_w_gate"]
            im["l1f_wu"] = inputs["L1_ffn_w_up"]
            im["l1f_wd"] = inputs["L1_ffn_w_down"]
        in_maps.append(im)
    return in_maps


def run_blocks(blocks, inputs, x_full):
    ctx, nc = build_program(blocks)
    in_maps = make_in_maps(blocks, inputs, x_full)
    res = run_bass_kernel_spmd(nc, in_maps, core_ids=list(range(8)))
    out = np.zeros((B, T, D), dtype=np.float32)
    for r in range(8):
        b, s = r // 4, r % 4
        out[b, s * NT:(s + 1) * NT, :] = res.results[r]["x_out"]
    return out


def kernel(**inputs):
    x = np.asarray(inputs["L0_attn_hidden_states"])
    return run_blocks(["l0a", "l0f", "l1a", "l1f"], inputs, x)
